# version 02
# baseline (speedup 1.0000x reference)
import numpy as np
import ml_dtypes
import concourse.bass as bass
import concourse.mybir as mybir
from concourse.bass_utils import run_bass_kernel_spmd
from contextlib import ExitStack

F32 = mybir.dt.float32
BF16 = mybir.dt.bfloat16
I32 = mybir.dt.int32
F16 = mybir.dt.float16
AF = mybir.ActivationFunctionType
AX = mybir.AxisListType
ALU = mybir.AluOpType
BF = ml_dtypes.bfloat16

N_CORES = 8
S = 8192
D = 2048
NH = 16
HD = 128
NT = S // 128
TPC = NT // N_CORES
TOK = TPC * 128

ENGINES = ("sync", "vector", "scalar", "tensor", "gpsimd")

def core_tiles(c):
    return list(range(c * TPC, (c + 1) * TPC))

def core_rows(c):
    idx = []
    for t in core_tiles(c):
        idx.extend(range(t * 128, (t + 1) * 128))
    return np.asarray(idx)


_SEM_ALIAS = {
    "sh_gmm": "eg_gmm", "sh_sil": "eg_sil", "sh_umm": "eg_umm",
    "sh_wcast": "eg_wcast", "sh_wdma": "eg_wdma", "sh_mev": "eg_mev",
    "sh_mdma": "gu_md",
    "gu_gmm": "eg_gmm", "gu_sil": "eg_sil", "gu_umm": "eg_umm", "gu_mev": "eg_mev",
    "fd_ev": "sd_ev", "fd_md": "sd_md", "fd_mm": "sd_mm", "fd_wd": "sd_wd",
    "pj_evac": "wo_evac", "pj_mm": "wo_mm", "pj_wcast": "wo_wcast",
    "pj_wdma": "wo_wdma",
    "x2_out": "out2", "h2T_out": "out2", "fin": "out2",
    "cb_v": "r_v",
}


class _CntDict(dict):
    def _k(self, k):
        return _SEM_ALIAS.get(k, k)
    def __getitem__(self, k):
        return super().__getitem__(self._k(k))
    def __setitem__(self, k, v):
        super().__setitem__(self._k(k), v)
    def get(self, k, d=None):
        return super().get(self._k(k), d)
    def __contains__(self, k):
        return super().__contains__(self._k(k))


class Prog:
    def __init__(self, nc, stack):
        self.nc = nc
        self.stack = stack
        self.ops = {e: [] for e in ENGINES}
        self._sems = {}
        self.cnt = _CntDict()

    def sem(self, name):
        name = _SEM_ALIAS.get(name, name)
        if name not in self._sems:
            self._sems[name] = self.stack.enter_context(self.nc.semaphore(name))
            self.cnt[name] = 0
        return self._sems[name]

    def op(self, eng, fn, inc=None, k=1):
        inc = _SEM_ALIAS.get(inc, inc)
        if inc is not None:
            s = self.sem(inc)
            def g(E, fn=fn, s=s, k=k):
                fn(E).then_inc(s, k)
            self.ops[eng].append(g)
            self.cnt[inc] += k
        else:
            self.ops[eng].append(fn)

    def wait(self, eng, sem_name, val=None):
        sem_name = _SEM_ALIAS.get(sem_name, sem_name)
        s = self.sem(sem_name)
        v = self.cnt[sem_name] if val is None else val
        self.ops[eng].append(lambda E, s=s, v=v: E.wait_ge(s, v))

    def finalize(self):
        with self.nc.Block() as block:
            for e in ENGINES:
                q = self.ops[e]
                if not q:
                    continue
                def runner(E, q=q):
                    for fn in q:
                        fn(E)
                getattr(block, e)(runner)



class FC:
    def __init__(self, nc, p, stack):
        self.nc = nc
        self.p = p
        self.stack = stack
        self.io = {}

def _dr(fc, nc, name, shape, dt, kind):
    if fc is None:
        return nc.dram_tensor(name, shape, dt, kind=kind)
    return fc.io[name]


_NAMECTR = [0]

def _uniq(name):
    _NAMECTR[0] += 1
    return f"{name}_{_NAMECTR[0]}"

def sb(stack, nc, name, shape, dt):
    return stack.enter_context(nc.sbuf_tensor(_uniq(name), shape, dt))

def ps(stack, nc, name, shape, dt=F32):
    return stack.enter_context(nc.psum_tensor(_uniq(name), shape, dt))


L0A_ROT = 32

def l0a_perm():
    j_of_r = np.empty(HD, np.int64)
    for r in range(HD):
        if r < 16: j = r
        elif r < 64: j = r + 16
        elif r < 80: j = r - 48
        else: j = r
        j_of_r[r] = j
    return j_of_r

def l0a_tables(pos, scale):
    inv = 1.0 / 10000.0 ** (np.arange(0, L0A_ROT, 2, dtype=np.float64) / L0A_ROT)
    C = np.ones((HD, len(pos)), np.float64)
    Sg = np.zeros((HD, len(pos)), np.float64)
    ang = pos[None, :].astype(np.float64) * inv[:, None]
    C[0:16] = np.cos(ang);  Sg[0:16] = -np.sin(ang)
    C[64:80] = np.cos(ang); Sg[64:80] = np.sin(ang)
    return (C * scale).astype(np.float32), (Sg * scale).astype(np.float32)

def l0f_tables(pos, scale):
    inv = 1.0 / 1e6 ** (np.arange(0, HD, 2, dtype=np.float64) / HD)
    ang = pos[None, :].astype(np.float64) * inv[:, None]
    C = np.concatenate([np.cos(ang), np.cos(ang)], 0)
    Sg = np.concatenate([-np.sin(ang), np.sin(ang)], 0)
    return (C * scale).astype(np.float32), (Sg * scale).astype(np.float32)



def emit_load_xT(p, x_dram, xT, xbuf, xbf, ident, ptile, n_tiles=TPC, dcols=D,
                 rms=None):
    T = n_tiles * 128
    nf = dcols // 128
    inv_n = 1.0 / dcols
    for ti in range(n_tiles):
        b = ti % 2
        if p.cnt.get("xt_cast", 0) >= 2:
            p.wait("sync", "xt_cast", p.cnt["xt_cast"] - 1)
        p.op("sync", lambda E, b=b, ti=ti: E.dma_start(xbuf[b][:], x_dram[ti*128:(ti+1)*128, :]),
             inc="xt_dma", k=16)
        if rms is None:
            p.wait("vector", "xt_dma")
            p.op("vector", lambda E, b=b: E.tensor_copy(xbf[b][:], xbuf[b][:]), inc="xt_cast")
        else:
            gam, eps_t, rs = rms
            p.wait("scalar", "xt_dma")
            p.op("scalar", lambda E, b=b: E.activation(xbf[b][:], xbuf[b][:], AF.Square,
                                                       accum_out=rs[:, 0:1]))
            p.op("scalar", lambda E, b=b: E.activation(rs[:, 3:4], xbuf[b][:, 0:1], AF.Copy))
            p.op("scalar", lambda E: E.activation(rs[:, 1:2], rs[:, 0:1], AF.Ln,
                                                  scale=inv_n, bias=eps_t[:]))
            p.op("scalar", lambda E, b=b: E.activation(rs[:, 3:4], xbuf[b][:, 0:1], AF.Copy))
            p.op("scalar", lambda E: E.activation(rs[:, 2:3], rs[:, 1:2], AF.Exp,
                                                  scale=-0.5))
            p.op("scalar", lambda E, b=b: E.activation(rs[:, 3:4], xbuf[b][:, 0:1],
                                                       AF.Copy), inc="xt_rms")
            p.wait("vector", "xt_rms")
            p.op("vector", lambda E, b=b: E.tensor_scalar(xbuf[b][:], xbuf[b][:],
                                                          rs[:, 2:3], None, ALU.mult))
            p.op("vector", lambda E, b=b: E.tensor_mul(xbf[b][:], xbuf[b][:], gam[:]),
                 inc="xt_cast")
        p.wait("tensor", "xt_cast")
        for j in range(nf):
            pb = j % 2
            if p.cnt.get("xt_evac", 0) >= 2:
                p.wait("tensor", "xt_evac", p.cnt["xt_evac"] - 1)
            p.op("tensor", lambda E, pb=pb, b=b, j=j: E.transpose(
                ptile[pb][:], xbf[b][:, j*128:(j+1)*128], ident[:]), inc="xt_tp")
            p.wait("vector", "xt_tp")
            p.op("vector", lambda E, pb=pb, j=j, ti=ti: E.tensor_copy(
                xT[:, j*T + ti*128 : j*T + (ti+1)*128], ptile[pb][:]), inc="xt_evac")


def emit_fm_proj_rope(p, nc, wT_dram, n_heads, head0_col, xT, ident_unused,
                      wld, wbf, biasrow_bf, ones_bf, CT, ST, qe, qsw,
                      out_dram, pp, tag, rope=True, Tn=TOK, out_sb=None,
                      scale_imm=None, bias_base=None, add_bias=True):
    nf = D // 128
    ntc = Tn // 512
    if bias_base is None:
        bias_base = head0_col
    for h in range(n_heads):
        b = h % 2
        col0 = head0_col + h * 128
        bcol0 = bias_base + h * 128
        mm_per_h = nf * ntc + (ntc if add_bias else 0)
        if p.cnt.get(tag + "_mm", 0) >= 2 * mm_per_h:
            p.wait("sync", tag + "_mm", p.cnt[tag + "_mm"] - mm_per_h)
        p.op("sync", lambda E, b=b, col0=col0: E.dma_start(
            wld[b][:].rearrange("p (a o) -> p a o", a=16), wT_dram[:, col0:col0+128].rearrange("(a p) o -> p a o", p=128)),
            inc=tag + "_wdma", k=16)
        p.wait("vector", tag + "_wdma")
        p.op("vector", lambda E, b=b: E.tensor_copy(wbf[b][:], wld[b][:]), inc=tag + "_wcast")
        if rope and p.cnt.get(tag + "_sw", 0) > 0:
            p.wait("vector", tag + "_sw", p.cnt[tag + "_sw"])
        p.wait("tensor", tag + "_wcast")
        for tc in range(ntc):
            pb = tc % 2
            if p.cnt.get(tag + "_evac", 0) >= 2:
                p.wait("tensor", tag + "_evac", p.cnt[tag + "_evac"] - 1)
            for j in range(nf):
                p.op("tensor", lambda E, pb=pb, b=b, j=j, tc=tc: E.matmul(
                    pp[pb][:], wbf[b][:, j*128:(j+1)*128],
                    xT[:, j*Tn + tc*512 : j*Tn + tc*512 + 512],
                    start=(j == 0), stop=(not add_bias) and (j == nf - 1)),
                    inc=tag + "_mm")
            if add_bias:
                p.op("tensor", lambda E, pb=pb, bcol0=bcol0, tc=tc: E.matmul(
                    pp[pb][:], biasrow_bf[0:1, bcol0:bcol0+128],
                    ones_bf[0:1, tc*512:tc*512+512],
                    start=False, stop=True), inc=tag + "_mm")
            p.wait("vector", tag + "_mm")
            p.op("vector", lambda E, pb=pb, tc=tc: E.tensor_copy(
                qe[:, tc*512:tc*512+512], pp[pb][:]), inc=tag + "_evac")
        if rope:
            p.wait("gpsimd", tag + "_evac")
            p.op("gpsimd", lambda E: E.dma_start(qsw[0:64, :], qe[64:128, :]), inc=tag + "_sw", k=16)
            p.op("gpsimd", lambda E: E.dma_start(qsw[64:128, :], qe[0:64, :]), inc=tag + "_sw", k=16)
            p.wait("vector", tag + "_sw")
            p.op("vector", lambda E: E.tensor_mul(qe[:], qe[:], CT[:]))
            p.op("vector", lambda E: E.tensor_mul(qsw[:], qsw[:], ST[:]))
            dst = out_sb if out_sb is not None else qsw
            if p.cnt.get(tag + "_out", 0) > 0:
                p.wait("vector", tag + "_out", p.cnt[tag + "_out"])
            p.op("vector", lambda E: E.tensor_add(_bfview(dst), qe[:], qsw[:]), inc=tag + "_done")
        else:
            p.wait("vector", tag + "_evac")
            dst = out_sb if out_sb is not None else qsw
            if p.cnt.get(tag + "_out", 0) > 0:
                p.wait("vector", tag + "_out", p.cnt[tag + "_out"])
            if scale_imm is not None:
                p.op("vector", lambda E: E.tensor_scalar_mul(_bfview(dst), qe[:], scale_imm),
                     inc=tag + "_done")
            else:
                p.op("vector", lambda E: E.tensor_copy(_bfview(dst), qe[:]), inc=tag + "_done")
        p.wait("gpsimd", tag + "_done")
        p.op("gpsimd", lambda E, h=h: E.dma_start(
            out_dram[h*128:(h+1)*128, :], _bfview(dst)), inc=tag + "_out", k=16)


_BFV = {}
def _bfview(t):
    return _BFV[id(t)][:]


def emit_tm_proj(p, wT_dram, col0, ncols, xT, wld, wbf, biasrow_bf, ones_bf,
                 out_dram, out_sb_bf, pp, tag, Tn=TOK, add_bias=True, ocw=512,
                 bias_base=None):
    nf = D // 128
    noc = ncols // ocw
    ntt = Tn // 128
    mm_per = ntt * (nf + (1 if add_bias else 0))
    if bias_base is None:
        bias_base = col0
    for oc in range(noc):
        b = oc % 2
        if p.cnt.get(tag + "_mm", 0) >= 2 * mm_per:
            p.wait("sync", tag + "_mm", p.cnt[tag + "_mm"] - mm_per)
        p.op("sync", lambda E, b=b, oc=oc: E.dma_start(
            wld[b][:].rearrange("p (a o) -> p a o", a=16), wT_dram[:, col0+oc*ocw:col0+(oc+1)*ocw].rearrange("(a p) o -> p a o", p=128)),
            inc=tag + "_wdma", k=16)
        p.wait("vector", tag + "_wdma")
        p.op("vector", lambda E, b=b: E.tensor_copy(wbf[b][:], wld[b][:]), inc=tag + "_wcast")
        p.wait("tensor", tag + "_wcast")
        for ti in range(ntt):
            pb = ti % 2
            if p.cnt.get(tag + "_evac", 0) >= 2:
                p.wait("tensor", tag + "_evac", p.cnt[tag + "_evac"] - 1)
            for j in range(nf):
                p.op("tensor", lambda E, pb=pb, b=b, j=j, ti=ti: E.matmul(
                    pp[pb][:, 0:ocw], xT[:, j*Tn + ti*128 : j*Tn + (ti+1)*128],
                    wbf[b][:, j*ocw:(j+1)*ocw],
                    start=(j == 0), stop=(not add_bias) and (j == nf - 1)), inc=tag + "_mm")
            if add_bias:
                p.op("tensor", lambda E, pb=pb, oc=oc: E.matmul(
                    pp[pb][:, 0:ocw], ones_bf[0:1, 0:128],
                    biasrow_bf[0:1, bias_base+oc*ocw:bias_base+(oc+1)*ocw],
                    start=False, stop=True), inc=tag + "_mm")
            p.wait("vector", tag + "_mm")
            if p.cnt.get(tag + "_out", 0) > 0:
                p.wait("vector", tag + "_out", p.cnt[tag + "_out"])
            p.op("vector", lambda E, pb=pb: E.tensor_copy(out_sb_bf[:], pp[pb][:, 0:ocw]), inc=tag + "_evac")
            p.wait("gpsimd", tag + "_evac")
            p.op("gpsimd", lambda E, ti=ti, oc=oc: E.dma_start(
                out_dram[ti*128:(ti+1)*128, oc*ocw:(oc+1)*ocw], out_sb_bf[:]),
                inc=tag + "_out", k=16)



def build_l0a_qkv(fc=None):
    nc = bass.Bass("TRN2", target_bir_lowering=False) if fc is None else fc.nc
    x = _dr(fc, nc, "x", [TOK, D], F32, "ExternalInput")
    wT = _dr(fc, nc, "wT", [D, 3 * D], F32, "ExternalInput")
    brow = _dr(fc, nc, "brow", [1, 3 * D], F32, "ExternalInput")
    ctq = _dr(fc, nc, "ctq", [HD, TOK], F32, "ExternalInput")
    stq = _dr(fc, nc, "stq", [HD, TOK], F32, "ExternalInput")
    ctk = _dr(fc, nc, "ctk", [HD, TOK], F32, "ExternalInput")
    stk = _dr(fc, nc, "stk", [HD, TOK], F32, "ExternalInput")
    identd = _dr(fc, nc, "ident", [128, 128], BF16, "ExternalInput")
    qf = _dr(fc, nc, "qf", [D, TOK], BF16, "ExternalOutput")
    kf = _dr(fc, nc, "kf", [D, TOK], BF16, "ExternalOutput")
    vt = _dr(fc, nc, "vt", [TOK, D], BF16, "ExternalOutput")

    with ExitStack() as stack:
        p = Prog(nc, stack) if fc is None else fc.p
        xT = sb(stack, nc, "xT", [128, 16 * TOK], BF16)
        ident = sb(stack, nc, "id", [128, 128], BF16)
        brow_b = sb(stack, nc, "browb", [1, 3 * D], BF16)
        ones_b = sb(stack, nc, "ones", [1, TOK], BF16)
        CTq = sb(stack, nc, "CTq", [128, TOK], F32)
        STq = sb(stack, nc, "STq", [128, TOK], F32)
        CTk = sb(stack, nc, "CTk", [128, TOK], F32)
        STk = sb(stack, nc, "STk", [128, TOK], F32)
        qe = sb(stack, nc, "qe", [128, TOK], F32)
        qsw = sb(stack, nc, "qsw", [128, TOK], F32)
        qbf = sb(stack, nc, "qbf", [128, TOK], BF16)
        _BFV[id(qsw)] = qbf
        vout = sb(stack, nc, "vout", [128, 256], BF16)
        ptile = [ps(stack, nc, f"pt{i}", [128, 128], BF16) for i in range(2)]
        pp = [ps(stack, nc, f"pp{i}", [128, 512], F32) for i in range(2)]

        with ExitStack() as stackA:
            xbuf = [sb(stackA, nc, f"xb{i}", [128, D], F32) for i in range(2)]
            xbf = [sb(stackA, nc, f"xf{i}", [128, D], BF16) for i in range(2)]

            p.op("sync", lambda E: E.dma_start(ident[:], identd[:, :]), inc="c_dma", k=16)
            p.op("sync", lambda E: E.dma_start(CTq[:], ctq[:, :]), inc="c_dma", k=16)
            p.op("sync", lambda E: E.dma_start(STq[:], stq[:, :]), inc="c_dma", k=16)
            p.op("sync", lambda E: E.dma_start(CTk[:], ctk[:, :]), inc="c_dma", k=16)
            p.op("sync", lambda E: E.dma_start(STk[:], stk[:, :]), inc="c_dma", k=16)
            for i in range(3):
                if i > 0:
                    p.wait("sync", "c_v", p.cnt.get("c_v", 0))
                p.op("sync", lambda E, i=i: E.dma_start(
                    xbuf[1][0:1, 0:D], brow[0:1, i*D:(i+1)*D]), inc="c_dma", k=16)
                p.wait("vector", "c_dma")
                p.op("vector", lambda E, i=i: E.tensor_copy(
                    brow_b[0:1, i*D:(i+1)*D], xbuf[1][0:1, 0:D]), inc="c_v")
            p.op("vector", lambda E: E.memset(ones_b[:], 1.0), inc="c_v")
            p.wait("tensor", "c_v")
            p.wait("sync", "c_v")

            emit_load_xT(p, x, xT, xbuf, xbf, ident, ptile)

            p.wait("tensor", "xt_evac")
            p.wait("sync", "xt_evac")
            p.wait("gpsimd", "xt_evac")
            p.wait("scalar", "xt_evac")
            p.wait("vector", "xt_evac")

        wld = [sb(stack, nc, f"wld{i}", [128, D], F32) for i in range(2)]
        wbf = [sb(stack, nc, f"wbf{i}", [128, D], BF16) for i in range(2)]
        wldv = [sb(stack, nc, f"wldv{i}", [128, 16 * 256], F32) for i in range(2)]
        wbfv = [sb(stack, nc, f"wbfv{i}", [128, 16 * 256], BF16) for i in range(2)]

        emit_fm_proj_rope(p, nc, wT, NH, 0, xT, None, wld, wbf, brow_b, ones_b,
                          CTq, STq, qe, qsw, qf, pp, "fm", rope=True)
        p.wait("tensor", "fm_evac")
        p.wait("vector", "fm_sw")
        p.wait("vector", "fm_out")
        p.wait("sync", "fm_mm")
        emit_fm_proj_rope(p, nc, wT, NH, D, xT, None, wld, wbf, brow_b, ones_b,
                          CTk, STk, qe, qsw, kf, pp, "fm", rope=True)
        p.wait("tensor", "fm_evac")
        emit_tm_proj(p, wT, 2 * D, D, xT, wldv, wbfv, brow_b, ones_b, vt, vout,
                     pp, "fm", ocw=256)

        p.wait("gpsimd", "fm_out")
        p.wait("gpsimd", "fm_out")
        p.wait("gpsimd", "fm_out")
        if fc is None:
            p.finalize()
    return nc


def emit_loadT_bf16(p, x_dram, xT, xbf, ident, ptile, n_tiles=TPC, dcols=D, tag="bt"):
    T = n_tiles * 128
    nf = dcols // 128
    for ti in range(n_tiles):
        b = ti % 2
        if p.cnt.get(tag + "_tp", 0) >= 2 * nf:
            p.wait("sync", tag + "_tp", p.cnt[tag + "_tp"] - nf)
        p.op("sync", lambda E, b=b, ti=ti: E.dma_start(xbf[b][:], x_dram[ti*128:(ti+1)*128, :]),
             inc=tag + "_dma", k=16)
        p.wait("tensor", tag + "_dma")
        for j in range(nf):
            pb = j % 2
            if p.cnt.get(tag + "_evac", 0) >= 2:
                p.wait("tensor", tag + "_evac", p.cnt[tag + "_evac"] - 1)
            p.op("tensor", lambda E, pb=pb, b=b, j=j: E.transpose(
                ptile[pb][:], xbf[b][:, j*128:(j+1)*128], ident[:]), inc=tag + "_tp")
            p.wait("vector", tag + "_tp")
            p.op("vector", lambda E, pb=pb, j=j, ti=ti: E.tensor_copy(
                xT[:, j*T + ti*128 : j*T + (ti+1)*128], ptile[pb][:]), inc=tag + "_evac")


def emit_rms_from_sbuf(p, src_ap, dst_bf_ap, gam_ap, eps_t, rs, n_cols, tag):
    inv_n = 1.0 / n_cols
    p.wait("scalar", tag + "_v", p.cnt.get(tag + "_v", 0))
    p.op("scalar", lambda E: E.activation(rs[:, 3:4], src_ap[:, 0:1], AF.Copy))
    p.op("scalar", lambda E: E.activation(dst_bf_ap, src_ap, AF.Square, accum_out=rs[:, 0:1]))
    p.op("scalar", lambda E: E.activation(rs[:, 3:4], src_ap[:, 0:1], AF.Copy))
    p.op("scalar", lambda E: E.activation(rs[:, 1:2], rs[:, 0:1], AF.Ln, scale=inv_n,
                                          bias=eps_t[:]))
    p.op("scalar", lambda E: E.activation(rs[:, 3:4], src_ap[:, 0:1], AF.Copy))
    p.op("scalar", lambda E: E.activation(rs[:, 2:3], rs[:, 1:2], AF.Exp, scale=-0.5))
    p.op("scalar", lambda E: E.activation(rs[:, 3:4], src_ap[:, 0:1], AF.Copy),
         inc=tag + "_s")
    p.wait("vector", tag + "_s")
    p.op("vector", lambda E: E.scalar_tensor_tensor(
        dst_bf_ap, src_ap, rs[:, 2:3], gam_ap, ALU.mult, ALU.mult), inc=tag + "_v")


def build_l0f_moe_a(fc=None):
    nc = bass.Bass("TRN2", target_bir_lowering=False) if fc is None else fc.nc
    x1 = _dr(fc, nc, "x1", [TOK, D], F32, "ExternalInput")
    ctx = _dr(fc, nc, "ctx", [TOK, D], BF16, "ExternalInput")
    wo = _dr(fc, nc, "wo", [D, D], F32, "ExternalInput")
    wr = _dr(fc, nc, "wr", [D, 8], F32, "ExternalInput")
    weg = _dr(fc, nc, "weg", [D, 1], F32, "ExternalInput")
    gam = _dr(fc, nc, "gam", [128, D], F32, "ExternalInput")
    identd = _dr(fc, nc, "ident", [128, 128], BF16, "ExternalInput")
    x2 = _dr(fc, nc, "x2", [TOK, D], F32, "ExternalOutput")
    h2T = _dr(fc, nc, "h2T", [D, TOK], BF16, "ExternalOutput")
    comb = _dr(fc, nc, "comb", [TOK, 8], F32, "ExternalOutput")
    sg = _dr(fc, nc, "sg", [TOK, 1], F32, "ExternalOutput")

    with ExitStack() as stack:
        p = Prog(nc, stack) if fc is None else fc.p
        xacc = sb(stack, nc, "xacc", [128, TPC * D], F32)
        h2Ts = sb(stack, nc, "h2Ts", [128, 16 * TOK], BF16)
        ident = sb(stack, nc, "id", [128, 128], BF16)
        gam_sb = sb(stack, nc, "gam_sb", [128, D], F32)
        eps_t = sb(stack, nc, "eps_t", [128, 1], F32)
        rs = sb(stack, nc, "rs", [128, 4], F32)
        h2tok = sb(stack, nc, "h2tok", [128, D], BF16)

        p.op("sync", lambda E: E.dma_start(ident[:], identd[:, :]), inc="c_dma", k=16)
        p.op("sync", lambda E: E.dma_start(gam_sb[:], gam[:, :]), inc="c_dma", k=16)
        p.op("sync", lambda E: E.dma_start(
            xacc[:].rearrange("p (a n) -> p a n", a=TPC),
            x1[:, :].rearrange("(a p) n -> p a n", p=128)), inc="c_dma", k=16)
        p.wait("vector", "c_dma")
        p.op("vector", lambda E: E.memset(eps_t[:], 1e-6), inc="c_v")
        p.wait("tensor", "c_v")
        p.wait("scalar", "c_v")

        with ExitStack() as sA:
            ctxT = sb(sA, nc, "ctxT", [128, 16 * TOK], BF16)
            cbf = [sb(sA, nc, f"cbf{i}", [128, D], BF16) for i in range(2)]
            wld = [sb(sA, nc, f"wld{i}", [128, 16 * 256], F32) for i in range(2)]
            wbf = [sb(sA, nc, f"wbf{i}", [128, 16 * 256], BF16) for i in range(2)]
            with ExitStack() as sP:
                ptile = [ps(sP, nc, f"pt{i}", [128, 128], BF16) for i in range(2)]
                emit_loadT_bf16(p, ctx, ctxT, cbf, ident, ptile, tag="xt")
                p.wait("tensor", "xt_evac")
                p.wait("sync", "xt_evac")
            with ExitStack() as sP:
                pp = [ps(sP, nc, f"pp{i}", [128, 512], F32) for i in range(2)]
                nf = 16; ocw = 256; noc = D // ocw
                for oc in range(noc):
                    b = oc % 2
                    if p.cnt.get("wo_mm", 0) >= 2 * TPC * nf:
                        p.wait("sync", "wo_mm", p.cnt["wo_mm"] - TPC * nf)
                    p.op("sync", lambda E, b=b, oc=oc: E.dma_start(
                        wld[b][:].rearrange("p (a o) -> p a o", a=16),
                        wo[:, oc*ocw:(oc+1)*ocw].rearrange("(a p) o -> p a o", p=128)),
                        inc="wo_wdma", k=16)
                    p.wait("vector", "wo_wdma")
                    p.op("vector", lambda E, b=b: E.tensor_copy(wbf[b][:], wld[b][:]),
                         inc="wo_wcast")
                    p.wait("tensor", "wo_wcast")
                    for ti in range(TPC):
                        pb = ti % 2
                        if p.cnt.get("wo_evac", 0) >= 2:
                            p.wait("tensor", "wo_evac", p.cnt["wo_evac"] - 1)
                        for j in range(nf):
                            p.op("tensor", lambda E, pb=pb, b=b, j=j, ti=ti: E.matmul(
                                pp[pb][:, 0:ocw], ctxT[:, j*TOK + ti*128 : j*TOK + (ti+1)*128],
                                wbf[b][:, j*ocw:(j+1)*ocw],
                                start=(j == 0), stop=(j == nf - 1)), inc="wo_mm")
                        p.wait("vector", "wo_mm")
                        p.op("vector", lambda E, pb=pb, ti=ti, oc=oc: E.tensor_add(
                            xacc[:, ti*D + oc*ocw : ti*D + (oc+1)*ocw],
                            xacc[:, ti*D + oc*ocw : ti*D + (oc+1)*ocw],
                            pp[pb][:, 0:ocw]), inc="wo_evac")

        p.wait("gpsimd", "wo_evac")
        p.wait("scalar", "wo_evac")
        p.wait("tensor", "wo_evac")
        p.wait("vector", "wo_evac")
        p.wait("sync", "wo_evac")
        for ti in range(TPC):
            p.op("gpsimd", lambda E, ti=ti: E.dma_start(
                x2[ti*128:(ti+1)*128, :], xacc[:, ti*D:(ti+1)*D]), inc="x2_out", k=16)
        with ExitStack() as sP:
            ptile = [ps(sP, nc, f"pth{i}", [128, 128], BF16) for i in range(2)]
            for ti in range(TPC):
                emit_rms_from_sbuf(p, xacc[:, ti*D:(ti+1)*D], h2tok[:], gam_sb[:],
                                   eps_t, rs, D, "h2")
                p.wait("tensor", "h2_v")
                for j in range(16):
                    pb = j % 2
                    if p.cnt.get("xt_evac", 0) >= 2:
                        p.wait("tensor", "xt_evac", p.cnt["xt_evac"] - 1)
                    p.op("tensor", lambda E, pb=pb, j=j: E.transpose(
                        ptile[pb][:], h2tok[:, j*128:(j+1)*128], ident[:]), inc="xt_tp")
                    p.wait("vector", "xt_tp")
                    p.op("vector", lambda E, pb=pb, j=j, ti=ti: E.tensor_copy(
                        h2Ts[:, j*TOK + ti*128 : j*TOK + (ti+1)*128], ptile[pb][:]),
                        inc="xt_evac")
                p.wait("scalar", "xt_evac")
        p.wait("gpsimd", "xt_evac")
        for j in range(16):
            p.op("gpsimd", lambda E, j=j: E.dma_start(
                h2T[j*128:(j+1)*128, :], h2Ts[:, j*TOK:(j+1)*TOK]), inc="h2T_out", k=16)

        wr_sb = sb(stack, nc, "wr_sb", [128, 16 * 8], BF16)
        weg_sb = sb(stack, nc, "weg_sb", [128, 16], BF16)
        wr_ld = sb(stack, nc, "wr_ld", [128, 16 * 8], F32)
        weg_ld = sb(stack, nc, "weg_ld", [128, 16], F32)
        probs = sb(stack, nc, "probs", [128, 8 * TPC], F32)
        stats = sb(stack, nc, "stats", [128, 8 * TPC], F32)
        combs = sb(stack, nc, "combs", [128, 8 * TPC], F32)
        sg_sb = sb(stack, nc, "sg_sb", [128, TPC], F32)
        p.op("sync", lambda E: E.dma_start(
            wr_ld[:].rearrange("p (a o) -> p a o", a=16),
            wr[:, :].rearrange("(a p) o -> p a o", p=128)), inc="r_dma", k=16)
        p.op("sync", lambda E: E.dma_start(
            weg_ld[:].rearrange("p (a o) -> p a o", a=16),
            weg[:, :].rearrange("(a p) o -> p a o", p=128),
            allow_slow_non_contiguous=True), inc="r_dma", k=16)
        p.wait("vector", "r_dma")
        p.op("vector", lambda E: E.tensor_copy(wr_sb[:], wr_ld[:]), inc="r_v")
        p.op("vector", lambda E: E.tensor_copy(weg_sb[:], weg_ld[:]), inc="r_v")
        p.wait("tensor", "r_v")
        with ExitStack() as sP:
            pr = ps(sP, nc, "pr", [128, 64], F32)
            pe = ps(sP, nc, "pe", [128, 8], F32)
            for ti in range(TPC):
                for j in range(16):
                    p.op("tensor", lambda E, ti=ti, j=j: E.matmul(
                        pr[:, ti*8:(ti+1)*8], h2Ts[:, j*TOK + ti*128 : j*TOK + (ti+1)*128],
                        wr_sb[:, j*8:(j+1)*8], start=(j == 0), stop=(j == 15)), inc="r_mm")
                for j in range(16):
                    p.op("tensor", lambda E, ti=ti, j=j: E.matmul(
                        pe[:, ti:ti+1], h2Ts[:, j*TOK + ti*128 : j*TOK + (ti+1)*128],
                        weg_sb[:, j:j+1], start=(j == 0), stop=(j == 15)), inc="r_mm")
            p.wait("vector", "r_mm")
            p.wait("scalar", "r_mm")
            for ti in range(TPC):
                p.op("vector", lambda E, ti=ti: E.reduce_max(
                    stats[:, ti*8:ti*8+1], pr[:, ti*8:(ti+1)*8], axis=AX.X, negate=True))
            p.op("vector", lambda E: E.tensor_copy(stats[:, 7:8], stats[:, 0:1]), inc="r_vs")
            p.wait("scalar", "r_vs")
            for ti in range(TPC):
                p.op("scalar", lambda E, ti=ti: E.activation(
                    probs[:, ti*8:(ti+1)*8], pr[:, ti*8:(ti+1)*8], AF.Exp,
                    bias=stats[:, ti*8:ti*8+1], accum_out=stats[:, ti*8+1:ti*8+2]))
            for ti in range(TPC):
                p.op("scalar", lambda E, ti=ti: E.activation(
                    sg_sb[:, ti:ti+1], pe[:, ti:ti+1], AF.Sigmoid))
            p.op("scalar", lambda E: E.activation(stats[:, 7:8], stats[:, 0:1], AF.Copy),
                 inc="r_sc")
            p.wait("vector", "r_sc")
        for ti in range(TPC):
            p.op("vector", lambda E, ti=ti: E.reduce_max(
                stats[:, ti*8+2:ti*8+3], probs[:, ti*8:(ti+1)*8], axis=AX.X))
        for ti in range(TPC):
            p.op("vector", lambda E, ti=ti: E.tensor_scalar(
                combs[:, ti*8:(ti+1)*8], probs[:, ti*8:(ti+1)*8],
                stats[:, ti*8+2:ti*8+3], None, ALU.is_equal))
        for ti in range(TPC):
            p.op("vector", lambda E, ti=ti: E.scalar_tensor_tensor(
                combs[:, ti*8:(ti+1)*8], combs[:, ti*8:(ti+1)*8], -1e30,
                probs[:, ti*8:(ti+1)*8], ALU.mult, ALU.add))
        for ti in range(TPC):
            p.op("vector", lambda E, ti=ti: E.reduce_max(
                stats[:, ti*8+3:ti*8+4], combs[:, ti*8:(ti+1)*8], axis=AX.X))
        for ti in range(TPC):
            p.op("vector", lambda E, ti=ti: E.tensor_scalar(
                combs[:, ti*8:(ti+1)*8], probs[:, ti*8:(ti+1)*8],
                stats[:, ti*8+3:ti*8+4], None, ALU.is_ge))
        for ti in range(TPC):
            p.op("vector", lambda E, ti=ti: E.reciprocal(
                stats[:, ti*8+4:ti*8+5], stats[:, ti*8+1:ti*8+2]))
        for ti in range(TPC):
            p.op("vector", lambda E, ti=ti: E.tensor_mul(
                combs[:, ti*8:(ti+1)*8], combs[:, ti*8:(ti+1)*8],
                probs[:, ti*8:(ti+1)*8]))
        for ti in range(TPC):
            p.op("vector", lambda E, ti=ti: E.tensor_scalar(
                combs[:, ti*8:(ti+1)*8], combs[:, ti*8:(ti+1)*8],
                stats[:, ti*8+4:ti*8+5], None, ALU.mult), inc="cb_v")
        p.wait("gpsimd", "cb_v")
        p.op("gpsimd", lambda E: E.dma_start(
            comb[:, :].rearrange("(a p) o -> p a o", p=128),
            combs[:].rearrange("p (a o) -> p a o", a=TPC)), inc="fin", k=16)
        p.op("gpsimd", lambda E: E.dma_start(
            sg[:, :].rearrange("(a p) o -> p a o", p=128),
            sg_sb[:].rearrange("p (a o) -> p a o", a=TPC),
            allow_slow_non_contiguous=True), inc="fin", k=16)
        p.wait("gpsimd", "fin")
        p.wait("gpsimd", "x2_out")
        p.wait("gpsimd", "h2T_out")
        if fc is None:
            p.finalize()
    return nc


def build_l0f_moe_b(debug_dump=False, fc=None):
    nc = bass.Bass("TRN2", target_bir_lowering=False) if fc is None else fc.nc
    x1 = _dr(fc, nc, "x1", [TOK, D], F32, "ExternalInput")
    x2 = _dr(fc, nc, "x2", [TOK, D], F32, "ExternalInput")
    h2Td = _dr(fc, nc, "h2T", [D, TOK], BF16, "ExternalInput")
    combd = _dr(fc, nc, "comb", [TOK, 8], F32, "ExternalInput")
    sgd = _dr(fc, nc, "sg", [TOK, 1], F32, "ExternalInput")
    wg = _dr(fc, nc, "wg", [8, D, 1408], F32, "ExternalInput")
    wu = _dr(fc, nc, "wu", [8, D, 1408], F32, "ExternalInput")
    wd = _dr(fc, nc, "wd", [8, 1408, D], F32, "ExternalInput")
    wsg = _dr(fc, nc, "wsg", [D, 5632], F32, "ExternalInput")
    wsu = _dr(fc, nc, "wsu", [D, 5632], F32, "ExternalInput")
    wsd = _dr(fc, nc, "wsd", [5632, D], F32, "ExternalInput")
    identd = _dr(fc, nc, "ident", [128, 128], BF16, "ExternalInput")
    x4 = _dr(fc, nc, "x4", [TOK, D], F32, "ExternalOutput")
    if debug_dump:
        dbg_xacc = _dr(fc, nc, "dbg_xacc", [TOK, D], F32, "ExternalOutput")
    mTs = _dr(fc, nc, "mTs", [5632, TOK], BF16, "Internal")

    OCWS = [512, 512, 384]
    OCOF = [0, 512, 1024]
    with ExitStack() as stack:
        p = Prog(nc, stack) if fc is None else fc.p
        xacc = sb(stack, nc, "xacc", [128, TPC * D], F32)
        h2Ts = sb(stack, nc, "h2Ts", [128, 16 * TOK], BF16)
        combs = sb(stack, nc, "combs", [128, 8 * TPC], F32)
        sg_sb = sb(stack, nc, "sg_sb", [128, TPC], F32)
        ident = sb(stack, nc, "id", [128, 128], BF16)

        p.op("sync", lambda E: E.dma_start(ident[:], identd[:, :]), inc="c_dma", k=16)
        p.op("sync", lambda E: E.dma_start(
            xacc[:].rearrange("p (a n) -> p a n", a=TPC),
            x2[:, :].rearrange("(a p) n -> p a n", p=128)), inc="c_dma", k=16)
        p.op("sync", lambda E: E.dma_start(
            combs[:].rearrange("p (a o) -> p a o", a=TPC),
            combd[:, :].rearrange("(a p) o -> p a o", p=128),
            allow_slow_non_contiguous=True), inc="c_dma", k=16)
        p.op("sync", lambda E: E.dma_start(
            sg_sb[:].rearrange("p (a o) -> p a o", a=TPC),
            sgd[:, :].rearrange("(a p) o -> p a o", p=128),
            allow_slow_non_contiguous=True), inc="c_dma", k=16)
        for j in range(16):
            p.op("sync", lambda E, j=j: E.dma_start(
                h2Ts[:, j*TOK:(j+1)*TOK], h2Td[j*128:(j+1)*128, :]), inc="c_dma", k=16)
        p.wait("tensor", "c_dma")
        p.wait("vector", "c_dma")
        p.wait("scalar", "c_dma")

        def gu_phase(wg_ap, wu_ap, wldg, wgbf, sgm, m_sb, mcols, ocws, ocof, comb_col,
                     pg, pu, tag):
            for oci, ocw in enumerate(ocws):
                of = ocof[oci]
                p.wait("sync", tag + "_umm")
                p.op("sync", lambda E, of=of, ocw=ocw: E.dma_start(
                    wldg[:, 0:16*ocw].rearrange("p (a o) -> p a o", a=16),
                    wg_ap[:, of:of+ocw].rearrange("(a p) o -> p a o", p=128)),
                    inc=tag + "_wdma", k=16)
                p.wait("vector", tag + "_wdma")
                p.op("vector", lambda E, ocw=ocw: E.tensor_copy(
                    wgbf[:, 0:16*ocw], wldg[:, 0:16*ocw]), inc=tag + "_wcast")
                p.wait("tensor", tag + "_wcast")
                for ti in range(TPC):
                    pb = ti % 2
                    if p.cnt.get(tag + "_sil", 0) >= 2:
                        p.wait("tensor", tag + "_sil", p.cnt[tag + "_sil"] - 1)
                    for j in range(16):
                        p.op("tensor", lambda E, pb=pb, j=j, ti=ti, ocw=ocw: E.matmul(
                            pg[pb][:, 0:ocw], h2Ts[:, j*TOK + ti*128 : j*TOK + (ti+1)*128],
                            wgbf[:, j*ocw:(j+1)*ocw], start=(j == 0), stop=(j == 15)),
                            inc=tag + "_gmm")
                    p.wait("scalar", tag + "_gmm")
                    p.op("scalar", lambda E, pb=pb, ti=ti, ocw=ocw: E.activation(
                        sgm[:, ti*512:ti*512+ocw], pg[pb][:, 0:ocw], AF.Silu),
                        inc=tag + "_sil")
                p.wait("sync", tag + "_gmm")
                p.op("sync", lambda E, of=of, ocw=ocw: E.dma_start(
                    wldg[:, 0:16*ocw].rearrange("p (a o) -> p a o", a=16),
                    wu_ap[:, of:of+ocw].rearrange("(a p) o -> p a o", p=128)),
                    inc=tag + "_wdma", k=16)
                p.wait("vector", tag + "_wdma")
                p.wait("vector", tag + "_sil")
                p.op("vector", lambda E, ocw=ocw: E.tensor_copy(
                    wgbf[:, 0:16*ocw], wldg[:, 0:16*ocw]), inc=tag + "_wcast")
                p.wait("tensor", tag + "_wcast")
                for ti in range(TPC):
                    pb = ti % 2
                    if p.cnt.get(tag + "_mev", 0) >= 2:
                        p.wait("tensor", tag + "_mev", p.cnt[tag + "_mev"] - 1)
                    for j in range(16):
                        p.op("tensor", lambda E, pb=pb, j=j, ti=ti, ocw=ocw: E.matmul(
                            pu[pb][:, 0:ocw], h2Ts[:, j*TOK + ti*128 : j*TOK + (ti+1)*128],
                            wgbf[:, j*ocw:(j+1)*ocw], start=(j == 0), stop=(j == 15)),
                            inc=tag + "_umm")
                    p.wait("vector", tag + "_umm")
                    if comb_col is not None:
                        p.op("vector", lambda E, pb=pb, ti=ti, ocw=ocw, of=of: E.scalar_tensor_tensor(
                            m_sb[:, ti*mcols + of : ti*mcols + of + ocw],
                            pu[pb][:, 0:ocw], comb_col(ti),
                            sgm[:, ti*512:ti*512+ocw], ALU.mult, ALU.mult),
                            inc=tag + "_mev")
                    else:
                        p.op("vector", lambda E, pb=pb, ti=ti, ocw=ocw, of=of: E.tensor_mul(
                            m_sb[:, ti*mcols + of : ti*mcols + of + ocw],
                            pu[pb][:, 0:ocw], sgm[:, ti*512:ti*512+ocw]),
                            inc=tag + "_mev")

        for e in range(8):
            with ExitStack() as sE:
                m_sb = sb(sE, nc, f"m{e}", [128, 8 * 1408], BF16)
                with ExitStack() as sE1, ExitStack() as sP:
                    wldg = sb(sE1, nc, f"wldg{e}", [128, 16 * 512], F32)
                    wgbf = sb(sE1, nc, f"wgbf{e}", [128, 16 * 512], BF16)
                    sgm = sb(sE1, nc, f"sgm{e}", [128, 8 * 512], BF16)
                    pg = [ps(sP, nc, f"pg{e}{i}", [128, 512], F32) for i in range(2)]
                    pu = [ps(sP, nc, f"pu{e}{i}", [128, 512], F32) for i in range(2)]
                    tag = "eg"
                    gu_phase(wg[e], wu[e], wldg, wgbf, sgm, m_sb, 1408, OCWS, OCOF,
                             lambda ti, e=e: combs[:, ti*8+e:ti*8+e+1], pg, pu, tag)
                    p.wait("tensor", tag + "_mev")
                    p.wait("sync", tag + "_mev")
                    p.wait("scalar", tag + "_mev")
                    p.wait("vector", tag + "_mev")
                mT_sb = sb(sE, nc, f"mT{e}", [128, 8 * 1408], BF16)
                with ExitStack() as sP:
                    ptile = [ps(sP, nc, f"ptm{e}{i}", [128, 128], BF16) for i in range(2)]
                    tag = "dd"
                    for ti in range(TPC):
                        for ic in range(11):
                            pb = (ti * 11 + ic) % 2
                            if p.cnt.get(tag + "_ev", 0) >= 2:
                                p.wait("tensor", tag + "_ev", p.cnt[tag + "_ev"] - 1)
                            p.op("tensor", lambda E, pb=pb, ti=ti, ic=ic: E.transpose(
                                ptile[pb][:], m_sb[:, ti*1408 + ic*128 : ti*1408 + (ic+1)*128],
                                ident[:]), inc=tag + "_tp")
                            p.wait("vector", tag + "_tp")
                            p.op("vector", lambda E, pb=pb, ti=ti, ic=ic: E.tensor_copy(
                                mT_sb[:, (ti*11+ic)*128 : (ti*11+ic+1)*128], ptile[pb][:]),
                                inc=tag + "_ev")
                    p.wait("tensor", tag + "_ev")
                    p.wait("sync", tag + "_ev")
                with ExitStack() as sE3, ExitStack() as sP:
                    wdld = sb(sE3, nc, f"wdld{e}", [128, D], F32)
                    wdbf = sb(sE3, nc, f"wdbf{e}", [128, 11 * D], BF16)
                    pd = [ps(sP, nc, f"pd{e}{i}", [128, 512], F32) for i in range(2)]
                    tag = "ed"
                    for ic in range(11):
                        if ic > 0 or e > 0:
                            p.wait("sync", tag + "_wc")
                        p.op("sync", lambda E, ic=ic, e=e, wdld=wdld: E.dma_start(
                            wdld[:], wd[e][ic*128:(ic+1)*128, :]), inc=tag + "_wd", k=16)
                        p.wait("vector", tag + "_wd")
                        p.op("vector", lambda E, ic=ic: E.tensor_copy(
                            wdbf[:, ic*D:(ic+1)*D], wdld[:]), inc=tag + "_wc")
                    p.wait("tensor", tag + "_wc")
                    for ti in range(TPC):
                        for oc in range(4):
                            pb = (ti * 4 + oc) % 2
                            if p.cnt.get(tag + "_ev", 0) >= 2:
                                p.wait("tensor", tag + "_ev", p.cnt[tag + "_ev"] - 1)
                            for ic in range(11):
                                p.op("tensor", lambda E, pb=pb, ti=ti, oc=oc, ic=ic: E.matmul(
                                    pd[pb][:], mT_sb[:, (ti*11+ic)*128 : (ti*11+ic+1)*128],
                                    wdbf[:, ic*D + oc*512 : ic*D + (oc+1)*512],
                                    start=(ic == 0), stop=(ic == 10)), inc=tag + "_mm")
                            p.wait("vector", tag + "_mm")
                            p.op("vector", lambda E, pb=pb, ti=ti, oc=oc: E.tensor_add(
                                xacc[:, ti*D + oc*512 : ti*D + (oc+1)*512],
                                xacc[:, ti*D + oc*512 : ti*D + (oc+1)*512],
                                pd[pb][:]), inc=tag + "_ev")
                    p.wait("tensor", tag + "_ev")
                    p.wait("sync", tag + "_ev")
                    p.wait("vector", tag + "_ev")
                    p.wait("scalar", tag + "_ev")

        if debug_dump:
            p.wait("gpsimd", "ed_ev")
            for ti in range(TPC):
                p.op("gpsimd", lambda E, ti=ti: E.dma_start(
                    dbg_xacc[ti*128:(ti+1)*128, :], xacc[:, ti*D:(ti+1)*D]),
                    inc="dbgx", k=16)
            p.wait("vector", "dbgx")
            p.wait("tensor", "dbgx")
        with ExitStack() as sS, ExitStack() as sP:
            wldg = sb(sS, nc, "swldg", [128, 16 * 512], F32)
            wgbf = sb(sS, nc, "swgbf", [128, 16 * 512], BF16)
            sgm = sb(sS, nc, "ssgm", [128, 8 * 512], BF16)
            mch = sb(sS, nc, "smch", [128, 512], BF16)
            mTb = sb(sS, nc, "smTb", [128, 512], BF16)
            pg = [ps(sP, nc, f"spg{i}", [128, 512], F32) for i in range(2)]
            pu = [ps(sP, nc, f"spu{i}", [128, 512], F32) for i in range(2)]
            ptile = [ps(sP, nc, f"spt{i}", [128, 128], BF16) for i in range(2)]
            for oc in range(11):
                of = oc * 512
                tag = "sh"
                p.wait("sync", tag + "_umm")
                p.op("sync", lambda E, of=of: E.dma_start(
                    wldg[:].rearrange("p (a o) -> p a o", a=16),
                    wsg[:, of:of+512].rearrange("(a p) o -> p a o", p=128)),
                    inc=tag + "_wdma", k=16)
                p.wait("vector", tag + "_wdma")
                p.op("vector", lambda E: E.tensor_copy(wgbf[:], wldg[:]), inc=tag + "_wcast")
                p.wait("tensor", tag + "_wcast")
                for ti in range(TPC):
                    pb = ti % 2
                    if p.cnt.get(tag + "_sil", 0) >= 2:
                        p.wait("tensor", tag + "_sil", p.cnt[tag + "_sil"] - 1)
                    for j in range(16):
                        p.op("tensor", lambda E, pb=pb, j=j, ti=ti: E.matmul(
                            pg[pb][:], h2Ts[:, j*TOK + ti*128 : j*TOK + (ti+1)*128],
                            wgbf[:, j*512:(j+1)*512], start=(j == 0), stop=(j == 15)),
                            inc=tag + "_gmm")
                    p.wait("scalar", tag + "_gmm")
                    p.op("scalar", lambda E, pb=pb, ti=ti: E.activation(
                        sgm[:, ti*512:(ti+1)*512], pg[pb][:], AF.Silu), inc=tag + "_sil")
                p.wait("sync", tag + "_gmm")
                p.op("sync", lambda E, of=of: E.dma_start(
                    wldg[:].rearrange("p (a o) -> p a o", a=16),
                    wsu[:, of:of+512].rearrange("(a p) o -> p a o", p=128)),
                    inc=tag + "_wdma", k=16)
                p.wait("vector", tag + "_wdma")
                p.wait("vector", tag + "_sil")
                p.op("vector", lambda E: E.tensor_copy(wgbf[:], wldg[:]), inc=tag + "_wcast")
                p.wait("tensor", tag + "_wcast")
                for ti in range(TPC):
                    pb = ti % 2
                    if p.cnt.get(tag + "_tp", 0) >= 4:
                        p.wait("vector", tag + "_tp", p.cnt[tag + "_tp"])
                    for j in range(16):
                        p.op("tensor", lambda E, pb=pb, j=j, ti=ti: E.matmul(
                            pu[pb][:], h2Ts[:, j*TOK + ti*128 : j*TOK + (ti+1)*128],
                            wgbf[:, j*512:(j+1)*512], start=(j == 0), stop=(j == 15)),
                            inc=tag + "_umm")
                    p.wait("vector", tag + "_umm")
                    p.op("vector", lambda E, pb=pb, ti=ti: E.tensor_mul(
                        mch[:], pu[pb][:], sgm[:, ti*512:(ti+1)*512]), inc=tag + "_mev")
                    p.wait("tensor", tag + "_mev")
                    for ic in range(4):
                        tb = ic % 2
                        if p.cnt.get(tag + "_tev", 0) >= 2:
                            p.wait("tensor", tag + "_tev", p.cnt[tag + "_tev"] - 1)
                        p.op("tensor", lambda E, tb=tb, ic=ic: E.transpose(
                            ptile[tb][:], mch[:, ic*128:(ic+1)*128], ident[:]),
                            inc=tag + "_tp")
                        p.wait("vector", tag + "_tp")
                        if p.cnt.get(tag + "_mdma", 0) > 0:
                            p.wait("vector", tag + "_mdma", p.cnt[tag + "_mdma"])
                        p.op("vector", lambda E, tb=tb, ic=ic: E.tensor_copy(
                            mTb[:, ic*128:(ic+1)*128], ptile[tb][:]), inc=tag + "_tev")
                    p.wait("gpsimd", tag + "_tev")
                    p.op("gpsimd", lambda E, oc=oc, ti=ti: E.dma_start(
                        mTs[oc*512:(oc+1)*512, ti*128:(ti+1)*128].rearrange(
                            "(a p) t -> p a t", p=128),
                        mTb[:].rearrange("p (a t) -> p a t", a=4)), inc=tag + "_mdma", k=16)
            p.wait("gpsimd", "sh_mdma")
            p.wait("sync", "sh_mdma")
            p.wait("tensor", "sh_mdma")
            p.wait("vector", "sh_mdma")
            p.wait("scalar", "sh_mdma")

        with ExitStack() as sS, ExitStack() as sP:
            wsld = sb(sS, nc, "wsld", [128, 11 * 512], F32)
            wsbf = sb(sS, nc, "wsbf", [128, 11 * 512], BF16)
            mTp = sb(sS, nc, "mTp", [128, 11 * 128], BF16)
            psh = [ps(sP, nc, f"psh{i}", [128, 512], F32) for i in range(8)]
            tag = "sd"
            for oc in range(4):
                for icg in range(4):
                    p.wait("sync", tag + "_mm")
                    p.op("sync", lambda E, oc=oc, icg=icg: E.dma_start(
                        wsld[:].rearrange("p (a o) -> p a o", a=11),
                        wsd[icg*1408:(icg+1)*1408, oc*512:(oc+1)*512].rearrange(
                            "(a p) o -> p a o", p=128)), inc=tag + "_wd", k=16)
                    p.wait("vector", tag + "_wd")
                    p.op("vector", lambda E: E.tensor_copy(wsbf[:], wsld[:]), inc=tag + "_wc")
                    for ti in range(TPC):
                        p.op("sync", lambda E, icg=icg, ti=ti: E.dma_start(
                            mTp[:].rearrange("p (a t) -> p a t", a=11),
                            mTs[icg*1408:(icg+1)*1408, ti*128:(ti+1)*128].rearrange(
                                "(a p) t -> p a t", p=128)), inc=tag + "_md", k=16)
                        p.wait("tensor", tag + "_md")
                        p.wait("tensor", tag + "_wc")
                        for ic in range(11):
                            p.op("tensor", lambda E, ti=ti, ic=ic, icg=icg: E.matmul(
                                psh[ti][:], mTp[:, ic*128:(ic+1)*128],
                                wsbf[:, ic*512:(ic+1)*512],
                                start=(icg == 0 and ic == 0),
                                stop=(icg == 3 and ic == 10),
                                skip_group_check=True), inc=tag + "_mm")
                        p.wait("sync", tag + "_mm")
                p.wait("vector", tag + "_mm")
                for ti in range(TPC):
                    p.op("vector", lambda E, ti=ti, oc=oc: E.scalar_tensor_tensor(
                        xacc[:, ti*D + oc*512 : ti*D + (oc+1)*512],
                        psh[ti][:], sg_sb[:, ti:ti+1],
                        xacc[:, ti*D + oc*512 : ti*D + (oc+1)*512],
                        ALU.mult, ALU.add), inc=tag + "_ev")
                p.wait("tensor", tag + "_ev")
            p.wait("sync", "sd_ev")
            p.wait("scalar", "sd_ev")
            p.wait("gpsimd", "sd_ev")

        xb = [sb(stack, nc, f"lxb{i}", [128, D], F32) for i in range(2)]
        sq = sb(stack, nc, "lsq", [128, D], F32)
        ob = [sb(stack, nc, f"lob{i}", [128, D], F32) for i in range(2)]
        mt = sb(stack, nc, "lmt", [128, 4], F32)
        vt_ = sb(stack, nc, "lvt", [128, 4], F32)
        eps_t = sb(stack, nc, "leps", [128, 1], F32)
        p.op("vector", lambda E: E.memset(eps_t[:], 1e-5), inc="ln_v")
        for ti in range(TPC):
            b = ti % 2
            if p.cnt.get("ln_dma", 0) >= 32:
                p.wait("sync", "ln_v", p.cnt.get("ln_v", 0) - 1)
            p.op("sync", lambda E, b=b, ti=ti: E.dma_start(xb[b][:], x1[ti*128:(ti+1)*128, :]),
                 inc="ln_dma", k=16)
            p.wait("vector", "ln_dma")
            if p.cnt.get("ln_out", 0) >= 32:
                p.wait("vector", "ln_out", p.cnt["ln_out"] - 16)
            p.op("vector", lambda E, b=b, ti=ti: E.tensor_add(
                ob[b][:], xb[b][:], xacc[:, ti*D:(ti+1)*D]))
            emit_ln_tile(p, ob[b][:], ob[b][:], sq, mt, vt_, D, eps_t, "ln")
            p.op("vector", lambda E: E.memset(mt[:, 3:4], 0.0), inc="ln_v")
            p.wait("gpsimd", "ln_v")
            p.op("gpsimd", lambda E, b=b, ti=ti: E.dma_start(x4[ti*128:(ti+1)*128, :], ob[b][:]),
                 inc="ln_out", k=16)
        p.wait("gpsimd", "ln_out")
        if fc is None:
            p.finalize()
    return nc


def build_l0f_qkv(fc=None):
    nc = bass.Bass("TRN2", target_bir_lowering=False) if fc is None else fc.nc
    x = _dr(fc, nc, "x", [TOK, D], F32, "ExternalInput")
    wq = _dr(fc, nc, "wq", [D, D], F32, "ExternalInput")
    wk = _dr(fc, nc, "wk", [D, D], F32, "ExternalInput")
    wv = _dr(fc, nc, "wv", [D, D], F32, "ExternalInput")
    brow = _dr(fc, nc, "brow", [1, 3 * D], F32, "ExternalInput")
    gam = _dr(fc, nc, "gam", [128, D], F32, "ExternalInput")
    ctq = _dr(fc, nc, "ctq", [HD, TOK], F32, "ExternalInput")
    stq = _dr(fc, nc, "stq", [HD, TOK], F32, "ExternalInput")
    ctk = _dr(fc, nc, "ctk", [HD, TOK], F32, "ExternalInput")
    stk = _dr(fc, nc, "stk", [HD, TOK], F32, "ExternalInput")
    identd = _dr(fc, nc, "ident", [128, 128], BF16, "ExternalInput")
    qf = _dr(fc, nc, "qf", [D, TOK], BF16, "ExternalOutput")
    kf = _dr(fc, nc, "kf", [D, TOK], BF16, "ExternalOutput")
    vt = _dr(fc, nc, "vt", [TOK, D], BF16, "ExternalOutput")

    with ExitStack() as stack:
        p = Prog(nc, stack) if fc is None else fc.p
        xT = sb(stack, nc, "xT", [128, 16 * TOK], BF16)
        ident = sb(stack, nc, "id", [128, 128], BF16)
        brow_b = sb(stack, nc, "browb", [1, 3 * D], BF16)
        ones_b = sb(stack, nc, "ones", [1, TOK], BF16)
        CTq = sb(stack, nc, "CTq", [128, TOK], F32)
        STq = sb(stack, nc, "STq", [128, TOK], F32)
        CTk = sb(stack, nc, "CTk", [128, TOK], F32)
        STk = sb(stack, nc, "STk", [128, TOK], F32)
        qe = sb(stack, nc, "qe", [128, TOK], F32)
        qsw = sb(stack, nc, "qsw", [128, TOK], F32)
        qbf = sb(stack, nc, "qbf", [128, TOK], BF16)
        _BFV[id(qsw)] = qbf
        vout = sb(stack, nc, "vout", [128, 256], BF16)
        gam_sb = sb(stack, nc, "gam_sb", [128, D], F32)
        eps_t = sb(stack, nc, "eps_t", [128, 1], F32)
        rs = sb(stack, nc, "rs", [128, 4], F32)
        ptile = [ps(stack, nc, f"pt{i}", [128, 128], BF16) for i in range(2)]
        pp = [ps(stack, nc, f"pp{i}", [128, 512], F32) for i in range(2)]

        with ExitStack() as stackA:
            xbuf = [sb(stackA, nc, f"xb{i}", [128, D], F32) for i in range(2)]
            xbf = [sb(stackA, nc, f"xf{i}", [128, D], BF16) for i in range(2)]

            p.op("sync", lambda E: E.dma_start(ident[:], identd[:, :]), inc="c_dma", k=16)
            p.op("sync", lambda E: E.dma_start(CTq[:], ctq[:, :]), inc="c_dma", k=16)
            p.op("sync", lambda E: E.dma_start(STq[:], stq[:, :]), inc="c_dma", k=16)
            p.op("sync", lambda E: E.dma_start(CTk[:], ctk[:, :]), inc="c_dma", k=16)
            p.op("sync", lambda E: E.dma_start(STk[:], stk[:, :]), inc="c_dma", k=16)
            p.op("sync", lambda E: E.dma_start(gam_sb[:], gam[:, :]), inc="c_dma", k=16)
            for i in range(3):
                if i > 0:
                    p.wait("sync", "c_v", p.cnt.get("c_v", 0))
                p.op("sync", lambda E, i=i: E.dma_start(
                    xbuf[1][0:1, 0:D], brow[0:1, i*D:(i+1)*D]), inc="c_dma", k=16)
                p.wait("vector", "c_dma")
                p.op("vector", lambda E, i=i: E.tensor_copy(
                    brow_b[0:1, i*D:(i+1)*D], xbuf[1][0:1, 0:D]), inc="c_v")
            p.op("vector", lambda E: E.memset(ones_b[:], 1.0), inc="c_v")
            p.op("vector", lambda E: E.memset(eps_t[:], 1e-6), inc="c_v")
            p.wait("tensor", "c_v")
            p.wait("sync", "c_v")
            p.wait("scalar", "c_v")

            emit_load_xT(p, x, xT, xbuf, xbf, ident, ptile, rms=(gam_sb, eps_t, rs))
            p.wait("tensor", "xt_evac")
            p.wait("sync", "xt_evac")
            p.wait("gpsimd", "xt_evac")
            p.wait("scalar", "xt_evac")
            p.wait("vector", "xt_evac")

        wld = [sb(stack, nc, f"wld{i}", [128, D], F32) for i in range(2)]
        wbf = [sb(stack, nc, f"wbf{i}", [128, D], BF16) for i in range(2)]
        wldv = [sb(stack, nc, f"wldv{i}", [128, 16 * 256], F32) for i in range(2)]
        wbfv = [sb(stack, nc, f"wbfv{i}", [128, 16 * 256], BF16) for i in range(2)]

        emit_fm_proj_rope(p, nc, wq, NH, 0, xT, None, wld, wbf, brow_b, ones_b,
                          CTq, STq, qe, qsw, qf, pp, "fm", rope=True, bias_base=0)
        p.wait("tensor", "fm_evac")
        p.wait("vector", "fm_sw")
        p.wait("vector", "fm_out")
        p.wait("sync", "fm_mm")
        emit_fm_proj_rope(p, nc, wk, NH, 0, xT, None, wld, wbf, brow_b, ones_b,
                          CTk, STk, qe, qsw, kf, pp, "fm", rope=True, bias_base=D)
        p.wait("tensor", "fm_evac")
        emit_tm_proj(p, wv, 0, D, xT, wldv, wbfv, brow_b, ones_b, vt, vout,
                     pp, "fm", ocw=256, bias_base=2 * D)

        p.wait("gpsimd", "fm_out")
        p.wait("gpsimd", "fm_out")
        p.wait("gpsimd", "fm_out")
        if fc is None:
            p.finalize()
    return nc


def _prep_l0f_qkv_inputs(inputs, x1_full):
    W_q = inputs["L0_ffn_wq"]; W_k = inputs["L0_ffn_wk"]; W_v = inputs["L0_ffn_wv"]
    brow = np.concatenate([inputs["L0_ffn_bq"], inputs["L0_ffn_bk"],
                           inputs["L0_ffn_bv"]])[None, :].astype(np.float32)
    gam = np.broadcast_to(inputs["L0_ffn_w_ln1"][None, :], (128, D)).astype(np.float32)
    gam = np.ascontiguousarray(gam)
    ident = np.eye(128, dtype=np.float32).astype(BF)
    qscale = HD ** -0.5
    maps = []
    for c in range(N_CORES):
        rows = core_rows(c)
        pos = rows.astype(np.float64)
        ctq, stq = l0f_tables(pos, qscale)
        ctk, stk = l0f_tables(pos, 1.0)
        maps.append({
            "x": np.ascontiguousarray(x1_full[rows]),
            "wq": W_q, "wk": W_k, "wv": W_v, "brow": brow, "gam": gam,
            "ident": ident, "ctq": ctq, "stq": stq, "ctk": ctk, "stk": stk,
        })
    return maps


def emit_ln_tile(p, x_ap, out_ap, sq, mt, vt_, n_cols, eps_t, tag, gamma=None):
    inv_n = 1.0 / n_cols
    p.op("vector", lambda E: E.reduce_sum(mt[:, 0:1], x_ap, axis=AX.X))
    p.op("vector", lambda E: E.tensor_copy(mt[:, 1:2], mt[:, 0:1]), inc=tag + "_v")
    p.wait("scalar", tag + "_v")
    p.op("scalar", lambda E: E.activation(mt[:, 2:3], mt[:, 0:1], AF.Copy, scale=-inv_n))
    p.op("scalar", lambda E: E.activation(sq[:, 0:256], x_ap[:, 0:256], AF.Copy),
         inc=tag + "_s")
    p.wait("vector", tag + "_s")
    p.op("vector", lambda E: E.tensor_scalar(out_ap, x_ap, mt[:, 2:3], None, ALU.add),
         inc=tag + "_v")
    p.wait("scalar", tag + "_v")
    p.op("scalar", lambda E: E.activation(sq[:], out_ap, AF.Square, accum_out=vt_[:, 0:1]))
    p.op("scalar", lambda E: E.activation(vt_[:, 1:2], mt[:, 2:3], AF.Copy))
    p.op("scalar", lambda E: E.activation(vt_[:, 2:3], vt_[:, 0:1], AF.Ln,
                                          scale=inv_n, bias=eps_t[:]))
    p.op("scalar", lambda E: E.activation(sq[:, 0:256], out_ap[:, 0:256], AF.Copy))
    p.op("scalar", lambda E: E.activation(vt_[:, 3:4], vt_[:, 2:3], AF.Exp, scale=-0.5))
    p.op("scalar", lambda E: E.activation(sq[:, 0:256], out_ap[:, 0:256], AF.Copy),
         inc=tag + "_s")
    p.wait("vector", tag + "_s")
    p.op("vector", lambda E: E.tensor_scalar(out_ap, out_ap, vt_[:, 3:4], None, ALU.mult),
         inc=tag + "_v")
    if gamma is not None:
        p.op("vector", lambda E: E.tensor_mul(out_ap, out_ap, gamma))


def build_attn(maskval_is_l0f=False, out_ln=True, nheads=NH, debug_dump=False, full=False, fc=None):
    nc = bass.Bass("TRN2", target_bir_lowering=False) if fc is None else fc.nc
    qf8 = _dr(fc, nc, "qf8", [D, TOK], BF16, "ExternalInput")
    if fc is None:
        kT = nc.dram_tensor("kT", [D, S], BF16, kind="ExternalInput")
        vpre = nc.dram_tensor("vpre", [D, S], BF16, kind="ExternalInput")
        x = nc.dram_tensor("x", [TOK, D], F32, kind="ExternalInput")
    else:
        agk = fc.io["agk"]
        agv = fc.io["agv"]
    maskd = _dr(fc, nc, "mask", [TOK, 1024], F32, "ExternalInput") if not full else \
            (_dr(fc, nc, "mask", [TOK, 1024], F32, "ExternalInput") if fc is None else None)
    identd = _dr(fc, nc, "ident", [128, 128], BF16, "ExternalInput")
    if out_ln:
        y = nc.dram_tensor("y", [TOK, D], F32, kind="ExternalOutput")
    else:
        y = _dr(fc, nc, "y", [TOK, D], BF16, "ExternalOutput")
    if debug_dump:
        dbg_s = nc.dram_tensor("dbg_s", [128, 1024], F32, kind="ExternalOutput")
        dbg_p = nc.dram_tensor("dbg_p", [128, 1024], BF16, kind="ExternalOutput")
        dbg_mz = nc.dram_tensor("dbg_mz", [128, 2], F32, kind="ExternalOutput")

    with ExitStack() as stack:
        p = Prog(nc, stack) if fc is None else fc.p
        attn_acc = sb(stack, nc, "attn_acc", [128, TPC * D], F32)
        ident = sb(stack, nc, "id", [128, 128], BF16)
        qhead = sb(stack, nc, "qhead", [128, TOK], BF16)
        PT_sb = [sb(stack, nc, f"PT{i}", [128, 128], BF16) for i in range(4)]
        mrow = sb(stack, nc, "mrow", [128, 4], F32)
        zrow = sb(stack, nc, "zrow", [128, 4], F32)
        jnk = sb(stack, nc, "jnk", [128, 256], F32)
        pp = [ps(stack, nc, f"pp{i}", [128, 512], F32) for i in range(2)]
        ppv = ps(stack, nc, "ppv", [128, 128], F32)
        ptP = [ps(stack, nc, f"ptP{i}", [128, 128], BF16) for i in range(4)]

        with ExitStack() as stackA:
            khead = sb(stackA, nc, "khead", [128, S], BF16)
            vhead = sb(stackA, nc, "vhead", [128, S], BF16)
            S_sb = sb(stackA, nc, "S_sb", [128, S], F32)
            P_bf = sb(stackA, nc, "P_bf", [128, S], BF16)
            mask_sb = sb(stackA, nc, "mask_sb", [128, TPC * 1024], F32)

            p.op("sync", lambda E: E.dma_start(ident[:], identd[:, :]), inc="c_dma", k=16)
            if maskd is not None:
                p.op("sync", lambda E: E.dma_start(
                    mask_sb[:].rearrange("p (s m) -> p s m", s=TPC),
                    maskd[:, :].rearrange("(s p) m -> p s m", p=128)), inc="c_dma", k=16)
            p.wait("tensor", "c_dma")
            p.wait("vector", "c_dma")

            nT = 0
            for h in range(nheads):
                if p.cnt.get("hd_done", 0) > 0:
                    p.wait("sync", "hd_done", p.cnt["hd_done"])
                p.op("sync", lambda E, h=h: E.dma_start(qhead[:], qf8[h*128:(h+1)*128, :]),
                     inc="h_dma", k=16)
                if fc is None:
                    p.op("sync", lambda E, h=h: E.dma_start(khead[:], kT[h*128:(h+1)*128, :]),
                         inc="h_dma", k=16)
                    p.op("sync", lambda E, h=h: E.dma_start(vhead[:], vpre[h*128:(h+1)*128, :]),
                         inc="h_dma", k=16)
                else:
                    kv_ = khead[:].rearrange("p (s r d) -> p s r d", r=8, d=128)
                    vv_ = vhead[:].rearrange("p (s r d) -> p s r d", r=8, d=128)
                    for r in range(8):
                        p.op("sync", lambda E, h=h, r=r, kv_=kv_: E.dma_start(
                            kv_[:, :, r, :],
                            agk[r*D + h*128 : r*D + (h+1)*128, :].rearrange(
                                "p (s d) -> p s d", d=128)), inc="h_dma", k=16)
                        p.op("sync", lambda E, h=h, r=r, vv_=vv_: E.dma_start(
                            vv_[:, :, r, :],
                            agv[r*TOK : (r+1)*TOK, h*128:(h+1)*128].rearrange(
                                "(s p) d -> p s d", p=128)), inc="h_dma", k=16)
                p.wait("tensor", "h_dma")
                for s in range(TPC):
                    ext = S if full else (s + 1) * 1024
                    nqk = ext // 512
                    npv = ext // 128
                    for kc in range(nqk):
                        pb = kc % 2
                        if p.cnt.get("qk_evac", 0) >= 2:
                            p.wait("tensor", "qk_evac", p.cnt.get("qk_evac", 0) - 1)
                        p.op("tensor", lambda E, pb=pb, s=s, kc=kc: E.matmul(
                            pp[pb][:], qhead[:, s*128:(s+1)*128],
                            khead[:, kc*512:(kc+1)*512], start=True, stop=True),
                            inc="qk_mm")
                        p.wait("vector", "qk_mm")
                        p.op("vector", lambda E, pb=pb, kc=kc: E.tensor_copy(
                            S_sb[:, kc*512:(kc+1)*512], pp[pb][:]), inc="qk_evac")
                    if not full:
                        p.op("vector", lambda E, s=s, ext=ext: E.tensor_add(
                            S_sb[:, ext-1024:ext], S_sb[:, ext-1024:ext],
                            mask_sb[:, s*1024:(s+1)*1024]))
                    p.op("vector", lambda E, ext=ext: E.reduce_max(
                        mrow[:, 0:1], S_sb[:, 0:ext], axis=AX.X, negate=True))
                    p.op("vector", lambda E: E.tensor_copy(mrow[:, 1:2], mrow[:, 0:1]),
                         inc="v_stat")
                    p.wait("scalar", "v_stat")
                    p.op("scalar", lambda E, ext=ext: E.activation(
                        P_bf[:, 0:ext], S_sb[:, 0:ext], AF.Exp, bias=mrow[:, 0:1],
                        accum_out=zrow[:, 0:1]))
                    p.op("scalar", lambda E: E.activation(
                        jnk[:], S_sb[:, 0:256], AF.Copy), inc="s_exp")
                    p.wait("vector", "s_exp")
                    p.op("vector", lambda E: E.reciprocal(zrow[:, 1:2], zrow[:, 0:1]),
                         inc="v_zinv")
                    p.wait("tensor", "s_exp")
                    for g0 in range(0, npv, 4):
                        gn = min(4, npv - g0)
                        if p.cnt.get("pt_evac", 0) > 0:
                            p.wait("tensor", "pt_evac", p.cnt["pt_evac"])
                        for i_ in range(gn):
                            p.op("tensor", lambda E, i_=i_, ck=g0+i_: E.transpose(
                                ptP[i_][:], P_bf[:, ck*128:(ck+1)*128], ident[:]),
                                inc="pt_tp")
                        p.wait("vector", "pt_tp")
                        if p.cnt.get("pv_mm", 0) > 0:
                            p.wait("vector", "pv_mm", p.cnt["pv_mm"])
                        for i_ in range(gn):
                            p.op("vector", lambda E, i_=i_: E.tensor_copy(
                                PT_sb[i_][:], ptP[i_][:]), inc="pt_evac")
                        p.wait("tensor", "pt_evac")
                        for i_ in range(gn):
                            p.op("tensor", lambda E, i_=i_, ck=g0+i_, npv=npv: E.matmul(
                                ppv[:], PT_sb[i_][:], vhead[:, ck*128:(ck+1)*128],
                                start=(ck == 0), stop=(ck == npv - 1)), inc="pv_mm")
                    if debug_dump and h == 0 and s == 0:
                        p.wait("gpsimd", "v_zinv")
                        p.op("gpsimd", lambda E, ext=ext: E.dma_start(dbg_s[:, :], S_sb[:, 0:1024]), inc="dbg", k=16)
                        p.op("gpsimd", lambda E, ext=ext: E.dma_start(dbg_p[:, :], P_bf[:, 0:1024]), inc="dbg", k=16)
                        p.op("gpsimd", lambda E: E.dma_start(dbg_mz[:, 0:1], mrow[:, 0:1], allow_slow_non_contiguous=True), inc="dbg", k=16)
                        p.op("gpsimd", lambda E: E.dma_start(dbg_mz[:, 1:2], zrow[:, 1:2], allow_slow_non_contiguous=True), inc="dbg", k=16)
                        p.wait("vector", "dbg")
                    p.wait("vector", "pv_mm")
                    p.wait("vector", "v_zinv")
                    p.op("vector", lambda E, s=s, h=h: E.tensor_scalar(
                        attn_acc[:, s*D + h*128 : s*D + (h+1)*128], ppv[:],
                        zrow[:, 1:2], None, ALU.mult), inc="v_ep")
                    p.wait("tensor", "v_ep")
                p.op("vector", lambda E: E.memset(mrow[:, 3:4], 0.0), inc="hd_done")

        xb = [sb(stack, nc, f"lxb{i}", [128, D], F32) for i in range(2)]
        sq = sb(stack, nc, "lsq", [128, D], F32)
        ob = [sb(stack, nc, f"lob{i}", [128, D], F32 if out_ln else BF16) for i in range(2)]
        mt = sb(stack, nc, "lmt", [128, 4], F32)
        vt_ = sb(stack, nc, "lvt", [128, 4], F32)
        eps_t = sb(stack, nc, "leps", [128, 1], F32)
        p.op("vector", lambda E: E.memset(eps_t[:], 1e-5), inc="ln_v")
        p.wait("sync", "v_ep")
        for ti in range(TPC):
            b = ti % 2
            if out_ln:
                if p.cnt.get("ln_dma", 0) >= 32:
                    p.wait("sync", "ln_v", p.cnt.get("ln_v", 0) - 1)
                p.op("sync", lambda E, b=b, ti=ti: E.dma_start(xb[b][:], x[ti*128:(ti+1)*128, :]),
                     inc="ln_dma", k=16)
                p.wait("vector", "ln_dma")
            if p.cnt.get("ln_out", 0) >= 32:
                p.wait("vector", "ln_out", p.cnt["ln_out"] - 16)
            if out_ln:
                p.op("vector", lambda E, b=b, ti=ti: E.tensor_add(
                    ob[b][:], xb[b][:], attn_acc[:, ti*D:(ti+1)*D]))
                emit_ln_tile(p, ob[b][:], ob[b][:], sq, mt, vt_, D, eps_t, "ln")
                p.op("vector", lambda E: E.memset(mt[:, 3:4], 0.0), inc="ln_v")
            else:
                p.op("vector", lambda E, b=b, ti=ti: E.tensor_copy(
                    ob[b][:], attn_acc[:, ti*D:(ti+1)*D]), inc="ln_v")
            p.wait("gpsimd", "ln_v")
            p.op("gpsimd", lambda E, b=b, ti=ti: E.dma_start(y[ti*128:(ti+1)*128, :], ob[b][:]),
                 inc="ln_out", k=16)
        p.wait("gpsimd", "ln_out")
        if fc is None:
            p.finalize()
    return nc



def _prep_l0a_qkv_inputs(inputs):
    x_full = np.ascontiguousarray(inputs["L0_attn_x"].reshape(S, D))
    W = inputs["L0_attn_Wqkv_w"]
    bvec = inputs["L0_attn_Wqkv_b"]
    perm = l0a_perm()
    colidx = np.empty(3 * D, np.int64)
    for sec in range(2):
        for h in range(NH):
            base = sec * D + h * HD
            colidx[base:base + HD] = base + perm
    colidx[2 * D:] = np.arange(2 * D, 3 * D)
    wT = np.ascontiguousarray(W.T[:, colidx])
    brow = np.ascontiguousarray(bvec[colidx][None, :])
    ident = np.eye(128, dtype=np.float32).astype(BF)
    maps = []
    qscale = 1.0 / np.sqrt(HD)
    for c in range(N_CORES):
        rows = core_rows(c)
        pos = rows.astype(np.float64)
        ctq, stq = l0a_tables(pos, qscale)
        ctk, stk = l0a_tables(pos, 1.0)
        maps.append({
            "x": np.ascontiguousarray(x_full[rows]),
            "wT": wT, "brow": brow, "ident": ident,
            "ctq": ctq, "stq": stq, "ctk": ctk, "stk": stk,
        })
    return maps


def emit_fp4_decode(p, pk_ap, nwords, wordoff, nf1, vals, t1, t2, ir1, ir2,
                    sc_ap_of_g, wdec_bf, eng="vector"):
    F = nwords * 8
    nf1v = nf1[:, 0:F].rearrange("p (w k) -> p w k", k=8)
    for k in range(8):
        if k > 0:
            p.op(eng, lambda E, k=k: E.tensor_single_scalar(
                ir1[:, 0:nwords], pk_ap, 4 * k, ALU.logical_shift_right))
            src = lambda: ir1[:, 0:nwords]
        else:
            src = lambda: pk_ap
        p.op(eng, lambda E, src=src: E.tensor_single_scalar(
            ir2[:, 0:nwords], src(), 15, ALU.bitwise_and))
        p.op(eng, lambda E, k=k: E.tensor_copy(nf1v[:, :, k], ir2[:, 0:nwords]))
    p.op(eng, lambda E: E.tensor_single_scalar(vals[:, 0:F], nf1[:, 0:F], 8.0, ALU.is_ge))
    p.op(eng, lambda E: E.scalar_tensor_tensor(
        nf1[:, 0:F], vals[:, 0:F], -8.0, nf1[:, 0:F], ALU.mult, ALU.add))
    p.op(eng, lambda E: E.tensor_scalar(vals[:, 0:F], vals[:, 0:F], -2.0, 1.0,
                                        ALU.mult, ALU.add))
    p.op(eng, lambda E: E.tensor_scalar(t1[:, 0:F], nf1[:, 0:F], 6.0, 0.0,
                                        ALU.subtract, ALU.max))
    p.op(eng, lambda E: E.tensor_scalar(t2[:, 0:F], nf1[:, 0:F], 4.0, 0.0,
                                        ALU.subtract, ALU.max))
    p.op(eng, lambda E: E.tensor_add(t2[:, 0:F], t2[:, 0:F], nf1[:, 0:F]))
    p.op(eng, lambda E: E.scalar_tensor_tensor(
        t1[:, 0:F], t2[:, 0:F], 0.5, t1[:, 0:F], ALU.mult, ALU.add))
    p.op(eng, lambda E: E.tensor_mul(vals[:, 0:F], t1[:, 0:F], vals[:, 0:F]))
    ngr = F // 128
    for g in range(ngr):
        p.op(eng, lambda E, g=g: E.tensor_scalar(
            wdec_bf[:, wordoff*8 + g*128 : wordoff*8 + (g+1)*128],
            vals[:, g*128:(g+1)*128], sc_ap_of_g(g), None, ALU.mult))


def build_l1f(fc=None):
    nc = bass.Bass("TRN2", target_bir_lowering=False) if fc is None else fc.nc
    x5 = _dr(fc, nc, "x5", [TOK, D], F32, "ExternalInput")
    gup = _dr(fc, nc, "gup", [11264, 256], I32, "ExternalInput")
    gus = _dr(fc, nc, "gus", [11264, 16], F16, "ExternalInput")
    dnp = _dr(fc, nc, "dnp", [D, 704], I32, "ExternalInput")
    dns = _dr(fc, nc, "dns", [D, 44], F16, "ExternalInput")
    identd = _dr(fc, nc, "ident", [128, 128], BF16, "ExternalInput")
    x6 = _dr(fc, nc, "x6", [TOK, D], F32, "ExternalOutput")
    mTs = _dr(fc, nc, "mTs", [5632, TOK], BF16, "Internal")
    wdnT = _dr(fc, nc, "wdnT", [5632, D], BF16, "Internal")

    with ExitStack() as stack:
        p = Prog(nc, stack) if fc is None else fc.p
        xT = sb(stack, nc, "xT", [128, 16 * TOK], BF16)
        yacc = sb(stack, nc, "yacc", [128, TPC * D], F32)
        ident = sb(stack, nc, "id", [128, 128], BF16)

        with ExitStack() as sA:
            xbuf = [sb(sA, nc, f"xb{i}", [128, D], F32) for i in range(2)]
            xbf = [sb(sA, nc, f"xf{i}", [128, D], BF16) for i in range(2)]
            with ExitStack() as sP:
                ptile = [ps(sP, nc, f"pt{i}", [128, 128], BF16) for i in range(2)]
                p.op("sync", lambda E: E.dma_start(ident[:], identd[:, :]), inc="c_dma", k=16)
                p.wait("vector", "c_dma")
                p.wait("tensor", "c_dma")
                emit_load_xT(p, x5, xT, xbuf, xbf, ident, ptile)
                for e_ in ENGINES:
                    p.wait(e_, "xt_evac")

        def _phase1():
            with ExitStack() as sA, ExitStack() as sP:
                pk = [sb(sA, nc, f"pk{i}", [128, 704], I32) for i in range(2)]
                scf16 = sb(sA, nc, "scf16", [128, 44], F16)
                scf = sb(sA, nc, "scf", [128, 44], F32)
                nf1 = sb(sA, nc, "nf1", [128, 2048], F32)
                vals = sb(sA, nc, "vals", [128, 2048], F32)
                t1 = sb(sA, nc, "t1", [128, 2048], F32)
                t2 = sb(sA, nc, "t2", [128, 2048], F32)
                ir1 = sb(sA, nc, "ir1", [128, 256], I32)
                ir2 = sb(sA, nc, "ir2", [128, 256], I32)
                wdec = sb(sA, nc, "wdec", [128, 5632], BF16)
                tpb = sb(sA, nc, "tpb", [128, 44 * 128], BF16)
                ptile = [ps(sP, nc, f"ptd{i}", [128, 128], BF16) for i in range(2)]
                CH = [(0, 256), (256, 256), (512, 192)]
                for ot in range(16):
                    b = ot % 2
                    if ot >= 2:
                        p.wait("sync", "dd_v", p.cnt.get("dd_v", 0))
                    p.op("sync", lambda E, b=b, ot=ot: E.dma_start(
                        pk[b][:], dnp[ot*128:(ot+1)*128, :]), inc="dd_dma", k=16)
                    p.op("sync", lambda E, ot=ot: E.dma_start(
                        scf16[:], dns[ot*128:(ot+1)*128, :]), inc="dd_dma", k=16)
                    p.wait("vector", "dd_dma")
                    p.op("vector", lambda E: E.tensor_copy(scf[:], scf16[:]))
                    for (off, nw) in CH:
                        emit_fp4_decode(p, pk[b][:, off:off+nw], nw, off, nf1, vals, t1, t2,
                                        ir1, ir2,
                                        (lambda g, off=off: scf[:, off//16 + g: off//16 + g + 1]),
                                        wdec)
                    p.op("vector", lambda E: E.tensor_copy(scf16[:], scf16[:]), inc="dd_v")
                    p.wait("tensor", "dd_v")
                    for ic in range(44):
                        pb = ic % 2
                        if p.cnt.get("dd_ev", 0) >= 2:
                            p.wait("tensor", "dd_ev", p.cnt["dd_ev"] - 1)
                        p.op("tensor", lambda E, pb=pb, ic=ic: E.transpose(
                            ptile[pb][:], wdec[:, ic*128:(ic+1)*128], ident[:]), inc="dd_tp")
                        p.wait("vector", "dd_tp")
                        if ic < 2 and p.cnt.get("dd_out", 0) > 0:
                            p.wait("vector", "dd_out", p.cnt["dd_out"])
                        p.op("vector", lambda E, pb=pb, ic=ic: E.tensor_copy(
                            tpb[:, ic*128:(ic+1)*128], ptile[pb][:]), inc="dd_ev")
                    p.wait("gpsimd", "dd_ev")
                    p.op("gpsimd", lambda E, ot=ot: E.dma_start(
                        wdnT[:, ot*128:(ot+1)*128].rearrange("(a p) t -> p a t", p=128),
                        tpb[:].rearrange("p (a t) -> p a t", a=44)), inc="dd_out", k=16)
                for e_ in ("sync", "vector", "tensor", "gpsimd", "scalar"):
                    p.wait(e_, "dd_out")

        def _phase2():
            with ExitStack() as sA, ExitStack() as sP:
                pk = [sb(sA, nc, f"gpk{i}", [128, 256], I32) for i in range(2)]
                scf16 = sb(sA, nc, "gscf16", [128, 16], F16)
                scf = sb(sA, nc, "gscf", [128, 16], F32)
                nf1 = sb(sA, nc, "gnf1", [128, 2048], F32)
                vals = sb(sA, nc, "gvals", [128, 2048], F32)
                t1 = sb(sA, nc, "gt1", [128, 2048], F32)
                t2 = sb(sA, nc, "gt2", [128, 2048], F32)
                ir1 = sb(sA, nc, "gir1", [128, 256], I32)
                ir2 = sb(sA, nc, "gir2", [128, 256], I32)
                wdec = sb(sA, nc, "gwdec", [128, 2048], BF16)
                rhs_g = sb(sA, nc, "rhs_g", [128, 16 * 512], BF16)
                rhs_u = sb(sA, nc, "rhs_u", [128, 16 * 512], BF16)
                sgm = sb(sA, nc, "gsgm", [128, 512], BF16)
                mch = sb(sA, nc, "gmch", [128, 512], BF16)
                mTb = sb(sA, nc, "gmTb", [128, 512], BF16)
                ptile = [ps(sP, nc, f"ptg{i}", [128, 128], BF16) for i in range(2)]
                pg = ps(sP, nc, "pgg", [128, 512], F32)
                pu = ps(sP, nc, "pgu", [128, 512], F32)

                def decode_tile_to_rhs(orow0, rhs, subcol, tag):
                    b = (orow0 // 128) % 2
                    p.wait("sync", tag + "_v", p.cnt.get(tag + "_v", 0))
                    p.op("sync", lambda E, b=b, orow0=orow0: E.dma_start(
                        pk[b][:], gup[orow0:orow0+128, :]), inc=tag + "_dma", k=16)
                    p.op("sync", lambda E, orow0=orow0: E.dma_start(
                        scf16[:], gus[orow0:orow0+128, :]), inc=tag + "_dma", k=16)
                    p.wait("vector", tag + "_dma")
                    p.op("vector", lambda E: E.tensor_copy(scf[:], scf16[:]))
                    emit_fp4_decode(p, pk[b][:], 256, 0, nf1, vals, t1, t2, ir1, ir2,
                                    lambda g: scf[:, g:g+1], wdec)
                    p.op("vector", lambda E: E.tensor_copy(scf16[:], scf16[:]), inc=tag + "_v")
                    p.wait("tensor", tag + "_v")
                    for fj in range(16):
                        pb = fj % 2
                        if p.cnt.get(tag + "_ev", 0) >= 2:
                            p.wait("tensor", tag + "_ev", p.cnt[tag + "_ev"] - 1)
                        p.op("tensor", lambda E, pb=pb, fj=fj: E.transpose(
                            ptile[pb][:], wdec[:, fj*128:(fj+1)*128], ident[:]), inc=tag + "_tp")
                        p.wait("vector", tag + "_tp")
                        p.op("vector", lambda E, pb=pb, fj=fj, rhs=rhs, subcol=subcol: E.tensor_copy(
                            rhs[:, fj*512 + subcol*128 : fj*512 + (subcol+1)*128], ptile[pb][:]),
                            inc=tag + "_ev")

                for ib in range(11):
                    p.wait("vector", "gu_umm", p.cnt.get("gu_umm", 0))
                    p.wait("vector", "gu_gmm", p.cnt.get("gu_gmm", 0))
                    for ot in range(4):
                        decode_tile_to_rhs(ib*512 + ot*128, rhs_g, ot, "dd")
                    for ot in range(4):
                        decode_tile_to_rhs(5632 + ib*512 + ot*128, rhs_u, ot, "dd")
                    p.wait("tensor", "gd_ev")
                    for ti in range(TPC):
                        if p.cnt.get("gu_sil", 0) > 0:
                            p.wait("tensor", "gu_sil", p.cnt["gu_sil"])
                        for j in range(16):
                            p.op("tensor", lambda E, j=j, ti=ti: E.matmul(
                                pg[:], xT[:, j*TOK + ti*128 : j*TOK + (ti+1)*128],
                                rhs_g[:, j*512:(j+1)*512], start=(j == 0), stop=(j == 15)),
                                inc="gu_gmm")
                        p.wait("scalar", "gu_gmm")
                        if p.cnt.get("gu_mev", 0) > 0:
                            p.wait("scalar", "gu_mev", p.cnt["gu_mev"])
                        p.op("scalar", lambda E: E.activation(sgm[:], pg[:], AF.Silu),
                             inc="gu_sil")
                        if p.cnt.get("gu_mev", 0) > 0:
                            p.wait("tensor", "gu_mev", p.cnt["gu_mev"])
                        for j in range(16):
                            p.op("tensor", lambda E, j=j, ti=ti: E.matmul(
                                pu[:], xT[:, j*TOK + ti*128 : j*TOK + (ti+1)*128],
                                rhs_u[:, j*512:(j+1)*512], start=(j == 0), stop=(j == 15)),
                                inc="gu_umm")
                        p.wait("vector", "gu_umm")
                        p.wait("vector", "gu_sil")
                        if p.cnt.get("dd_tp", 0) > 0:
                            p.wait("vector", "dd_tp", p.cnt["dd_tp"])
                        p.op("vector", lambda E: E.tensor_mul(mch[:], pu[:], sgm[:]),
                             inc="gu_mev")
                        p.wait("tensor", "gu_mev")
                        for ic in range(4):
                            tb = ic % 2
                            if p.cnt.get("dd_ev", 0) >= 2:
                                p.wait("tensor", "dd_ev", p.cnt["dd_ev"] - 1)
                            p.op("tensor", lambda E, tb=tb, ic=ic: E.transpose(
                                ptile[tb][:], mch[:, ic*128:(ic+1)*128], ident[:]),
                                inc="dd_tp")
                            p.wait("vector", "dd_tp")
                            if p.cnt.get("gu_md", 0) > 0:
                                p.wait("vector", "gu_md", p.cnt["gu_md"])
                            p.op("vector", lambda E, tb=tb, ic=ic: E.tensor_copy(
                                mTb[:, ic*128:(ic+1)*128], ptile[tb][:]), inc="dd_ev")
                        p.wait("gpsimd", "dd_ev")
                        p.op("gpsimd", lambda E, ib=ib, ti=ti: E.dma_start(
                            mTs[ib*512:(ib+1)*512, ti*128:(ti+1)*128].rearrange(
                                "(a p) t -> p a t", p=128),
                            mTb[:].rearrange("p (a t) -> p a t", a=4)), inc="gu_md", k=16)
                for e_ in ENGINES:
                    p.wait(e_, "gu_md")

        def _phase3():
            with ExitStack() as sS, ExitStack() as sP:
                wsbf = sb(sS, nc, "wsbf", [128, 11 * 512], BF16)
                mTp = sb(sS, nc, "mTp", [128, 11 * 128], BF16)
                psh = [ps(sP, nc, f"psh{i}", [128, 512], F32) for i in range(8)]
                tag = "fd"
                for oc in range(4):
                    for icg in range(4):
                        p.wait("sync", tag + "_mm")
                        p.op("sync", lambda E, oc=oc, icg=icg: E.dma_start(
                            wsbf[:].rearrange("p (a o) -> p a o", a=11),
                            wdnT[icg*1408:(icg+1)*1408, oc*512:(oc+1)*512].rearrange(
                                "(a p) o -> p a o", p=128)), inc=tag + "_wd", k=16)
                        for ti in range(TPC):
                            p.op("sync", lambda E, icg=icg, ti=ti: E.dma_start(
                                mTp[:].rearrange("p (a t) -> p a t", a=11),
                                mTs[icg*1408:(icg+1)*1408, ti*128:(ti+1)*128].rearrange(
                                    "(a p) t -> p a t", p=128)), inc=tag + "_md", k=16)
                            p.wait("tensor", tag + "_md")
                            p.wait("tensor", tag + "_wd")
                            for ic in range(11):
                                p.op("tensor", lambda E, ti=ti, ic=ic, icg=icg: E.matmul(
                                    psh[ti][:], mTp[:, ic*128:(ic+1)*128],
                                    wsbf[:, ic*512:(ic+1)*512],
                                    start=(icg == 0 and ic == 0),
                                    stop=(icg == 3 and ic == 10),
                                    skip_group_check=True), inc=tag + "_mm")
                            p.wait("sync", tag + "_mm")
                    p.wait("vector", tag + "_mm")
                    for ti in range(TPC):
                        p.op("vector", lambda E, ti=ti, oc=oc: E.tensor_copy(
                            yacc[:, ti*D + oc*512 : ti*D + (oc+1)*512], psh[ti][:]),
                            inc=tag + "_ev")
                    p.wait("tensor", tag + "_ev")
                for e_ in ENGINES:
                    p.wait(e_, "fd_ev")


        _phase1()
        _phase2()
        _phase3()
        xb = [sb(stack, nc, f"lxb{i}", [128, D], F32) for i in range(2)]
        sq = sb(stack, nc, "lsq", [128, D], F32)
        ob = [sb(stack, nc, f"lob{i}", [128, D], F32) for i in range(2)]
        mt = sb(stack, nc, "lmt", [128, 4], F32)
        vt_ = sb(stack, nc, "lvt", [128, 4], F32)
        rs = sb(stack, nc, "lrs", [128, 4], F32)
        eps5 = sb(stack, nc, "leps5", [128, 1], F32)
        eps6 = sb(stack, nc, "leps6", [128, 1], F32)
        p.op("vector", lambda E: E.memset(eps5[:], 1e-5))
        p.op("vector", lambda E: E.memset(eps6[:], 1e-6), inc="ln_v")
        for ti in range(TPC):
            b = ti % 2
            if p.cnt.get("ln_dma", 0) >= 32:
                p.wait("sync", "ln_v", p.cnt.get("ln_v", 0) - 1)
            p.op("sync", lambda E, b=b, ti=ti: E.dma_start(xb[b][:], x5[ti*128:(ti+1)*128, :]),
                 inc="ln_dma", k=16)
            p.wait("scalar", "ln_v", p.cnt.get("ln_v", 0))
            p.op("scalar", lambda E, ti=ti: E.activation(sq[:], yacc[:, ti*D:(ti+1)*D],
                                                         AF.Square, accum_out=rs[:, 0:1]))
            p.op("scalar", lambda E, ti=ti: E.activation(rs[:, 3:4], yacc[:, ti*D:ti*D+1], AF.Copy))
            p.op("scalar", lambda E: E.activation(rs[:, 1:2], rs[:, 0:1], AF.Ln,
                                                  scale=1.0/D, bias=eps6[:]))
            p.op("scalar", lambda E, ti=ti: E.activation(rs[:, 3:4], yacc[:, ti*D:ti*D+1], AF.Copy))
            p.op("scalar", lambda E: E.activation(rs[:, 2:3], rs[:, 1:2], AF.Exp, scale=-0.5))
            p.op("scalar", lambda E, ti=ti: E.activation(rs[:, 3:4], yacc[:, ti*D:ti*D+1], AF.Copy),
                 inc="ln_s")
            p.wait("vector", "ln_s")
            p.wait("vector", "ln_dma")
            if p.cnt.get("ln_out", 0) >= 32:
                p.wait("vector", "ln_out", p.cnt["ln_out"] - 16)
            p.op("vector", lambda E, b=b, ti=ti: E.scalar_tensor_tensor(
                ob[b][:], yacc[:, ti*D:(ti+1)*D], rs[:, 2:3], xb[b][:],
                ALU.mult, ALU.add))
            emit_ln_tile(p, ob[b][:], ob[b][:], sq, mt, vt_, D, eps5, "ln")
            p.op("vector", lambda E: E.memset(mt[:, 3:4], 0.0), inc="ln_v")
            p.wait("gpsimd", "ln_v")
            p.op("gpsimd", lambda E, b=b, ti=ti: E.dma_start(x6[ti*128:(ti+1)*128, :], ob[b][:]),
                 inc="ln_out", k=16)
        p.wait("gpsimd", "ln_out")
        if fc is None:
            p.finalize()
    return nc


def build_l1a_qkv(fc=None):
    nc = bass.Bass("TRN2", target_bir_lowering=False) if fc is None else fc.nc
    x = _dr(fc, nc, "x", [TOK, D], F32, "ExternalInput")
    wT = _dr(fc, nc, "wT", [D, 3 * D], F32, "ExternalInput")
    identd = _dr(fc, nc, "ident", [128, 128], BF16, "ExternalInput")
    qf = _dr(fc, nc, "qf", [D, TOK], BF16, "ExternalOutput")
    kf = _dr(fc, nc, "kf", [D, TOK], BF16, "ExternalOutput")
    vt = _dr(fc, nc, "vt", [TOK, D], BF16, "ExternalOutput")

    with ExitStack() as stack:
        p = Prog(nc, stack) if fc is None else fc.p
        xT = sb(stack, nc, "xT", [128, 16 * TOK], BF16)
        ident = sb(stack, nc, "id", [128, 128], BF16)
        ones_b = sb(stack, nc, "ones", [1, TOK], BF16)
        qe = sb(stack, nc, "qe", [128, TOK], F32)
        qsw = sb(stack, nc, "qsw", [128, TOK], F32)
        qbf = sb(stack, nc, "qbf", [128, TOK], BF16)
        _BFV[id(qsw)] = qbf
        vout = sb(stack, nc, "vout", [128, 256], BF16)
        ptile = [ps(stack, nc, f"pt{i}", [128, 128], BF16) for i in range(2)]
        pp = [ps(stack, nc, f"pp{i}", [128, 512], F32) for i in range(2)]

        with ExitStack() as stackA:
            xbuf = [sb(stackA, nc, f"xb{i}", [128, D], F32) for i in range(2)]
            xbf = [sb(stackA, nc, f"xf{i}", [128, D], BF16) for i in range(2)]
            p.op("sync", lambda E: E.dma_start(ident[:], identd[:, :]), inc="c_dma", k=16)
            p.wait("vector", "c_dma")
            p.op("vector", lambda E: E.memset(ones_b[:], 1.0), inc="c_v")
            p.wait("tensor", "c_v")
            p.wait("sync", "c_v")
            emit_load_xT(p, x, xT, xbuf, xbf, ident, ptile)
            p.wait("tensor", "xt_evac")
            p.wait("sync", "xt_evac")
            p.wait("gpsimd", "xt_evac")
            p.wait("scalar", "xt_evac")
            p.wait("vector", "xt_evac")

        wld = [sb(stack, nc, f"wld{i}", [128, D], F32) for i in range(2)]
        wbf = [sb(stack, nc, f"wbf{i}", [128, D], BF16) for i in range(2)]
        wldv = [sb(stack, nc, f"wldv{i}", [128, 16 * 256], F32) for i in range(2)]
        wbfv = [sb(stack, nc, f"wbfv{i}", [128, 16 * 256], BF16) for i in range(2)]

        emit_fm_proj_rope(p, nc, wT, NH, 0, xT, None, wld, wbf, None, ones_b,
                          None, None, qe, qsw, qf, pp, "fm", rope=False,
                          scale_imm=HD ** -0.5, add_bias=False)
        p.wait("tensor", "fm_evac")
        p.wait("vector", "fm_out")
        p.wait("sync", "fm_mm")
        emit_fm_proj_rope(p, nc, wT, NH, D, xT, None, wld, wbf, None, ones_b,
                          None, None, qe, qsw, kf, pp, "fm", rope=False,
                          add_bias=False)
        p.wait("tensor", "fm_evac")
        emit_tm_proj(p, wT, 2 * D, D, xT, wldv, wbfv, None, ones_b, vt, vout,
                     pp, "fm", ocw=256, add_bias=False)
        p.wait("gpsimd", "fm_out")
        p.wait("gpsimd", "fm_out")
        p.wait("gpsimd", "fm_out")
        if fc is None:
            p.finalize()
    return nc


def build_proj_ln(fc=None):
    nc = bass.Bass("TRN2", target_bir_lowering=False) if fc is None else fc.nc
    x = _dr(fc, nc, "x", [TOK, D], F32, "ExternalInput")
    ctx = _dr(fc, nc, "ctx", [TOK, D], BF16, "ExternalInput")
    wT = _dr(fc, nc, "wT", [D, D], F32, "ExternalInput")
    brow = _dr(fc, nc, "brow", [1, D], F32, "ExternalInput")
    identd = _dr(fc, nc, "ident", [128, 128], BF16, "ExternalInput")
    y = _dr(fc, nc, "y", [TOK, D], F32, "ExternalOutput")

    with ExitStack() as stack:
        p = Prog(nc, stack) if fc is None else fc.p
        xacc = sb(stack, nc, "xacc", [128, TPC * D], F32)
        ident = sb(stack, nc, "id", [128, 128], BF16)
        brow_b = sb(stack, nc, "browb", [1, D], BF16)
        ones_b = sb(stack, nc, "ones", [1, 128], BF16)
        brow_f = sb(stack, nc, "browf", [1, D], F32)

        p.op("sync", lambda E: E.dma_start(ident[:], identd[:, :]), inc="c_dma", k=16)
        p.op("sync", lambda E: E.dma_start(brow_f[:], brow[:, :]), inc="c_dma", k=16)
        p.op("sync", lambda E: E.dma_start(
            xacc[:].rearrange("p (a n) -> p a n", a=TPC),
            x[:, :].rearrange("(a p) n -> p a n", p=128)), inc="c_dma", k=16)
        p.wait("vector", "c_dma")
        p.op("vector", lambda E: E.tensor_copy(brow_b[:], brow_f[:]), inc="c_v")
        p.op("vector", lambda E: E.memset(ones_b[:], 1.0), inc="c_v")
        p.wait("tensor", "c_v")
        p.wait("sync", "c_v")

        with ExitStack() as sA:
            ctxT = sb(sA, nc, "ctxT", [128, 16 * TOK], BF16)
            cbf = [sb(sA, nc, f"cbf{i}", [128, D], BF16) for i in range(2)]
            wld = [sb(sA, nc, f"wld{i}", [128, 16 * 256], F32) for i in range(2)]
            wbf = [sb(sA, nc, f"wbf{i}", [128, 16 * 256], BF16) for i in range(2)]
            with ExitStack() as sP:
                ptile = [ps(sP, nc, f"pt{i}", [128, 128], BF16) for i in range(2)]
                emit_loadT_bf16(p, ctx, ctxT, cbf, ident, ptile, tag="xt")
                p.wait("tensor", "xt_evac")
                p.wait("sync", "xt_evac")
            with ExitStack() as sP:
                pp = [ps(sP, nc, f"pp{i}", [128, 512], F32) for i in range(2)]
                nf = 16; ocw = 256; noc = D // ocw
                for oc in range(noc):
                    b = oc % 2
                    if p.cnt.get("pj_mm", 0) >= 2 * TPC * (nf + 1):
                        p.wait("sync", "pj_mm", p.cnt["pj_mm"] - TPC * (nf + 1))
                    p.op("sync", lambda E, b=b, oc=oc: E.dma_start(
                        wld[b][:].rearrange("p (a o) -> p a o", a=16),
                        wT[:, oc*ocw:(oc+1)*ocw].rearrange("(a p) o -> p a o", p=128)),
                        inc="pj_wdma", k=16)
                    p.wait("vector", "pj_wdma")
                    p.op("vector", lambda E, b=b: E.tensor_copy(wbf[b][:], wld[b][:]),
                         inc="pj_wcast")
                    p.wait("tensor", "pj_wcast")
                    for ti in range(TPC):
                        pb = ti % 2
                        if p.cnt.get("pj_evac", 0) >= 2:
                            p.wait("tensor", "pj_evac", p.cnt["pj_evac"] - 1)
                        for j in range(nf):
                            p.op("tensor", lambda E, pb=pb, b=b, j=j, ti=ti: E.matmul(
                                pp[pb][:, 0:ocw], ctxT[:, j*TOK + ti*128 : j*TOK + (ti+1)*128],
                                wbf[b][:, j*ocw:(j+1)*ocw],
                                start=(j == 0), stop=False), inc="pj_mm")
                        p.op("tensor", lambda E, pb=pb, oc=oc: E.matmul(
                            pp[pb][:, 0:ocw], ones_b[0:1, 0:128],
                            brow_b[0:1, oc*ocw:(oc+1)*ocw],
                            start=False, stop=True), inc="pj_mm")
                        p.wait("vector", "pj_mm")
                        p.op("vector", lambda E, pb=pb, ti=ti, oc=oc: E.tensor_add(
                            xacc[:, ti*D + oc*ocw : ti*D + (oc+1)*ocw],
                            xacc[:, ti*D + oc*ocw : ti*D + (oc+1)*ocw],
                            pp[pb][:, 0:ocw]), inc="pj_evac")
            p.wait("scalar", "pj_evac")
            p.wait("vector", "pj_evac")
            p.wait("sync", "pj_evac")
            p.wait("gpsimd", "pj_evac")
            p.wait("tensor", "pj_evac")

        sq = sb(stack, nc, "lsq", [128, D], F32)
        ob = [sb(stack, nc, f"lob{i}", [128, D], F32) for i in range(2)]
        mt = sb(stack, nc, "lmt", [128, 4], F32)
        vt_ = sb(stack, nc, "lvt", [128, 4], F32)
        eps_t = sb(stack, nc, "leps", [128, 1], F32)
        p.op("vector", lambda E: E.memset(eps_t[:], 1e-5), inc="ln_v")
        for ti in range(TPC):
            b = ti % 2
            if p.cnt.get("ln_out", 0) >= 32:
                p.wait("vector", "ln_out", p.cnt["ln_out"] - 16)
            emit_ln_tile(p, xacc[:, ti*D:(ti+1)*D], ob[b][:], sq, mt, vt_, D, eps_t, "ln")
            p.op("vector", lambda E: E.memset(mt[:, 3:4], 0.0), inc="ln_v")
            p.wait("gpsimd", "ln_v")
            p.op("gpsimd", lambda E, b=b, ti=ti: E.dma_start(y[ti*128:(ti+1)*128, :], ob[b][:]),
                 inc="ln_out", k=16)
        p.wait("gpsimd", "ln_out")
        if fc is None:
            p.finalize()
    return nc


def attn_ranks(c):
    return [8 * s + c for s in range(TPC)]

def attn_rows(c):
    idx = []
    for r in attn_ranks(c):
        idx.extend(range(r * 128, (r + 1) * 128))
    return np.asarray(idx)


def _prep_attn_inputs(qfm_bf, kfm_bf, vtm_bf, x_resid, maskval):
    vpre = np.ascontiguousarray(
        vtm_bf.reshape(NT, 128, NH, HD).transpose(2, 1, 0, 3).reshape(D, S))
    ident = np.eye(128, dtype=np.float32).astype(BF)
    maps = []
    for c in range(N_CORES):
        rows = attn_rows(c)
        qf8 = np.ascontiguousarray(qfm_bf[:, rows])
        mask = np.zeros((TPC, 128, 1024), np.float32)
        for s in range(TPC):
            rank = 8 * s + c
            kk0 = (s * 8) * 128
            kk = kk0 + np.arange(1024)[None, :]
            qpos = rank * 128 + np.arange(128)[:, None]
            mask[s] = np.where(kk > qpos, maskval, 0.0)
        maps.append({
            "qf8": qf8, "kT": kfm_bf, "vpre": vpre,
            "x": np.ascontiguousarray(x_resid[rows]),
            "mask": mask.reshape(TOK, 1024), "ident": ident,
        })
    return maps


def _bar(p):
    for e in ENGINES:
        s = p.sem("bar_" + e)
        p.ops[e].append(lambda E, s=s: E.sem_inc(s, 1))
        p.cnt["bar_" + e] += 1
    for e in ENGINES:
        for f in ENGINES:
            if f != e:
                p.wait(e, "bar_" + f)


def phase_res_ln(p, nc, x_dram, ctx_dram, out_dram):
    with ExitStack() as stack:
        xb = [sb(stack, nc, f"rlxb{i}", [128, D], F32) for i in range(2)]
        cb = [sb(stack, nc, f"rlcb{i}", [128, D], BF16) for i in range(2)]
        sq = sb(stack, nc, "rlsq", [128, D], F32)
        ob = [sb(stack, nc, f"rlob{i}", [128, D], F32) for i in range(2)]
        mt = sb(stack, nc, "rlmt", [128, 4], F32)
        vt_ = sb(stack, nc, "rlvt", [128, 4], F32)
        eps_t = sb(stack, nc, "rleps", [128, 1], F32)
        p.op("vector", lambda E: E.memset(eps_t[:], 1e-5), inc="rl_v")
        for ti in range(TPC):
            b = ti % 2
            if p.cnt.get("rl_dma", 0) >= 64:
                p.wait("sync", "rl_v", p.cnt.get("rl_v", 0) - 1)
            p.op("sync", lambda E, b=b, ti=ti: E.dma_start(
                xb[b][:], x_dram[ti*128:(ti+1)*128, :]), inc="rl_dma", k=16)
            p.op("sync", lambda E, b=b, ti=ti: E.dma_start(
                cb[b][:], ctx_dram[ti*128:(ti+1)*128, :]), inc="rl_dma", k=16)
            p.wait("vector", "rl_dma")
            if p.cnt.get("rl_out", 0) >= 32:
                p.wait("vector", "rl_out", p.cnt["rl_out"] - 16)
            p.op("vector", lambda E, b=b: E.tensor_add(ob[b][:], xb[b][:], cb[b][:]))
            emit_ln_tile(p, ob[b][:], ob[b][:], sq, mt, vt_, D, eps_t, "rl")
            p.op("vector", lambda E: E.memset(mt[:, 3:4], 0.0), inc="rl_v")
            p.wait("gpsimd", "rl_v")
            p.op("gpsimd", lambda E, b=b, ti=ti: E.dma_start(
                out_dram[ti*128:(ti+1)*128, :], ob[b][:]), inc="rl_out", k=16)
        p.wait("gpsimd", "rl_out")


def build_fused():
    nc = bass.Bass("TRN2", target_bir_lowering=False)
    EI = lambda n, s, d: nc.dram_tensor(n, s, d, kind="ExternalInput")
    IT = lambda n, s, d: nc.dram_tensor(n, s, d)
    ident = EI("ident", [128, 128], BF16)
    x0 = EI("x0", [TOK, D], F32)
    wT0 = EI("wT0", [D, 3 * D], F32)
    brow0 = EI("brow0", [1, 3 * D], F32)
    t0 = {k: EI("t0" + k, [HD, TOK], F32) for k in ("cq", "sq", "ck", "sk")}
    mask0 = EI("mask0", [TOK, 1024], F32)
    wq1 = EI("wq1", [D, D], F32); wk1 = EI("wk1", [D, D], F32); wv1 = EI("wv1", [D, D], F32)
    brow1 = EI("brow1", [1, 3 * D], F32)
    gam1 = EI("gam1", [128, D], F32)
    t1 = {k: EI("t1" + k, [HD, TOK], F32) for k in ("cq", "sq", "ck", "sk")}
    mask1 = EI("mask1", [TOK, 1024], F32)
    wo = EI("wo", [D, D], F32); wr = EI("wr", [D, 8], F32); weg = EI("weg", [D, 1], F32)
    gam2 = EI("gam2", [128, D], F32)
    wg = EI("wg", [8, D, 1408], F32); wu = EI("wu", [8, D, 1408], F32)
    wd = EI("wd", [8, 1408, D], F32)
    wsg = EI("wsg", [D, 5632], F32); wsu = EI("wsu", [D, 5632], F32)
    wsd = EI("wsd", [5632, D], F32)
    wT1 = EI("wT1", [D, 3 * D], F32)
    wTp = EI("wTp", [D, D], F32); browp = EI("browp", [1, D], F32)
    gup = EI("gup", [11264, 256], I32); gus = EI("gus", [11264, 16], F16)
    dnp = EI("dnp", [D, 704], I32); dns = EI("dns", [D, 44], F16)
    x6 = nc.dram_tensor("x6", [TOK, D], F32, kind="ExternalOutput")
    qf_d = IT("qf_d", [D, TOK], BF16)
    kf_d = IT("kf_d", [D, TOK], BF16)
    vt_d = IT("vt_d", [TOK, D], BF16)
    agk = IT("agk", [8 * D, TOK], BF16)
    agv = IT("agv", [S, D], BF16)
    ctx_d = IT("ctx_d", [TOK, D], BF16)
    x1_d = IT("x1_d", [TOK, D], F32)
    x2_d = IT("x2_d", [TOK, D], F32)
    x4_d = IT("x4_d", [TOK, D], F32)
    x5_d = IT("x5_d", [TOK, D], F32)
    h2T_d = IT("h2T_d", [D, TOK], BF16)
    comb_d = IT("comb_d", [TOK, 8], F32)
    sg_d = IT("sg_d", [TOK, 1], F32)
    mTs = IT("mTs_d", [5632, TOK], BF16)
    wdnT = IT("wdnT_d", [5632, D], BF16)

    with ExitStack() as stack:
        p = Prog(nc, stack)
        fc = FC(nc, p, stack)

        def ag_kv():
            p.wait("gpsimd", "fm_out")
            p.wait("gpsimd", "fm_out")
            p.op("gpsimd", lambda E: E.collective_compute(
                "AllGather", ALU.bypass, replica_groups=[list(range(8))],
                ins=[kf_d.ap()], outs=[agk.ap()]), inc="cc", k=1)
            p.op("gpsimd", lambda E: E.collective_compute(
                "AllGather", ALU.bypass, replica_groups=[list(range(8))],
                ins=[vt_d.ap()], outs=[agv.ap()]), inc="cc", k=1)
            p.wait("gpsimd", "cc")
            _bar(p)

        fc.io = {"x": x0, "wT": wT0, "brow": brow0, "ctq": t0["cq"], "stq": t0["sq"],
                 "ctk": t0["ck"], "stk": t0["sk"], "ident": ident,
                 "qf": qf_d, "kf": kf_d, "vt": vt_d}
        build_l0a_qkv(fc=fc)
        _bar(p)
        ag_kv()
        fc.io = {"qf8": qf_d, "agk": agk, "agv": agv, "mask": mask0, "ident": ident,
                 "y": ctx_d}
        build_attn(out_ln=False, fc=fc)
        _bar(p)
        phase_res_ln(p, nc, x0, ctx_d, x1_d)
        _bar(p)
        fc.io = {"x": x1_d, "wq": wq1, "wk": wk1, "wv": wv1, "brow": brow1,
                 "gam": gam1, "ctq": t1["cq"], "stq": t1["sq"], "ctk": t1["ck"],
                 "stk": t1["sk"], "ident": ident, "qf": qf_d, "kf": kf_d, "vt": vt_d}
        build_l0f_qkv(fc=fc)
        _bar(p)
        ag_kv()
        fc.io = {"qf8": qf_d, "agk": agk, "agv": agv, "mask": mask1, "ident": ident,
                 "y": ctx_d}
        build_attn(out_ln=False, fc=fc)
        _bar(p)
        fc.io = {"x1": x1_d, "ctx": ctx_d, "wo": wo, "wr": wr, "weg": weg,
                 "gam": gam2, "ident": ident, "x2": x2_d, "h2T": h2T_d,
                 "comb": comb_d, "sg": sg_d}
        build_l0f_moe_a(fc=fc)
        _bar(p)
        fc.io = {"x1": x1_d, "x2": x2_d, "h2T": h2T_d, "comb": comb_d, "sg": sg_d,
                 "wg": wg, "wu": wu, "wd": wd, "wsg": wsg, "wsu": wsu, "wsd": wsd,
                 "ident": ident, "x4": x4_d, "mTs": mTs}
        build_l0f_moe_b(fc=fc)
        _bar(p)
        fc.io = {"x": x4_d, "wT": wT1, "ident": ident, "qf": qf_d, "kf": kf_d,
                 "vt": vt_d}
        build_l1a_qkv(fc=fc)
        _bar(p)
        ag_kv()
        fc.io = {"qf8": qf_d, "agk": agk, "agv": agv, "ident": ident, "y": ctx_d}
        build_attn(out_ln=False, full=True, fc=fc)
        _bar(p)
        fc.io = {"x": x4_d, "ctx": ctx_d, "wT": wTp, "brow": browp, "ident": ident,
                 "y": x5_d}
        build_proj_ln(fc=fc)
        _bar(p)
        fc.io = {"x5": x5_d, "gup": gup, "gus": gus, "dnp": dnp, "dns": dns,
                 "ident": ident, "x6": x6, "mTs": mTs, "wdnT": wdnT}
        build_l1f(fc=fc)
        p.finalize()
    return nc


def _prep_fused_inputs(inputs):
    ident = np.eye(128, dtype=np.float32).astype(BF)
    x_full = np.ascontiguousarray(np.asarray(inputs["L0_attn_x"], np.float32).reshape(S, D))
    W = np.asarray(inputs["L0_attn_Wqkv_w"], np.float32)
    bvec = np.asarray(inputs["L0_attn_Wqkv_b"], np.float32)
    perm = l0a_perm()
    colidx = np.empty(3 * D, np.int64)
    for sec in range(2):
        for h in range(NH):
            base = sec * D + h * HD
            colidx[base:base + HD] = base + perm
    colidx[2 * D:] = np.arange(2 * D, 3 * D)
    wT0 = np.ascontiguousarray(W.T[:, colidx])
    brow0 = np.ascontiguousarray(bvec[colidx][None, :])
    brow1 = np.concatenate([np.asarray(inputs["L0_ffn_bq"], np.float32),
                            np.asarray(inputs["L0_ffn_bk"], np.float32),
                            np.asarray(inputs["L0_ffn_bv"], np.float32)])[None, :]
    brow1 = np.ascontiguousarray(brow1)
    gam1 = np.ascontiguousarray(np.broadcast_to(
        np.asarray(inputs["L0_ffn_w_ln1"], np.float32)[None, :], (128, D)))
    gam2 = np.ascontiguousarray(np.broadcast_to(
        np.asarray(inputs["L0_ffn_w_ln2"], np.float32)[None, :], (128, D)))
    W1 = np.asarray(inputs["L1_attn_Wqkv"], np.float32)
    idx = np.empty(3 * D, np.int64)
    for h in range(NH):
        for cset in range(3):
            idx[cset*D + h*128 : cset*D + (h+1)*128] = np.arange(
                h*384 + cset*128, h*384 + (cset+1)*128)
    wT1 = np.ascontiguousarray(W1[idx].T)
    wTp = np.ascontiguousarray(np.asarray(inputs["L1_attn_Wproj"], np.float32).T)
    browp = np.ascontiguousarray(np.asarray(inputs["L1_attn_bproj"], np.float32)[None, :])
    gus = np.ascontiguousarray(np.asarray(inputs["L1_ffn_gate_up_scales"]).T)
    dns = np.ascontiguousarray(np.asarray(inputs["L1_ffn_down_scales"]).T)
    gup = np.ascontiguousarray(np.asarray(inputs["L1_ffn_gate_up_packed"], np.int32))
    dnp = np.ascontiguousarray(np.asarray(inputs["L1_ffn_down_packed"], np.int32))
    qscale = HD ** -0.5
    NEGMIN = float(np.finfo(np.float32).min)
    maps = []
    for c in range(N_CORES):
        rows = attn_rows(c)
        pos = rows.astype(np.float64)
        t0cq, t0sq = l0a_tables(pos, qscale)
        t0ck, t0sk = l0a_tables(pos, 1.0)
        t1cq, t1sq = l0f_tables(pos, qscale)
        t1ck, t1sk = l0f_tables(pos, 1.0)
        mask_a = np.zeros((TPC, 128, 1024), np.float32)
        mask_f = np.zeros((TPC, 128, 1024), np.float32)
        for s_ in range(TPC):
            rank = 8 * s_ + c
            kk = (s_ * 8) * 128 + np.arange(1024)[None, :]
            qpos = rank * 128 + np.arange(128)[:, None]
            mask_a[s_] = np.where(kk > qpos, -10000.0, 0.0)
            mask_f[s_] = np.where(kk > qpos, NEGMIN, 0.0)
        maps.append({
            "ident": ident,
            "x0": np.ascontiguousarray(x_full[rows]),
            "wT0": wT0, "brow0": brow0,
            "t0cq": t0cq, "t0sq": t0sq, "t0ck": t0ck, "t0sk": t0sk,
            "mask0": mask_a.reshape(TOK, 1024),
            "wq1": np.asarray(inputs["L0_ffn_wq"], np.float32),
            "wk1": np.asarray(inputs["L0_ffn_wk"], np.float32),
            "wv1": np.asarray(inputs["L0_ffn_wv"], np.float32),
            "brow1": brow1, "gam1": gam1,
            "t1cq": t1cq, "t1sq": t1sq, "t1ck": t1ck, "t1sk": t1sk,
            "mask1": mask_f.reshape(TOK, 1024),
            "wo": np.asarray(inputs["L0_ffn_wo"], np.float32),
            "wr": np.asarray(inputs["L0_ffn_w_router"], np.float32),
            "weg": np.asarray(inputs["L0_ffn_w_sh_eg"], np.float32),
            "gam2": gam2,
            "wg": np.asarray(inputs["L0_ffn_w_gate_e"], np.float32),
            "wu": np.asarray(inputs["L0_ffn_w_up_e"], np.float32),
            "wd": np.asarray(inputs["L0_ffn_w_down_e"], np.float32),
            "wsg": np.asarray(inputs["L0_ffn_w_sh_gate"], np.float32),
            "wsu": np.asarray(inputs["L0_ffn_w_sh_up"], np.float32),
            "wsd": np.asarray(inputs["L0_ffn_w_sh_down"], np.float32),
            "wT1": wT1, "wTp": wTp, "browp": browp,
            "gup": gup, "gus": gus, "dnp": dnp, "dns": dns,
        })
    return maps


def kernel(**inputs):
    _ = (inputs.get("L1_attn_key"), inputs.get("L1_attn_value"))
    nc = build_fused()
    maps = _prep_fused_inputs(inputs)
    res = run_bass_kernel_spmd(nc, maps, core_ids=list(range(N_CORES)))
    out = np.empty((S, D), np.float32)
    for c in range(N_CORES):
        out[attn_rows(c)] = np.asarray(res.results[c]["x6"])
    return out.reshape(1, S, D)


def _run(nc, maps):
    return run_bass_kernel_spmd(nc, maps, core_ids=list(range(N_CORES)))


def _gather_rows(res, key, rows_fn, dtype, ncols=D):
    out = np.empty((S, ncols), dtype)
    for c in range(N_CORES):
        out[rows_fn(c)] = np.asarray(res.results[c][key])
    return out


def _gather_cols(res, key, rows_fn, dtype, nrows=D):
    out = np.empty((nrows, S), dtype)
    for c in range(N_CORES):
        out[:, rows_fn(c)] = np.asarray(res.results[c][key])
    return out


def kernel_v0(**inputs):
    ident = np.eye(128, dtype=np.float32).astype(BF)
    x0 = np.ascontiguousarray(np.asarray(inputs["L0_attn_x"], np.float32).reshape(S, D))

    res = _run(build_l0a_qkv(), _prep_l0a_qkv_inputs(inputs))
    qfm = _gather_cols(res, "qf", core_rows, BF)
    kfm = _gather_cols(res, "kf", core_rows, BF)
    vtm = _gather_rows(res, "vt", core_rows, BF)

    res = _run(build_attn(out_ln=True),
               _prep_attn_inputs(qfm, kfm, vtm, x0, -10000.0))
    x1 = _gather_rows(res, "y", attn_rows, np.float32)

    res = _run(build_l0f_qkv(), _prep_l0f_qkv_inputs(inputs, x1))
    qfm = _gather_cols(res, "qf", core_rows, BF)
    kfm = _gather_cols(res, "kf", core_rows, BF)
    vtm = _gather_rows(res, "vt", core_rows, BF)

    res = _run(build_attn(out_ln=False),
               _prep_attn_inputs(qfm, kfm, vtm, x1, float(np.finfo(np.float32).min)))
    ctx = _gather_rows(res, "y", attn_rows, BF)

    gam2 = np.ascontiguousarray(np.broadcast_to(
        np.asarray(inputs["L0_ffn_w_ln2"], np.float32)[None, :], (128, D)))
    nc = build_l0f_moe_a()
    maps = []
    for c in range(N_CORES):
        rows = core_rows(c)
        maps.append({
            "x1": np.ascontiguousarray(x1[rows]),
            "ctx": np.ascontiguousarray(ctx[rows]),
            "wo": np.asarray(inputs["L0_ffn_wo"], np.float32),
            "wr": np.asarray(inputs["L0_ffn_w_router"], np.float32),
            "weg": np.asarray(inputs["L0_ffn_w_sh_eg"], np.float32),
            "gam": gam2, "ident": ident,
        })
    res = _run(nc, maps)
    x2s = [np.asarray(res.results[c]["x2"]) for c in range(N_CORES)]
    h2Ts = [np.asarray(res.results[c]["h2T"]) for c in range(N_CORES)]
    combs = [np.asarray(res.results[c]["comb"]) for c in range(N_CORES)]
    sgs = [np.asarray(res.results[c]["sg"]) for c in range(N_CORES)]

    nc = build_l0f_moe_b()
    maps = []
    for c in range(N_CORES):
        rows = core_rows(c)
        maps.append({
            "x1": np.ascontiguousarray(x1[rows]),
            "x2": x2s[c], "h2T": h2Ts[c], "comb": combs[c], "sg": sgs[c],
            "wg": np.asarray(inputs["L0_ffn_w_gate_e"], np.float32),
            "wu": np.asarray(inputs["L0_ffn_w_up_e"], np.float32),
            "wd": np.asarray(inputs["L0_ffn_w_down_e"], np.float32),
            "wsg": np.asarray(inputs["L0_ffn_w_sh_gate"], np.float32),
            "wsu": np.asarray(inputs["L0_ffn_w_sh_up"], np.float32),
            "wsd": np.asarray(inputs["L0_ffn_w_sh_down"], np.float32),
            "ident": ident,
        })
    res = _run(nc, maps)
    x4 = _gather_rows(res, "x4", core_rows, np.float32)

    W = np.asarray(inputs["L1_attn_Wqkv"], np.float32)
    idx = np.empty(3 * D, np.int64)
    for h in range(NH):
        for cset in range(3):
            idx[cset*D + h*128 : cset*D + (h+1)*128] = np.arange(
                h*384 + cset*128, h*384 + (cset+1)*128)
    wT = np.ascontiguousarray(W[idx].T)
    nc = build_l1a_qkv()
    maps = [{"x": np.ascontiguousarray(x4[core_rows(c)]), "wT": wT, "ident": ident}
            for c in range(N_CORES)]
    res = _run(nc, maps)
    qfm = _gather_cols(res, "qf", core_rows, BF)
    kfm = _gather_cols(res, "kf", core_rows, BF)
    vtm = _gather_rows(res, "vt", core_rows, BF)

    res = _run(build_attn(out_ln=False, full=True),
               _prep_attn_inputs(qfm, kfm, vtm, x4, 0.0))
    ctx = _gather_rows(res, "y", attn_rows, BF)

    nc = build_proj_ln()
    wTp = np.ascontiguousarray(np.asarray(inputs["L1_attn_Wproj"], np.float32).T)
    brow = np.ascontiguousarray(np.asarray(inputs["L1_attn_bproj"], np.float32)[None, :])
    maps = [{"x": np.ascontiguousarray(x4[core_rows(c)]),
             "ctx": np.ascontiguousarray(ctx[core_rows(c)]),
             "wT": wTp, "brow": brow, "ident": ident} for c in range(N_CORES)]
    res = _run(nc, maps)
    x5 = _gather_rows(res, "y", core_rows, np.float32)

    nc = build_l1f()
    gus = np.ascontiguousarray(np.asarray(inputs["L1_ffn_gate_up_scales"]).T)
    dns = np.ascontiguousarray(np.asarray(inputs["L1_ffn_down_scales"]).T)
    gup = np.ascontiguousarray(np.asarray(inputs["L1_ffn_gate_up_packed"], np.int32))
    dnp = np.ascontiguousarray(np.asarray(inputs["L1_ffn_down_packed"], np.int32))
    maps = [{"x5": np.ascontiguousarray(x5[core_rows(c)]),
             "gup": gup, "gus": gus, "dnp": dnp, "dns": dns, "ident": ident}
            for c in range(N_CORES)]
    res = _run(nc, maps)
    x6 = _gather_rows(res, "x6", core_rows, np.float32)
    return x6.reshape(1, S, D)


# version 03
# speedup vs baseline: 1.0079x; 1.0079x over previous
import numpy as np
import ml_dtypes
import concourse.bass as bass
import concourse.mybir as mybir
from concourse.bass_utils import run_bass_kernel_spmd
from contextlib import ExitStack

F32 = mybir.dt.float32
BF16 = mybir.dt.bfloat16
I32 = mybir.dt.int32
F16 = mybir.dt.float16
AF = mybir.ActivationFunctionType
AX = mybir.AxisListType
ALU = mybir.AluOpType
BF = ml_dtypes.bfloat16

N_CORES = 8
S = 8192
D = 2048
NH = 16
HD = 128
NT = S // 128
TPC = NT // N_CORES
TOK = TPC * 128

ENGINES = ("sync", "vector", "scalar", "tensor", "gpsimd")

def core_tiles(c):
    return list(range(c * TPC, (c + 1) * TPC))

def core_rows(c):
    idx = []
    for t in core_tiles(c):
        idx.extend(range(t * 128, (t + 1) * 128))
    return np.asarray(idx)


_SEM_ALIAS = {
    "sh_gmm": "eg_gmm", "sh_sil": "eg_sil", "sh_umm": "eg_umm",
    "sh_wcast": "eg_wcast", "sh_wdma": "eg_wdma", "sh_mev": "eg_mev",
    "sh_mdma": "gu_md",
    "gu_gmm": "eg_gmm", "gu_sil": "eg_sil", "gu_umm": "eg_umm", "gu_mev": "eg_mev",
    "fd_ev": "sd_ev", "fd_md": "sd_md", "fd_mm": "sd_mm", "fd_wd": "sd_wd",
    "pj_evac": "wo_evac", "pj_mm": "wo_mm", "pj_wcast": "wo_wcast",
    "pj_wdma": "wo_wdma",
    "x2_out": "out2", "h2T_out": "out2", "fin": "out2",
    "cb_v": "r_v",
}


class _CntDict(dict):
    def _k(self, k):
        return _SEM_ALIAS.get(k, k)
    def __getitem__(self, k):
        return super().__getitem__(self._k(k))
    def __setitem__(self, k, v):
        super().__setitem__(self._k(k), v)
    def get(self, k, d=None):
        return super().get(self._k(k), d)
    def __contains__(self, k):
        return super().__contains__(self._k(k))


class Prog:
    def __init__(self, nc, stack):
        self.nc = nc
        self.stack = stack
        self.ops = {e: [] for e in ENGINES}
        self._sems = {}
        self.cnt = _CntDict()

    def sem(self, name):
        name = _SEM_ALIAS.get(name, name)
        if name not in self._sems:
            self._sems[name] = self.stack.enter_context(self.nc.semaphore(name))
            self.cnt[name] = 0
        return self._sems[name]

    def op(self, eng, fn, inc=None, k=1):
        inc = _SEM_ALIAS.get(inc, inc)
        if inc is not None:
            s = self.sem(inc)
            def g(E, fn=fn, s=s, k=k):
                fn(E).then_inc(s, k)
            self.ops[eng].append(g)
            self.cnt[inc] += k
        else:
            self.ops[eng].append(fn)

    def wait(self, eng, sem_name, val=None):
        sem_name = _SEM_ALIAS.get(sem_name, sem_name)
        s = self.sem(sem_name)
        v = self.cnt[sem_name] if val is None else val
        self.ops[eng].append(lambda E, s=s, v=v: E.wait_ge(s, v))

    def finalize(self):
        with self.nc.Block() as block:
            for e in ENGINES:
                q = self.ops[e]
                if not q:
                    continue
                def runner(E, q=q):
                    for fn in q:
                        fn(E)
                getattr(block, e)(runner)



class FC:
    def __init__(self, nc, p, stack):
        self.nc = nc
        self.p = p
        self.stack = stack
        self.io = {}

def _dr(fc, nc, name, shape, dt, kind):
    if fc is None:
        return nc.dram_tensor(name, shape, dt, kind=kind)
    return fc.io[name]


_NAMECTR = [0]

def _uniq(name):
    _NAMECTR[0] += 1
    return f"{name}_{_NAMECTR[0]}"

def sb(stack, nc, name, shape, dt):
    return stack.enter_context(nc.sbuf_tensor(_uniq(name), shape, dt))

def ps(stack, nc, name, shape, dt=F32):
    return stack.enter_context(nc.psum_tensor(_uniq(name), shape, dt))


L0A_ROT = 32

def l0a_perm():
    j_of_r = np.empty(HD, np.int64)
    for r in range(HD):
        if r < 16: j = r
        elif r < 64: j = r + 16
        elif r < 80: j = r - 48
        else: j = r
        j_of_r[r] = j
    return j_of_r

def l0a_tables(pos, scale):
    inv = 1.0 / 10000.0 ** (np.arange(0, L0A_ROT, 2, dtype=np.float64) / L0A_ROT)
    C = np.ones((HD, len(pos)), np.float64)
    Sg = np.zeros((HD, len(pos)), np.float64)
    ang = pos[None, :].astype(np.float64) * inv[:, None]
    C[0:16] = np.cos(ang);  Sg[0:16] = -np.sin(ang)
    C[64:80] = np.cos(ang); Sg[64:80] = np.sin(ang)
    return (C * scale).astype(np.float32), (Sg * scale).astype(np.float32)

def l0f_tables(pos, scale):
    inv = 1.0 / 1e6 ** (np.arange(0, HD, 2, dtype=np.float64) / HD)
    ang = pos[None, :].astype(np.float64) * inv[:, None]
    C = np.concatenate([np.cos(ang), np.cos(ang)], 0)
    Sg = np.concatenate([-np.sin(ang), np.sin(ang)], 0)
    return (C * scale).astype(np.float32), (Sg * scale).astype(np.float32)



def emit_load_xT(p, x_dram, xT, xbuf, xbf, ident, ptile, n_tiles=TPC, dcols=D,
                 rms=None):
    T = n_tiles * 128
    nf = dcols // 128
    inv_n = 1.0 / dcols
    for ti in range(n_tiles):
        b = ti % 2
        if p.cnt.get("xt_cast", 0) >= 2:
            p.wait("sync", "xt_cast", p.cnt["xt_cast"] - 1)
        p.op("sync", lambda E, b=b, ti=ti: E.dma_start(xbuf[b][:], x_dram[ti*128:(ti+1)*128, :]),
             inc="xt_dma", k=16)
        if rms is None:
            p.wait("vector", "xt_dma")
            p.op("vector", lambda E, b=b: E.tensor_copy(xbf[b][:], xbuf[b][:]), inc="xt_cast")
        else:
            gam, eps_t, rs = rms
            p.wait("scalar", "xt_dma")
            p.op("scalar", lambda E, b=b: E.activation(xbf[b][:], xbuf[b][:], AF.Square,
                                                       accum_out=rs[:, 0:1]))
            p.op("scalar", lambda E, b=b: E.activation(rs[:, 3:4], xbuf[b][:, 0:1], AF.Copy))
            p.op("scalar", lambda E: E.activation(rs[:, 1:2], rs[:, 0:1], AF.Ln,
                                                  scale=inv_n, bias=eps_t[:]))
            p.op("scalar", lambda E, b=b: E.activation(rs[:, 3:4], xbuf[b][:, 0:1], AF.Copy))
            p.op("scalar", lambda E: E.activation(rs[:, 2:3], rs[:, 1:2], AF.Exp,
                                                  scale=-0.5))
            p.op("scalar", lambda E, b=b: E.activation(rs[:, 3:4], xbuf[b][:, 0:1],
                                                       AF.Copy), inc="xt_rms")
            p.wait("vector", "xt_rms")
            p.op("vector", lambda E, b=b: E.tensor_scalar(xbuf[b][:], xbuf[b][:],
                                                          rs[:, 2:3], None, ALU.mult))
            p.op("vector", lambda E, b=b: E.tensor_mul(xbf[b][:], xbuf[b][:], gam[:]),
                 inc="xt_cast")
        p.wait("tensor", "xt_cast")
        for j in range(nf):
            pb = j % 2
            if p.cnt.get("xt_evac", 0) >= 2:
                p.wait("tensor", "xt_evac", p.cnt["xt_evac"] - 1)
            p.op("tensor", lambda E, pb=pb, b=b, j=j: E.transpose(
                ptile[pb][:], xbf[b][:, j*128:(j+1)*128], ident[:]), inc="xt_tp")
            p.wait("vector", "xt_tp")
            p.op("vector", lambda E, pb=pb, j=j, ti=ti: E.tensor_copy(
                xT[:, j*T + ti*128 : j*T + (ti+1)*128], ptile[pb][:]), inc="xt_evac")


def emit_fm_proj_rope(p, nc, wT_dram, n_heads, head0_col, xT, ident_unused,
                      wld, wbf, biasrow_bf, ones_bf, CT, ST, qe, qsw,
                      out_dram, pp, tag, rope=True, Tn=TOK, out_sb=None,
                      scale_imm=None, bias_base=None, add_bias=True):
    nf = D // 128
    ntc = Tn // 512
    if bias_base is None:
        bias_base = head0_col
    for h in range(n_heads):
        b = h % 2
        col0 = head0_col + h * 128
        bcol0 = bias_base + h * 128
        mm_per_h = nf * ntc + (ntc if add_bias else 0)
        if p.cnt.get(tag + "_mm", 0) >= 2 * mm_per_h:
            p.wait("sync", tag + "_mm", p.cnt[tag + "_mm"] - mm_per_h)
        p.op("sync", lambda E, b=b, col0=col0: E.dma_start(
            wld[b][:].rearrange("p (a o) -> p a o", a=16), wT_dram[:, col0:col0+128].rearrange("(a p) o -> p a o", p=128)),
            inc=tag + "_wdma", k=16)
        p.wait("vector", tag + "_wdma")
        p.op("vector", lambda E, b=b: E.tensor_copy(wbf[b][:], wld[b][:]), inc=tag + "_wcast")
        if rope and p.cnt.get(tag + "_sw", 0) > 0:
            p.wait("vector", tag + "_sw", p.cnt[tag + "_sw"])
        p.wait("tensor", tag + "_wcast")
        for tc in range(ntc):
            pb = tc % 2
            if p.cnt.get(tag + "_evac", 0) >= 2:
                p.wait("tensor", tag + "_evac", p.cnt[tag + "_evac"] - 1)
            for j in range(nf):
                p.op("tensor", lambda E, pb=pb, b=b, j=j, tc=tc: E.matmul(
                    pp[pb][:], wbf[b][:, j*128:(j+1)*128],
                    xT[:, j*Tn + tc*512 : j*Tn + tc*512 + 512],
                    start=(j == 0), stop=(not add_bias) and (j == nf - 1)),
                    inc=tag + "_mm")
            if add_bias:
                p.op("tensor", lambda E, pb=pb, bcol0=bcol0, tc=tc: E.matmul(
                    pp[pb][:], biasrow_bf[0:1, bcol0:bcol0+128],
                    ones_bf[0:1, tc*512:tc*512+512],
                    start=False, stop=True), inc=tag + "_mm")
            p.wait("vector", tag + "_mm")
            p.op("vector", lambda E, pb=pb, tc=tc: E.tensor_copy(
                qe[:, tc*512:tc*512+512], pp[pb][:]), inc=tag + "_evac")
        if rope:
            p.wait("gpsimd", tag + "_evac")
            p.op("gpsimd", lambda E: E.dma_start(qsw[0:64, :], qe[64:128, :]), inc=tag + "_sw", k=16)
            p.op("gpsimd", lambda E: E.dma_start(qsw[64:128, :], qe[0:64, :]), inc=tag + "_sw", k=16)
            p.wait("vector", tag + "_sw")
            p.op("vector", lambda E: E.tensor_mul(qe[:], qe[:], CT[:]))
            p.op("vector", lambda E: E.tensor_mul(qsw[:], qsw[:], ST[:]))
            dst = out_sb if out_sb is not None else qsw
            if p.cnt.get(tag + "_out", 0) > 0:
                p.wait("vector", tag + "_out", p.cnt[tag + "_out"])
            p.op("vector", lambda E: E.tensor_add(_bfview(dst), qe[:], qsw[:]), inc=tag + "_done")
        else:
            p.wait("vector", tag + "_evac")
            dst = out_sb if out_sb is not None else qsw
            if p.cnt.get(tag + "_out", 0) > 0:
                p.wait("vector", tag + "_out", p.cnt[tag + "_out"])
            if scale_imm is not None:
                p.op("vector", lambda E: E.tensor_scalar_mul(_bfview(dst), qe[:], scale_imm),
                     inc=tag + "_done")
            else:
                p.op("vector", lambda E: E.tensor_copy(_bfview(dst), qe[:]), inc=tag + "_done")
        p.wait("gpsimd", tag + "_done")
        p.op("gpsimd", lambda E, h=h: E.dma_start(
            out_dram[h*128:(h+1)*128, :], _bfview(dst)), inc=tag + "_out", k=16)


_BFV = {}
def _bfview(t):
    return _BFV[id(t)][:]


def emit_tm_proj(p, wT_dram, col0, ncols, xT, wld, wbf, biasrow_bf, ones_bf,
                 out_dram, out_sb_bf, pp, tag, Tn=TOK, add_bias=True, ocw=512,
                 bias_base=None):
    nf = D // 128
    noc = ncols // ocw
    ntt = Tn // 128
    mm_per = ntt * (nf + (1 if add_bias else 0))
    if bias_base is None:
        bias_base = col0
    for oc in range(noc):
        b = oc % 2
        if p.cnt.get(tag + "_mm", 0) >= 2 * mm_per:
            p.wait("sync", tag + "_mm", p.cnt[tag + "_mm"] - mm_per)
        p.op("sync", lambda E, b=b, oc=oc: E.dma_start(
            wld[b][:].rearrange("p (a o) -> p a o", a=16), wT_dram[:, col0+oc*ocw:col0+(oc+1)*ocw].rearrange("(a p) o -> p a o", p=128)),
            inc=tag + "_wdma", k=16)
        p.wait("vector", tag + "_wdma")
        p.op("vector", lambda E, b=b: E.tensor_copy(wbf[b][:], wld[b][:]), inc=tag + "_wcast")
        p.wait("tensor", tag + "_wcast")
        for ti in range(ntt):
            pb = ti % 2
            if p.cnt.get(tag + "_evac", 0) >= 2:
                p.wait("tensor", tag + "_evac", p.cnt[tag + "_evac"] - 1)
            for j in range(nf):
                p.op("tensor", lambda E, pb=pb, b=b, j=j, ti=ti: E.matmul(
                    pp[pb][:, 0:ocw], xT[:, j*Tn + ti*128 : j*Tn + (ti+1)*128],
                    wbf[b][:, j*ocw:(j+1)*ocw],
                    start=(j == 0), stop=(not add_bias) and (j == nf - 1)), inc=tag + "_mm")
            if add_bias:
                p.op("tensor", lambda E, pb=pb, oc=oc: E.matmul(
                    pp[pb][:, 0:ocw], ones_bf[0:1, 0:128],
                    biasrow_bf[0:1, bias_base+oc*ocw:bias_base+(oc+1)*ocw],
                    start=False, stop=True), inc=tag + "_mm")
            p.wait("vector", tag + "_mm")
            if p.cnt.get(tag + "_out", 0) > 0:
                p.wait("vector", tag + "_out", p.cnt[tag + "_out"])
            p.op("vector", lambda E, pb=pb: E.tensor_copy(out_sb_bf[:], pp[pb][:, 0:ocw]), inc=tag + "_evac")
            p.wait("gpsimd", tag + "_evac")
            p.op("gpsimd", lambda E, ti=ti, oc=oc: E.dma_start(
                out_dram[ti*128:(ti+1)*128, oc*ocw:(oc+1)*ocw], out_sb_bf[:]),
                inc=tag + "_out", k=16)



def build_l0a_qkv(fc=None):
    nc = bass.Bass("TRN2", target_bir_lowering=False) if fc is None else fc.nc
    x = _dr(fc, nc, "x", [TOK, D], F32, "ExternalInput")
    wT = _dr(fc, nc, "wT", [D, 3 * D], F32, "ExternalInput")
    brow = _dr(fc, nc, "brow", [1, 3 * D], F32, "ExternalInput")
    ctq = _dr(fc, nc, "ctq", [HD, TOK], F32, "ExternalInput")
    stq = _dr(fc, nc, "stq", [HD, TOK], F32, "ExternalInput")
    ctk = _dr(fc, nc, "ctk", [HD, TOK], F32, "ExternalInput")
    stk = _dr(fc, nc, "stk", [HD, TOK], F32, "ExternalInput")
    identd = _dr(fc, nc, "ident", [128, 128], BF16, "ExternalInput")
    qf = _dr(fc, nc, "qf", [D, TOK], BF16, "ExternalOutput")
    kf = _dr(fc, nc, "kf", [D, TOK], BF16, "ExternalOutput")
    vt = _dr(fc, nc, "vt", [TOK, D], BF16, "ExternalOutput")

    with ExitStack() as stack:
        p = Prog(nc, stack) if fc is None else fc.p
        xT = sb(stack, nc, "xT", [128, 16 * TOK], BF16)
        ident = sb(stack, nc, "id", [128, 128], BF16)
        brow_b = sb(stack, nc, "browb", [1, 3 * D], BF16)
        ones_b = sb(stack, nc, "ones", [1, TOK], BF16)
        CTq = sb(stack, nc, "CTq", [128, TOK], F32)
        STq = sb(stack, nc, "STq", [128, TOK], F32)
        CTk = sb(stack, nc, "CTk", [128, TOK], F32)
        STk = sb(stack, nc, "STk", [128, TOK], F32)
        qe = sb(stack, nc, "qe", [128, TOK], F32)
        qsw = sb(stack, nc, "qsw", [128, TOK], F32)
        qbf = sb(stack, nc, "qbf", [128, TOK], BF16)
        _BFV[id(qsw)] = qbf
        vout = sb(stack, nc, "vout", [128, 256], BF16)
        ptile = [ps(stack, nc, f"pt{i}", [128, 128], BF16) for i in range(2)]
        pp = [ps(stack, nc, f"pp{i}", [128, 512], F32) for i in range(2)]

        with ExitStack() as stackA:
            xbuf = [sb(stackA, nc, f"xb{i}", [128, D], F32) for i in range(2)]
            xbf = [sb(stackA, nc, f"xf{i}", [128, D], BF16) for i in range(2)]

            p.op("sync", lambda E: E.dma_start(ident[:], identd[:, :]), inc="c_dma", k=16)
            p.op("sync", lambda E: E.dma_start(CTq[:], ctq[:, :]), inc="c_dma", k=16)
            p.op("sync", lambda E: E.dma_start(STq[:], stq[:, :]), inc="c_dma", k=16)
            p.op("sync", lambda E: E.dma_start(CTk[:], ctk[:, :]), inc="c_dma", k=16)
            p.op("sync", lambda E: E.dma_start(STk[:], stk[:, :]), inc="c_dma", k=16)
            for i in range(3):
                if i > 0:
                    p.wait("sync", "c_v", p.cnt.get("c_v", 0))
                p.op("sync", lambda E, i=i: E.dma_start(
                    xbuf[1][0:1, 0:D], brow[0:1, i*D:(i+1)*D]), inc="c_dma", k=16)
                p.wait("vector", "c_dma")
                p.op("vector", lambda E, i=i: E.tensor_copy(
                    brow_b[0:1, i*D:(i+1)*D], xbuf[1][0:1, 0:D]), inc="c_v")
            p.op("vector", lambda E: E.memset(ones_b[:], 1.0), inc="c_v")
            p.wait("tensor", "c_v")
            p.wait("sync", "c_v")

            emit_load_xT(p, x, xT, xbuf, xbf, ident, ptile)

            p.wait("tensor", "xt_evac")
            p.wait("sync", "xt_evac")
            p.wait("gpsimd", "xt_evac")
            p.wait("scalar", "xt_evac")
            p.wait("vector", "xt_evac")

        wld = [sb(stack, nc, f"wld{i}", [128, D], F32) for i in range(2)]
        wbf = [sb(stack, nc, f"wbf{i}", [128, D], BF16) for i in range(2)]
        wldv = [sb(stack, nc, f"wldv{i}", [128, 16 * 256], F32) for i in range(2)]
        wbfv = [sb(stack, nc, f"wbfv{i}", [128, 16 * 256], BF16) for i in range(2)]

        emit_fm_proj_rope(p, nc, wT, NH, 0, xT, None, wld, wbf, brow_b, ones_b,
                          CTq, STq, qe, qsw, qf, pp, "fm", rope=True)
        p.wait("tensor", "fm_evac")
        p.wait("vector", "fm_sw")
        p.wait("vector", "fm_out")
        p.wait("sync", "fm_mm")
        emit_fm_proj_rope(p, nc, wT, NH, D, xT, None, wld, wbf, brow_b, ones_b,
                          CTk, STk, qe, qsw, kf, pp, "fm", rope=True)
        p.wait("tensor", "fm_evac")
        emit_tm_proj(p, wT, 2 * D, D, xT, wldv, wbfv, brow_b, ones_b, vt, vout,
                     pp, "fm", ocw=256)

        p.wait("gpsimd", "fm_out")
        p.wait("gpsimd", "fm_out")
        p.wait("gpsimd", "fm_out")
        if fc is None:
            p.finalize()
    return nc


def emit_loadT_bf16(p, x_dram, xT, xbf, ident, ptile, n_tiles=TPC, dcols=D, tag="bt"):
    T = n_tiles * 128
    nf = dcols // 128
    for ti in range(n_tiles):
        b = ti % 2
        if p.cnt.get(tag + "_tp", 0) >= 2 * nf:
            p.wait("sync", tag + "_tp", p.cnt[tag + "_tp"] - nf)
        p.op("sync", lambda E, b=b, ti=ti: E.dma_start(xbf[b][:], x_dram[ti*128:(ti+1)*128, :]),
             inc=tag + "_dma", k=16)
        p.wait("tensor", tag + "_dma")
        for j in range(nf):
            pb = j % 2
            if p.cnt.get(tag + "_evac", 0) >= 2:
                p.wait("tensor", tag + "_evac", p.cnt[tag + "_evac"] - 1)
            p.op("tensor", lambda E, pb=pb, b=b, j=j: E.transpose(
                ptile[pb][:], xbf[b][:, j*128:(j+1)*128], ident[:]), inc=tag + "_tp")
            p.wait("vector", tag + "_tp")
            p.op("vector", lambda E, pb=pb, j=j, ti=ti: E.tensor_copy(
                xT[:, j*T + ti*128 : j*T + (ti+1)*128], ptile[pb][:]), inc=tag + "_evac")


def emit_rms_from_sbuf(p, src_ap, dst_bf_ap, gam_ap, eps_t, rs, n_cols, tag):
    inv_n = 1.0 / n_cols
    p.wait("scalar", tag + "_v", p.cnt.get(tag + "_v", 0))
    p.op("scalar", lambda E: E.activation(rs[:, 3:4], src_ap[:, 0:1], AF.Copy))
    p.op("scalar", lambda E: E.activation(dst_bf_ap, src_ap, AF.Square, accum_out=rs[:, 0:1]))
    p.op("scalar", lambda E: E.activation(rs[:, 3:4], src_ap[:, 0:1], AF.Copy))
    p.op("scalar", lambda E: E.activation(rs[:, 1:2], rs[:, 0:1], AF.Ln, scale=inv_n,
                                          bias=eps_t[:]))
    p.op("scalar", lambda E: E.activation(rs[:, 3:4], src_ap[:, 0:1], AF.Copy))
    p.op("scalar", lambda E: E.activation(rs[:, 2:3], rs[:, 1:2], AF.Exp, scale=-0.5))
    p.op("scalar", lambda E: E.activation(rs[:, 3:4], src_ap[:, 0:1], AF.Copy),
         inc=tag + "_s")
    p.wait("vector", tag + "_s")
    p.op("vector", lambda E: E.scalar_tensor_tensor(
        dst_bf_ap, src_ap, rs[:, 2:3], gam_ap, ALU.mult, ALU.mult), inc=tag + "_v")


def build_l0f_moe_a(fc=None):
    nc = bass.Bass("TRN2", target_bir_lowering=False) if fc is None else fc.nc
    x1 = _dr(fc, nc, "x1", [TOK, D], F32, "ExternalInput")
    ctx = _dr(fc, nc, "ctx", [TOK, D], BF16, "ExternalInput")
    wo = _dr(fc, nc, "wo", [D, D], F32, "ExternalInput")
    wr = _dr(fc, nc, "wr", [D, 8], F32, "ExternalInput")
    weg = _dr(fc, nc, "weg", [D, 1], F32, "ExternalInput")
    gam = _dr(fc, nc, "gam", [128, D], F32, "ExternalInput")
    identd = _dr(fc, nc, "ident", [128, 128], BF16, "ExternalInput")
    x2 = _dr(fc, nc, "x2", [TOK, D], F32, "ExternalOutput")
    h2T = _dr(fc, nc, "h2T", [D, TOK], BF16, "ExternalOutput")
    comb = _dr(fc, nc, "comb", [TOK, 8], F32, "ExternalOutput")
    sg = _dr(fc, nc, "sg", [TOK, 1], F32, "ExternalOutput")

    with ExitStack() as stack:
        p = Prog(nc, stack) if fc is None else fc.p
        xacc = sb(stack, nc, "xacc", [128, TPC * D], F32)
        h2Ts = sb(stack, nc, "h2Ts", [128, 16 * TOK], BF16)
        ident = sb(stack, nc, "id", [128, 128], BF16)
        gam_sb = sb(stack, nc, "gam_sb", [128, D], F32)
        eps_t = sb(stack, nc, "eps_t", [128, 1], F32)
        rs = sb(stack, nc, "rs", [128, 4], F32)
        h2tok = sb(stack, nc, "h2tok", [128, D], BF16)

        p.op("sync", lambda E: E.dma_start(ident[:], identd[:, :]), inc="c_dma", k=16)
        p.op("sync", lambda E: E.dma_start(gam_sb[:], gam[:, :]), inc="c_dma", k=16)
        p.op("sync", lambda E: E.dma_start(
            xacc[:].rearrange("p (a n) -> p a n", a=TPC),
            x1[:, :].rearrange("(a p) n -> p a n", p=128)), inc="c_dma", k=16)
        p.wait("vector", "c_dma")
        p.op("vector", lambda E: E.memset(eps_t[:], 1e-6), inc="c_v")
        p.wait("tensor", "c_v")
        p.wait("scalar", "c_v")

        with ExitStack() as sA:
            ctxT = sb(sA, nc, "ctxT", [128, 16 * TOK], BF16)
            cbf = [sb(sA, nc, f"cbf{i}", [128, D], BF16) for i in range(2)]
            wld = [sb(sA, nc, f"wld{i}", [128, 16 * 256], F32) for i in range(2)]
            wbf = [sb(sA, nc, f"wbf{i}", [128, 16 * 256], BF16) for i in range(2)]
            with ExitStack() as sP:
                ptile = [ps(sP, nc, f"pt{i}", [128, 128], BF16) for i in range(2)]
                emit_loadT_bf16(p, ctx, ctxT, cbf, ident, ptile, tag="xt")
                p.wait("tensor", "xt_evac")
                p.wait("sync", "xt_evac")
            with ExitStack() as sP:
                pp = [ps(sP, nc, f"pp{i}", [128, 512], F32) for i in range(2)]
                nf = 16; ocw = 256; noc = D // ocw
                for oc in range(noc):
                    b = oc % 2
                    if p.cnt.get("wo_mm", 0) >= 2 * TPC * nf:
                        p.wait("sync", "wo_mm", p.cnt["wo_mm"] - TPC * nf)
                    p.op("sync", lambda E, b=b, oc=oc: E.dma_start(
                        wld[b][:].rearrange("p (a o) -> p a o", a=16),
                        wo[:, oc*ocw:(oc+1)*ocw].rearrange("(a p) o -> p a o", p=128)),
                        inc="wo_wdma", k=16)
                    p.wait("vector", "wo_wdma")
                    p.op("vector", lambda E, b=b: E.tensor_copy(wbf[b][:], wld[b][:]),
                         inc="wo_wcast")
                    p.wait("tensor", "wo_wcast")
                    for ti in range(TPC):
                        pb = ti % 2
                        if p.cnt.get("wo_evac", 0) >= 2:
                            p.wait("tensor", "wo_evac", p.cnt["wo_evac"] - 1)
                        for j in range(nf):
                            p.op("tensor", lambda E, pb=pb, b=b, j=j, ti=ti: E.matmul(
                                pp[pb][:, 0:ocw], ctxT[:, j*TOK + ti*128 : j*TOK + (ti+1)*128],
                                wbf[b][:, j*ocw:(j+1)*ocw],
                                start=(j == 0), stop=(j == nf - 1)), inc="wo_mm")
                        p.wait("vector", "wo_mm")
                        p.op("vector", lambda E, pb=pb, ti=ti, oc=oc: E.tensor_add(
                            xacc[:, ti*D + oc*ocw : ti*D + (oc+1)*ocw],
                            xacc[:, ti*D + oc*ocw : ti*D + (oc+1)*ocw],
                            pp[pb][:, 0:ocw]), inc="wo_evac")

        p.wait("gpsimd", "wo_evac")
        p.wait("scalar", "wo_evac")
        p.wait("tensor", "wo_evac")
        p.wait("vector", "wo_evac")
        p.wait("sync", "wo_evac")
        for ti in range(TPC):
            p.op("gpsimd", lambda E, ti=ti: E.dma_start(
                x2[ti*128:(ti+1)*128, :], xacc[:, ti*D:(ti+1)*D]), inc="x2_out", k=16)
        with ExitStack() as sP:
            ptile = [ps(sP, nc, f"pth{i}", [128, 128], BF16) for i in range(2)]
            for ti in range(TPC):
                emit_rms_from_sbuf(p, xacc[:, ti*D:(ti+1)*D], h2tok[:], gam_sb[:],
                                   eps_t, rs, D, "h2")
                p.wait("tensor", "h2_v")
                for j in range(16):
                    pb = j % 2
                    if p.cnt.get("xt_evac", 0) >= 2:
                        p.wait("tensor", "xt_evac", p.cnt["xt_evac"] - 1)
                    p.op("tensor", lambda E, pb=pb, j=j: E.transpose(
                        ptile[pb][:], h2tok[:, j*128:(j+1)*128], ident[:]), inc="xt_tp")
                    p.wait("vector", "xt_tp")
                    p.op("vector", lambda E, pb=pb, j=j, ti=ti: E.tensor_copy(
                        h2Ts[:, j*TOK + ti*128 : j*TOK + (ti+1)*128], ptile[pb][:]),
                        inc="xt_evac")
                p.wait("scalar", "xt_evac")
        p.wait("gpsimd", "xt_evac")
        for j in range(16):
            p.op("gpsimd", lambda E, j=j: E.dma_start(
                h2T[j*128:(j+1)*128, :], h2Ts[:, j*TOK:(j+1)*TOK]), inc="h2T_out", k=16)

        wr_sb = sb(stack, nc, "wr_sb", [128, 16 * 8], BF16)
        weg_sb = sb(stack, nc, "weg_sb", [128, 16], BF16)
        wr_ld = sb(stack, nc, "wr_ld", [128, 16 * 8], F32)
        weg_ld = sb(stack, nc, "weg_ld", [128, 16], F32)
        probs = sb(stack, nc, "probs", [128, 8 * TPC], F32)
        stats = sb(stack, nc, "stats", [128, 8 * TPC], F32)
        combs = sb(stack, nc, "combs", [128, 8 * TPC], F32)
        sg_sb = sb(stack, nc, "sg_sb", [128, TPC], F32)
        p.op("sync", lambda E: E.dma_start(
            wr_ld[:].rearrange("p (a o) -> p a o", a=16),
            wr[:, :].rearrange("(a p) o -> p a o", p=128)), inc="r_dma", k=16)
        p.op("sync", lambda E: E.dma_start(
            weg_ld[:].rearrange("p (a o) -> p a o", a=16),
            weg[:, :].rearrange("(a p) o -> p a o", p=128),
            allow_slow_non_contiguous=True), inc="r_dma", k=16)
        p.wait("vector", "r_dma")
        p.op("vector", lambda E: E.tensor_copy(wr_sb[:], wr_ld[:]), inc="r_v")
        p.op("vector", lambda E: E.tensor_copy(weg_sb[:], weg_ld[:]), inc="r_v")
        p.wait("tensor", "r_v")
        with ExitStack() as sP:
            pr = ps(sP, nc, "pr", [128, 64], F32)
            pe = ps(sP, nc, "pe", [128, 8], F32)
            for ti in range(TPC):
                for j in range(16):
                    p.op("tensor", lambda E, ti=ti, j=j: E.matmul(
                        pr[:, ti*8:(ti+1)*8], h2Ts[:, j*TOK + ti*128 : j*TOK + (ti+1)*128],
                        wr_sb[:, j*8:(j+1)*8], start=(j == 0), stop=(j == 15)), inc="r_mm")
                for j in range(16):
                    p.op("tensor", lambda E, ti=ti, j=j: E.matmul(
                        pe[:, ti:ti+1], h2Ts[:, j*TOK + ti*128 : j*TOK + (ti+1)*128],
                        weg_sb[:, j:j+1], start=(j == 0), stop=(j == 15)), inc="r_mm")
            p.wait("vector", "r_mm")
            p.wait("scalar", "r_mm")
            for ti in range(TPC):
                p.op("vector", lambda E, ti=ti: E.reduce_max(
                    stats[:, ti*8:ti*8+1], pr[:, ti*8:(ti+1)*8], axis=AX.X, negate=True))
            p.op("vector", lambda E: E.tensor_copy(stats[:, 7:8], stats[:, 0:1]), inc="r_vs")
            p.wait("scalar", "r_vs")
            for ti in range(TPC):
                p.op("scalar", lambda E, ti=ti: E.activation(
                    probs[:, ti*8:(ti+1)*8], pr[:, ti*8:(ti+1)*8], AF.Exp,
                    bias=stats[:, ti*8:ti*8+1], accum_out=stats[:, ti*8+1:ti*8+2]))
            for ti in range(TPC):
                p.op("scalar", lambda E, ti=ti: E.activation(
                    sg_sb[:, ti:ti+1], pe[:, ti:ti+1], AF.Sigmoid))
            p.op("scalar", lambda E: E.activation(stats[:, 7:8], stats[:, 0:1], AF.Copy),
                 inc="r_sc")
            p.wait("vector", "r_sc")
        for ti in range(TPC):
            p.op("vector", lambda E, ti=ti: E.reduce_max(
                stats[:, ti*8+2:ti*8+3], probs[:, ti*8:(ti+1)*8], axis=AX.X))
        for ti in range(TPC):
            p.op("vector", lambda E, ti=ti: E.tensor_scalar(
                combs[:, ti*8:(ti+1)*8], probs[:, ti*8:(ti+1)*8],
                stats[:, ti*8+2:ti*8+3], None, ALU.is_equal))
        for ti in range(TPC):
            p.op("vector", lambda E, ti=ti: E.scalar_tensor_tensor(
                combs[:, ti*8:(ti+1)*8], combs[:, ti*8:(ti+1)*8], -1e30,
                probs[:, ti*8:(ti+1)*8], ALU.mult, ALU.add))
        for ti in range(TPC):
            p.op("vector", lambda E, ti=ti: E.reduce_max(
                stats[:, ti*8+3:ti*8+4], combs[:, ti*8:(ti+1)*8], axis=AX.X))
        for ti in range(TPC):
            p.op("vector", lambda E, ti=ti: E.tensor_scalar(
                combs[:, ti*8:(ti+1)*8], probs[:, ti*8:(ti+1)*8],
                stats[:, ti*8+3:ti*8+4], None, ALU.is_ge))
        for ti in range(TPC):
            p.op("vector", lambda E, ti=ti: E.reciprocal(
                stats[:, ti*8+4:ti*8+5], stats[:, ti*8+1:ti*8+2]))
        for ti in range(TPC):
            p.op("vector", lambda E, ti=ti: E.tensor_mul(
                combs[:, ti*8:(ti+1)*8], combs[:, ti*8:(ti+1)*8],
                probs[:, ti*8:(ti+1)*8]))
        for ti in range(TPC):
            p.op("vector", lambda E, ti=ti: E.tensor_scalar(
                combs[:, ti*8:(ti+1)*8], combs[:, ti*8:(ti+1)*8],
                stats[:, ti*8+4:ti*8+5], None, ALU.mult), inc="cb_v")
        p.wait("gpsimd", "cb_v")
        p.op("gpsimd", lambda E: E.dma_start(
            comb[:, :].rearrange("(a p) o -> p a o", p=128),
            combs[:].rearrange("p (a o) -> p a o", a=TPC)), inc="fin", k=16)
        p.op("gpsimd", lambda E: E.dma_start(
            sg[:, :].rearrange("(a p) o -> p a o", p=128),
            sg_sb[:].rearrange("p (a o) -> p a o", a=TPC),
            allow_slow_non_contiguous=True), inc="fin", k=16)
        p.wait("gpsimd", "fin")
        p.wait("gpsimd", "x2_out")
        p.wait("gpsimd", "h2T_out")
        if fc is None:
            p.finalize()
    return nc


def build_l0f_moe_b(debug_dump=False, fc=None):
    nc = bass.Bass("TRN2", target_bir_lowering=False) if fc is None else fc.nc
    x1 = _dr(fc, nc, "x1", [TOK, D], F32, "ExternalInput")
    x2 = _dr(fc, nc, "x2", [TOK, D], F32, "ExternalInput")
    h2Td = _dr(fc, nc, "h2T", [D, TOK], BF16, "ExternalInput")
    combd = _dr(fc, nc, "comb", [TOK, 8], F32, "ExternalInput")
    sgd = _dr(fc, nc, "sg", [TOK, 1], F32, "ExternalInput")
    wg = _dr(fc, nc, "wg", [8, D, 1408], F32, "ExternalInput")
    wu = _dr(fc, nc, "wu", [8, D, 1408], F32, "ExternalInput")
    wd = _dr(fc, nc, "wd", [8, 1408, D], F32, "ExternalInput")
    wsg = _dr(fc, nc, "wsg", [D, 5632], F32, "ExternalInput")
    wsu = _dr(fc, nc, "wsu", [D, 5632], F32, "ExternalInput")
    wsd = _dr(fc, nc, "wsd", [5632, D], F32, "ExternalInput")
    identd = _dr(fc, nc, "ident", [128, 128], BF16, "ExternalInput")
    x4 = _dr(fc, nc, "x4", [TOK, D], F32, "ExternalOutput")
    if debug_dump:
        dbg_xacc = _dr(fc, nc, "dbg_xacc", [TOK, D], F32, "ExternalOutput")
    mTs = _dr(fc, nc, "mTs", [5632, TOK], BF16, "Internal")

    OCWS = [512, 512, 384]
    OCOF = [0, 512, 1024]
    with ExitStack() as stack:
        p = Prog(nc, stack) if fc is None else fc.p
        xacc = sb(stack, nc, "xacc", [128, TPC * D], F32)
        h2Ts = sb(stack, nc, "h2Ts", [128, 16 * TOK], BF16)
        combs = sb(stack, nc, "combs", [128, 8 * TPC], F32)
        sg_sb = sb(stack, nc, "sg_sb", [128, TPC], F32)
        ident = sb(stack, nc, "id", [128, 128], BF16)

        p.op("sync", lambda E: E.dma_start(ident[:], identd[:, :]), inc="c_dma", k=16)
        p.op("sync", lambda E: E.dma_start(
            xacc[:].rearrange("p (a n) -> p a n", a=TPC),
            x2[:, :].rearrange("(a p) n -> p a n", p=128)), inc="c_dma", k=16)
        p.op("sync", lambda E: E.dma_start(
            combs[:].rearrange("p (a o) -> p a o", a=TPC),
            combd[:, :].rearrange("(a p) o -> p a o", p=128),
            allow_slow_non_contiguous=True), inc="c_dma", k=16)
        p.op("sync", lambda E: E.dma_start(
            sg_sb[:].rearrange("p (a o) -> p a o", a=TPC),
            sgd[:, :].rearrange("(a p) o -> p a o", p=128),
            allow_slow_non_contiguous=True), inc="c_dma", k=16)
        for j in range(16):
            p.op("sync", lambda E, j=j: E.dma_start(
                h2Ts[:, j*TOK:(j+1)*TOK], h2Td[j*128:(j+1)*128, :]), inc="c_dma", k=16)
        p.wait("tensor", "c_dma")
        p.wait("vector", "c_dma")
        p.wait("scalar", "c_dma")

        def gu_phase(wg_ap, wu_ap, wldg, wgbf, sgm, m_sb, mcols, ocws, ocof, comb_col,
                     pg, pu, tag):
            chunks = []
            for oci, ocw in enumerate(ocws):
                chunks.append(("g", oci, ocw, ocof[oci], wg_ap))
                chunks.append(("u", oci, ocw, ocof[oci], wu_ap))
            mm_per_chunk = TPC * 16
            wcast_base = p.cnt.get(tag + "_wcast", 0)

            def emit_load(ci):
                kind, oci, ocw, of, wap = chunks[ci]
                b = ci % 2
                if ci >= 2:
                    p.wait("sync", tag + "_anymm", p.cnt[tag + "_anymm"])
                for hf in range(2):
                    w0 = hf * 8
                    if p.cnt.get(tag + "_wcast", 0) >= 1 and not (ci == 0 and hf == 0):
                        p.wait("sync", tag + "_wcast", p.cnt[tag + "_wcast"])
                    p.op("sync", lambda E, of=of, ocw=ocw, wap=wap, w0=w0, hf=hf: E.dma_start(
                        wldg[0][:, 0:8*ocw].rearrange("p (a o) -> p a o", a=8),
                        wap[w0*128:(w0+8)*128, of:of+ocw].rearrange("(a p) o -> p a o", p=128)),
                        inc=tag + "_wdma", k=16)
                    p.wait("vector", tag + "_wdma")
                    p.op("vector", lambda E, b=b, ocw=ocw, w0=w0: E.tensor_copy(
                        wgbf[b][:, w0*ocw:(w0+8)*ocw], wldg[0][:, 0:8*ocw]),
                        inc=tag + "_wcast")

            emit_load(0)
            for ci in range(len(chunks)):
                kind, oci, ocw, of, wap = chunks[ci]
                b = ci % 2
                p.wait("tensor", tag + "_wcast", wcast_base + 2 * (ci + 1))
                if ci + 1 < len(chunks):
                    emit_load(ci + 1)
                if kind == "g":
                    for ti in range(TPC):
                        pb = ti % 2
                        if p.cnt.get(tag + "_sil", 0) >= 2:
                            p.wait("tensor", tag + "_sil", p.cnt[tag + "_sil"] - 1)
                        for j in range(16):
                            p.op("tensor", lambda E, pb=pb, b=b, j=j, ti=ti, ocw=ocw: E.matmul(
                                pg[pb][:, 0:ocw], h2Ts[:, j*TOK + ti*128 : j*TOK + (ti+1)*128],
                                wgbf[b][:, j*ocw:(j+1)*ocw], start=(j == 0), stop=(j == 15)),
                                inc=tag + "_anymm")
                        p.wait("scalar", tag + "_anymm")
                        p.op("scalar", lambda E, pb=pb, ti=ti, ocw=ocw: E.activation(
                            sgm[:, ti*512:ti*512+ocw], pg[pb][:, 0:ocw], AF.Silu),
                            inc=tag + "_sil")
                else:
                    p.wait("vector", tag + "_sil")
                    for ti in range(TPC):
                        pb = ti % 2
                        if p.cnt.get(tag + "_mev", 0) >= 2:
                            p.wait("tensor", tag + "_mev", p.cnt[tag + "_mev"] - 1)
                        for j in range(16):
                            p.op("tensor", lambda E, pb=pb, b=b, j=j, ti=ti, ocw=ocw: E.matmul(
                                pu[pb][:, 0:ocw], h2Ts[:, j*TOK + ti*128 : j*TOK + (ti+1)*128],
                                wgbf[b][:, j*ocw:(j+1)*ocw], start=(j == 0), stop=(j == 15)),
                                inc=tag + "_anymm")
                        p.wait("vector", tag + "_anymm")
                        if comb_col is not None:
                            p.op("vector", lambda E, pb=pb, ti=ti, ocw=ocw, of=of: E.scalar_tensor_tensor(
                                m_sb[:, ti*mcols + of : ti*mcols + of + ocw],
                                pu[pb][:, 0:ocw], comb_col(ti),
                                sgm[:, ti*512:ti*512+ocw], ALU.mult, ALU.mult),
                                inc=tag + "_mev")
                        else:
                            p.op("vector", lambda E, pb=pb, ti=ti, ocw=ocw, of=of: E.tensor_mul(
                                m_sb[:, ti*mcols + of : ti*mcols + of + ocw],
                                pu[pb][:, 0:ocw], sgm[:, ti*512:ti*512+ocw]),
                                inc=tag + "_mev")

        for e in range(8):
            with ExitStack() as sE:
                m_sb = sb(sE, nc, f"m{e}", [128, 8 * 1408], BF16)
                with ExitStack() as sE1, ExitStack() as sP:
                    wldg = [sb(sE1, nc, f"wldg{e}{i}", [128, 8 * 512], F32) for i in range(1)]
                    wgbf = [sb(sE1, nc, f"wgbf{e}{i}", [128, 16 * 512], BF16) for i in range(2)]
                    sgm = sb(sE1, nc, f"sgm{e}", [128, 8 * 512], BF16)
                    pg = [ps(sP, nc, f"pg{e}{i}", [128, 512], F32) for i in range(2)]
                    pu = [ps(sP, nc, f"pu{e}{i}", [128, 512], F32) for i in range(2)]
                    tag = "eg"
                    gu_phase(wg[e], wu[e], wldg, wgbf, sgm, m_sb, 1408, OCWS, OCOF,
                             lambda ti, e=e: combs[:, ti*8+e:ti*8+e+1], pg, pu, tag)
                    p.wait("tensor", tag + "_mev")
                    p.wait("sync", tag + "_mev")
                    p.wait("scalar", tag + "_mev")
                    p.wait("vector", tag + "_mev")
                mT_sb = sb(sE, nc, f"mT{e}", [128, 8 * 1408], BF16)
                wdld = sb(sE, nc, f"wdld{e}", [128, D], F32)
                wdbf = sb(sE, nc, f"wdbf{e}", [128, 11 * D], BF16)
                tag = "ed"
                for ic in range(11):
                    if ic > 0 or e > 0:
                        p.wait("sync", tag + "_wc")
                    p.op("sync", lambda E, ic=ic, e=e, wdld=wdld: E.dma_start(
                        wdld[:], wd[e][ic*128:(ic+1)*128, :]), inc=tag + "_wd", k=16)
                    p.wait("vector", tag + "_wd")
                    p.op("vector", lambda E, ic=ic, wdld=wdld, wdbf=wdbf: E.tensor_copy(
                        wdbf[:, ic*D:(ic+1)*D], wdld[:]), inc=tag + "_wc")
                with ExitStack() as sP:
                    ptile = [ps(sP, nc, f"ptm{e}{i}", [128, 128], BF16) for i in range(2)]
                    tag = "dd"
                    for ti in range(TPC):
                        for ic in range(11):
                            pb = (ti * 11 + ic) % 2
                            if p.cnt.get(tag + "_ev", 0) >= 2:
                                p.wait("tensor", tag + "_ev", p.cnt[tag + "_ev"] - 1)
                            p.op("tensor", lambda E, pb=pb, ti=ti, ic=ic: E.transpose(
                                ptile[pb][:], m_sb[:, ti*1408 + ic*128 : ti*1408 + (ic+1)*128],
                                ident[:]), inc=tag + "_tp")
                            p.wait("vector", tag + "_tp")
                            p.op("vector", lambda E, pb=pb, ti=ti, ic=ic: E.tensor_copy(
                                mT_sb[:, (ti*11+ic)*128 : (ti*11+ic+1)*128], ptile[pb][:]),
                                inc=tag + "_ev")
                    p.wait("tensor", tag + "_ev")
                    p.wait("sync", tag + "_ev")
                with ExitStack() as sE3, ExitStack() as sP:
                    pd = [ps(sP, nc, f"pd{e}{i}", [128, 512], F32) for i in range(2)]
                    tag = "ed"
                    p.wait("tensor", tag + "_wc")
                    for ti in range(TPC):
                        for oc in range(4):
                            pb = (ti * 4 + oc) % 2
                            if p.cnt.get(tag + "_ev", 0) >= 2:
                                p.wait("tensor", tag + "_ev", p.cnt[tag + "_ev"] - 1)
                            for ic in range(11):
                                p.op("tensor", lambda E, pb=pb, ti=ti, oc=oc, ic=ic: E.matmul(
                                    pd[pb][:], mT_sb[:, (ti*11+ic)*128 : (ti*11+ic+1)*128],
                                    wdbf[:, ic*D + oc*512 : ic*D + (oc+1)*512],
                                    start=(ic == 0), stop=(ic == 10)), inc=tag + "_mm")
                            p.wait("vector", tag + "_mm")
                            p.op("vector", lambda E, pb=pb, ti=ti, oc=oc: E.tensor_add(
                                xacc[:, ti*D + oc*512 : ti*D + (oc+1)*512],
                                xacc[:, ti*D + oc*512 : ti*D + (oc+1)*512],
                                pd[pb][:]), inc=tag + "_ev")
                    p.wait("tensor", tag + "_ev")
                    p.wait("sync", tag + "_ev")
                    p.wait("vector", tag + "_ev")
                    p.wait("scalar", tag + "_ev")

        if debug_dump:
            p.wait("gpsimd", "ed_ev")
            for ti in range(TPC):
                p.op("gpsimd", lambda E, ti=ti: E.dma_start(
                    dbg_xacc[ti*128:(ti+1)*128, :], xacc[:, ti*D:(ti+1)*D]),
                    inc="dbgx", k=16)
            p.wait("vector", "dbgx")
            p.wait("tensor", "dbgx")
        with ExitStack() as sS, ExitStack() as sP:
            wldg = [sb(sS, nc, f"swldg{i}", [128, 8 * 512], F32) for i in range(1)]
            wgbf = [sb(sS, nc, f"swgbf{i}", [128, 16 * 512], BF16) for i in range(2)]
            sgm = sb(sS, nc, "ssgm", [128, 8 * 512], BF16)
            mch = sb(sS, nc, "smch", [128, 512], BF16)
            mTb = sb(sS, nc, "smTb", [128, 512], BF16)
            pg = [ps(sP, nc, f"spg{i}", [128, 512], F32) for i in range(2)]
            pu = [ps(sP, nc, f"spu{i}", [128, 512], F32) for i in range(2)]
            ptile = [ps(sP, nc, f"spt{i}", [128, 128], BF16) for i in range(2)]
            for oc in range(11):
                of = oc * 512
                tag = "sh"
                p.wait("sync", tag + "_umm")
                for hf in range(2):
                    if p.cnt.get(tag + "_wcast", 0) > 0:
                        p.wait("sync", tag + "_wcast", p.cnt[tag + "_wcast"])
                    p.op("sync", lambda E, of=of, hf=hf: E.dma_start(
                        wldg[0][:].rearrange("p (a o) -> p a o", a=8),
                        wsg[hf*1024:(hf+1)*1024, of:of+512].rearrange("(a p) o -> p a o", p=128)),
                        inc=tag + "_wdma", k=16)
                    p.wait("vector", tag + "_wdma")
                    p.op("vector", lambda E, hf=hf: E.tensor_copy(
                        wgbf[0][:, hf*8*512:(hf+1)*8*512], wldg[0][:]), inc=tag + "_wcast")
                p.wait("tensor", tag + "_wcast")
                for ti in range(TPC):
                    pb = ti % 2
                    if p.cnt.get(tag + "_sil", 0) >= 2:
                        p.wait("tensor", tag + "_sil", p.cnt[tag + "_sil"] - 1)
                    for j in range(16):
                        p.op("tensor", lambda E, pb=pb, j=j, ti=ti: E.matmul(
                            pg[pb][:], h2Ts[:, j*TOK + ti*128 : j*TOK + (ti+1)*128],
                            wgbf[0][:, j*512:(j+1)*512], start=(j == 0), stop=(j == 15)),
                            inc=tag + "_gmm")
                    p.wait("scalar", tag + "_gmm")
                    p.op("scalar", lambda E, pb=pb, ti=ti: E.activation(
                        sgm[:, ti*512:(ti+1)*512], pg[pb][:], AF.Silu), inc=tag + "_sil")
                p.wait("sync", tag + "_gmm")
                for hf in range(2):
                    if p.cnt.get(tag + "_wcast", 0) > 0:
                        p.wait("sync", tag + "_wcast", p.cnt[tag + "_wcast"])
                    p.op("sync", lambda E, of=of, hf=hf: E.dma_start(
                        wldg[0][:].rearrange("p (a o) -> p a o", a=8),
                        wsu[hf*1024:(hf+1)*1024, of:of+512].rearrange("(a p) o -> p a o", p=128)),
                        inc=tag + "_wdma", k=16)
                    p.wait("vector", tag + "_wdma")
                    if hf == 0:
                        p.wait("vector", tag + "_sil")
                    p.op("vector", lambda E, hf=hf: E.tensor_copy(
                        wgbf[0][:, hf*8*512:(hf+1)*8*512], wldg[0][:]), inc=tag + "_wcast")
                p.wait("tensor", tag + "_wcast")
                for ti in range(TPC):
                    pb = ti % 2
                    if p.cnt.get(tag + "_tp", 0) >= 4:
                        p.wait("vector", tag + "_tp", p.cnt[tag + "_tp"])
                    for j in range(16):
                        p.op("tensor", lambda E, pb=pb, j=j, ti=ti: E.matmul(
                            pu[pb][:], h2Ts[:, j*TOK + ti*128 : j*TOK + (ti+1)*128],
                            wgbf[0][:, j*512:(j+1)*512], start=(j == 0), stop=(j == 15)),
                            inc=tag + "_umm")
                    p.wait("vector", tag + "_umm")
                    p.op("vector", lambda E, pb=pb, ti=ti: E.tensor_mul(
                        mch[:], pu[pb][:], sgm[:, ti*512:(ti+1)*512]), inc=tag + "_mev")
                    p.wait("tensor", tag + "_mev")
                    for ic in range(4):
                        tb = ic % 2
                        if p.cnt.get(tag + "_tev", 0) >= 2:
                            p.wait("tensor", tag + "_tev", p.cnt[tag + "_tev"] - 1)
                        p.op("tensor", lambda E, tb=tb, ic=ic: E.transpose(
                            ptile[tb][:], mch[:, ic*128:(ic+1)*128], ident[:]),
                            inc=tag + "_tp")
                        p.wait("vector", tag + "_tp")
                        if p.cnt.get(tag + "_mdma", 0) > 0:
                            p.wait("vector", tag + "_mdma", p.cnt[tag + "_mdma"])
                        p.op("vector", lambda E, tb=tb, ic=ic: E.tensor_copy(
                            mTb[:, ic*128:(ic+1)*128], ptile[tb][:]), inc=tag + "_tev")
                    p.wait("gpsimd", tag + "_tev")
                    p.op("gpsimd", lambda E, oc=oc, ti=ti: E.dma_start(
                        mTs[oc*512:(oc+1)*512, ti*128:(ti+1)*128].rearrange(
                            "(a p) t -> p a t", p=128),
                        mTb[:].rearrange("p (a t) -> p a t", a=4)), inc=tag + "_mdma", k=16)
            p.wait("gpsimd", "sh_mdma")
            p.wait("sync", "sh_mdma")
            p.wait("tensor", "sh_mdma")
            p.wait("vector", "sh_mdma")
            p.wait("scalar", "sh_mdma")

        with ExitStack() as sS, ExitStack() as sP:
            wsld = sb(sS, nc, "wsld", [128, 11 * 512], F32)
            wsbf = sb(sS, nc, "wsbf", [128, 11 * 512], BF16)
            mTp = sb(sS, nc, "mTp", [128, 11 * 128], BF16)
            psh = [ps(sP, nc, f"psh{i}", [128, 512], F32) for i in range(8)]
            tag = "sd"
            for oc in range(4):
                for icg in range(4):
                    p.wait("sync", tag + "_mm")
                    p.op("sync", lambda E, oc=oc, icg=icg: E.dma_start(
                        wsld[:].rearrange("p (a o) -> p a o", a=11),
                        wsd[icg*1408:(icg+1)*1408, oc*512:(oc+1)*512].rearrange(
                            "(a p) o -> p a o", p=128)), inc=tag + "_wd", k=16)
                    p.wait("vector", tag + "_wd")
                    p.op("vector", lambda E: E.tensor_copy(wsbf[:], wsld[:]), inc=tag + "_wc")
                    for ti in range(TPC):
                        p.op("sync", lambda E, icg=icg, ti=ti: E.dma_start(
                            mTp[:].rearrange("p (a t) -> p a t", a=11),
                            mTs[icg*1408:(icg+1)*1408, ti*128:(ti+1)*128].rearrange(
                                "(a p) t -> p a t", p=128)), inc=tag + "_md", k=16)
                        p.wait("tensor", tag + "_md")
                        p.wait("tensor", tag + "_wc")
                        for ic in range(11):
                            p.op("tensor", lambda E, ti=ti, ic=ic, icg=icg: E.matmul(
                                psh[ti][:], mTp[:, ic*128:(ic+1)*128],
                                wsbf[:, ic*512:(ic+1)*512],
                                start=(icg == 0 and ic == 0),
                                stop=(icg == 3 and ic == 10),
                                skip_group_check=True), inc=tag + "_mm")
                        p.wait("sync", tag + "_mm")
                p.wait("vector", tag + "_mm")
                for ti in range(TPC):
                    p.op("vector", lambda E, ti=ti, oc=oc: E.scalar_tensor_tensor(
                        xacc[:, ti*D + oc*512 : ti*D + (oc+1)*512],
                        psh[ti][:], sg_sb[:, ti:ti+1],
                        xacc[:, ti*D + oc*512 : ti*D + (oc+1)*512],
                        ALU.mult, ALU.add), inc=tag + "_ev")
                p.wait("tensor", tag + "_ev")
            p.wait("sync", "sd_ev")
            p.wait("scalar", "sd_ev")
            p.wait("gpsimd", "sd_ev")

        xb = [sb(stack, nc, f"lxb{i}", [128, D], F32) for i in range(2)]
        sq = sb(stack, nc, "lsq", [128, D], F32)
        ob = [sb(stack, nc, f"lob{i}", [128, D], F32) for i in range(2)]
        mt = sb(stack, nc, "lmt", [128, 4], F32)
        vt_ = sb(stack, nc, "lvt", [128, 4], F32)
        eps_t = sb(stack, nc, "leps", [128, 1], F32)
        p.op("vector", lambda E: E.memset(eps_t[:], 1e-5), inc="ln_v")
        for ti in range(TPC):
            b = ti % 2
            if p.cnt.get("ln_dma", 0) >= 32:
                p.wait("sync", "ln_v", p.cnt.get("ln_v", 0) - 1)
            p.op("sync", lambda E, b=b, ti=ti: E.dma_start(xb[b][:], x1[ti*128:(ti+1)*128, :]),
                 inc="ln_dma", k=16)
            p.wait("vector", "ln_dma")
            if p.cnt.get("ln_out", 0) >= 32:
                p.wait("vector", "ln_out", p.cnt["ln_out"] - 16)
            p.op("vector", lambda E, b=b, ti=ti: E.tensor_add(
                ob[b][:], xb[b][:], xacc[:, ti*D:(ti+1)*D]))
            emit_ln_tile(p, ob[b][:], ob[b][:], sq, mt, vt_, D, eps_t, "ln")
            p.op("vector", lambda E: E.memset(mt[:, 3:4], 0.0), inc="ln_v")
            p.wait("gpsimd", "ln_v")
            p.op("gpsimd", lambda E, b=b, ti=ti: E.dma_start(x4[ti*128:(ti+1)*128, :], ob[b][:]),
                 inc="ln_out", k=16)
        p.wait("gpsimd", "ln_out")
        if fc is None:
            p.finalize()
    return nc


def build_l0f_qkv(fc=None):
    nc = bass.Bass("TRN2", target_bir_lowering=False) if fc is None else fc.nc
    x = _dr(fc, nc, "x", [TOK, D], F32, "ExternalInput")
    wq = _dr(fc, nc, "wq", [D, D], F32, "ExternalInput")
    wk = _dr(fc, nc, "wk", [D, D], F32, "ExternalInput")
    wv = _dr(fc, nc, "wv", [D, D], F32, "ExternalInput")
    brow = _dr(fc, nc, "brow", [1, 3 * D], F32, "ExternalInput")
    gam = _dr(fc, nc, "gam", [128, D], F32, "ExternalInput")
    ctq = _dr(fc, nc, "ctq", [HD, TOK], F32, "ExternalInput")
    stq = _dr(fc, nc, "stq", [HD, TOK], F32, "ExternalInput")
    ctk = _dr(fc, nc, "ctk", [HD, TOK], F32, "ExternalInput")
    stk = _dr(fc, nc, "stk", [HD, TOK], F32, "ExternalInput")
    identd = _dr(fc, nc, "ident", [128, 128], BF16, "ExternalInput")
    qf = _dr(fc, nc, "qf", [D, TOK], BF16, "ExternalOutput")
    kf = _dr(fc, nc, "kf", [D, TOK], BF16, "ExternalOutput")
    vt = _dr(fc, nc, "vt", [TOK, D], BF16, "ExternalOutput")

    with ExitStack() as stack:
        p = Prog(nc, stack) if fc is None else fc.p
        xT = sb(stack, nc, "xT", [128, 16 * TOK], BF16)
        ident = sb(stack, nc, "id", [128, 128], BF16)
        brow_b = sb(stack, nc, "browb", [1, 3 * D], BF16)
        ones_b = sb(stack, nc, "ones", [1, TOK], BF16)
        CTq = sb(stack, nc, "CTq", [128, TOK], F32)
        STq = sb(stack, nc, "STq", [128, TOK], F32)
        CTk = sb(stack, nc, "CTk", [128, TOK], F32)
        STk = sb(stack, nc, "STk", [128, TOK], F32)
        qe = sb(stack, nc, "qe", [128, TOK], F32)
        qsw = sb(stack, nc, "qsw", [128, TOK], F32)
        qbf = sb(stack, nc, "qbf", [128, TOK], BF16)
        _BFV[id(qsw)] = qbf
        vout = sb(stack, nc, "vout", [128, 256], BF16)
        gam_sb = sb(stack, nc, "gam_sb", [128, D], F32)
        eps_t = sb(stack, nc, "eps_t", [128, 1], F32)
        rs = sb(stack, nc, "rs", [128, 4], F32)
        ptile = [ps(stack, nc, f"pt{i}", [128, 128], BF16) for i in range(2)]
        pp = [ps(stack, nc, f"pp{i}", [128, 512], F32) for i in range(2)]

        with ExitStack() as stackA:
            xbuf = [sb(stackA, nc, f"xb{i}", [128, D], F32) for i in range(2)]
            xbf = [sb(stackA, nc, f"xf{i}", [128, D], BF16) for i in range(2)]

            p.op("sync", lambda E: E.dma_start(ident[:], identd[:, :]), inc="c_dma", k=16)
            p.op("sync", lambda E: E.dma_start(CTq[:], ctq[:, :]), inc="c_dma", k=16)
            p.op("sync", lambda E: E.dma_start(STq[:], stq[:, :]), inc="c_dma", k=16)
            p.op("sync", lambda E: E.dma_start(CTk[:], ctk[:, :]), inc="c_dma", k=16)
            p.op("sync", lambda E: E.dma_start(STk[:], stk[:, :]), inc="c_dma", k=16)
            p.op("sync", lambda E: E.dma_start(gam_sb[:], gam[:, :]), inc="c_dma", k=16)
            for i in range(3):
                if i > 0:
                    p.wait("sync", "c_v", p.cnt.get("c_v", 0))
                p.op("sync", lambda E, i=i: E.dma_start(
                    xbuf[1][0:1, 0:D], brow[0:1, i*D:(i+1)*D]), inc="c_dma", k=16)
                p.wait("vector", "c_dma")
                p.op("vector", lambda E, i=i: E.tensor_copy(
                    brow_b[0:1, i*D:(i+1)*D], xbuf[1][0:1, 0:D]), inc="c_v")
            p.op("vector", lambda E: E.memset(ones_b[:], 1.0), inc="c_v")
            p.op("vector", lambda E: E.memset(eps_t[:], 1e-6), inc="c_v")
            p.wait("tensor", "c_v")
            p.wait("sync", "c_v")
            p.wait("scalar", "c_v")

            emit_load_xT(p, x, xT, xbuf, xbf, ident, ptile, rms=(gam_sb, eps_t, rs))
            p.wait("tensor", "xt_evac")
            p.wait("sync", "xt_evac")
            p.wait("gpsimd", "xt_evac")
            p.wait("scalar", "xt_evac")
            p.wait("vector", "xt_evac")

        wld = [sb(stack, nc, f"wld{i}", [128, D], F32) for i in range(2)]
        wbf = [sb(stack, nc, f"wbf{i}", [128, D], BF16) for i in range(2)]
        wldv = [sb(stack, nc, f"wldv{i}", [128, 16 * 256], F32) for i in range(2)]
        wbfv = [sb(stack, nc, f"wbfv{i}", [128, 16 * 256], BF16) for i in range(2)]

        emit_fm_proj_rope(p, nc, wq, NH, 0, xT, None, wld, wbf, brow_b, ones_b,
                          CTq, STq, qe, qsw, qf, pp, "fm", rope=True, bias_base=0)
        p.wait("tensor", "fm_evac")
        p.wait("vector", "fm_sw")
        p.wait("vector", "fm_out")
        p.wait("sync", "fm_mm")
        emit_fm_proj_rope(p, nc, wk, NH, 0, xT, None, wld, wbf, brow_b, ones_b,
                          CTk, STk, qe, qsw, kf, pp, "fm", rope=True, bias_base=D)
        p.wait("tensor", "fm_evac")
        emit_tm_proj(p, wv, 0, D, xT, wldv, wbfv, brow_b, ones_b, vt, vout,
                     pp, "fm", ocw=256, bias_base=2 * D)

        p.wait("gpsimd", "fm_out")
        p.wait("gpsimd", "fm_out")
        p.wait("gpsimd", "fm_out")
        if fc is None:
            p.finalize()
    return nc


def _prep_l0f_qkv_inputs(inputs, x1_full):
    W_q = inputs["L0_ffn_wq"]; W_k = inputs["L0_ffn_wk"]; W_v = inputs["L0_ffn_wv"]
    brow = np.concatenate([inputs["L0_ffn_bq"], inputs["L0_ffn_bk"],
                           inputs["L0_ffn_bv"]])[None, :].astype(np.float32)
    gam = np.broadcast_to(inputs["L0_ffn_w_ln1"][None, :], (128, D)).astype(np.float32)
    gam = np.ascontiguousarray(gam)
    ident = np.eye(128, dtype=np.float32).astype(BF)
    qscale = HD ** -0.5
    maps = []
    for c in range(N_CORES):
        rows = core_rows(c)
        pos = rows.astype(np.float64)
        ctq, stq = l0f_tables(pos, qscale)
        ctk, stk = l0f_tables(pos, 1.0)
        maps.append({
            "x": np.ascontiguousarray(x1_full[rows]),
            "wq": W_q, "wk": W_k, "wv": W_v, "brow": brow, "gam": gam,
            "ident": ident, "ctq": ctq, "stq": stq, "ctk": ctk, "stk": stk,
        })
    return maps


def emit_ln_tile(p, x_ap, out_ap, sq, mt, vt_, n_cols, eps_t, tag, gamma=None):
    inv_n = 1.0 / n_cols
    p.op("vector", lambda E: E.reduce_sum(mt[:, 0:1], x_ap, axis=AX.X))
    p.op("vector", lambda E: E.tensor_copy(mt[:, 1:2], mt[:, 0:1]), inc=tag + "_v")
    p.wait("scalar", tag + "_v")
    p.op("scalar", lambda E: E.activation(mt[:, 2:3], mt[:, 0:1], AF.Copy, scale=-inv_n))
    p.op("scalar", lambda E: E.activation(sq[:, 0:256], x_ap[:, 0:256], AF.Copy),
         inc=tag + "_s")
    p.wait("vector", tag + "_s")
    p.op("vector", lambda E: E.tensor_scalar(out_ap, x_ap, mt[:, 2:3], None, ALU.add),
         inc=tag + "_v")
    p.wait("scalar", tag + "_v")
    p.op("scalar", lambda E: E.activation(sq[:], out_ap, AF.Square, accum_out=vt_[:, 0:1]))
    p.op("scalar", lambda E: E.activation(vt_[:, 1:2], mt[:, 2:3], AF.Copy))
    p.op("scalar", lambda E: E.activation(vt_[:, 2:3], vt_[:, 0:1], AF.Ln,
                                          scale=inv_n, bias=eps_t[:]))
    p.op("scalar", lambda E: E.activation(sq[:, 0:256], out_ap[:, 0:256], AF.Copy))
    p.op("scalar", lambda E: E.activation(vt_[:, 3:4], vt_[:, 2:3], AF.Exp, scale=-0.5))
    p.op("scalar", lambda E: E.activation(sq[:, 0:256], out_ap[:, 0:256], AF.Copy),
         inc=tag + "_s")
    p.wait("vector", tag + "_s")
    p.op("vector", lambda E: E.tensor_scalar(out_ap, out_ap, vt_[:, 3:4], None, ALU.mult),
         inc=tag + "_v")
    if gamma is not None:
        p.op("vector", lambda E: E.tensor_mul(out_ap, out_ap, gamma))


def build_attn(maskval_is_l0f=False, out_ln=True, nheads=NH, debug_dump=False, full=False, fc=None):
    nc = bass.Bass("TRN2", target_bir_lowering=False) if fc is None else fc.nc
    qf8 = _dr(fc, nc, "qf8", [D, TOK], BF16, "ExternalInput")
    if fc is None:
        kT = nc.dram_tensor("kT", [D, S], BF16, kind="ExternalInput")
        vpre = nc.dram_tensor("vpre", [D, S], BF16, kind="ExternalInput")
        x = nc.dram_tensor("x", [TOK, D], F32, kind="ExternalInput")
    else:
        agk = fc.io["agk"]
        agv = fc.io["agv"]
    maskd = _dr(fc, nc, "mask", [TOK, 1024], F32, "ExternalInput") if not full else \
            (_dr(fc, nc, "mask", [TOK, 1024], F32, "ExternalInput") if fc is None else None)
    identd = _dr(fc, nc, "ident", [128, 128], BF16, "ExternalInput")
    if out_ln:
        y = nc.dram_tensor("y", [TOK, D], F32, kind="ExternalOutput")
    else:
        y = _dr(fc, nc, "y", [TOK, D], BF16, "ExternalOutput")
    if debug_dump:
        dbg_s = nc.dram_tensor("dbg_s", [128, 1024], F32, kind="ExternalOutput")
        dbg_p = nc.dram_tensor("dbg_p", [128, 1024], BF16, kind="ExternalOutput")
        dbg_mz = nc.dram_tensor("dbg_mz", [128, 2], F32, kind="ExternalOutput")

    with ExitStack() as stack:
        p = Prog(nc, stack) if fc is None else fc.p
        attn_acc = sb(stack, nc, "attn_acc", [128, TPC * D], F32)
        ident = sb(stack, nc, "id", [128, 128], BF16)
        qhead = sb(stack, nc, "qhead", [128, TOK], BF16)
        PT_sb = [sb(stack, nc, f"PT{i}", [128, 128], BF16) for i in range(4)]
        mrow = sb(stack, nc, "mrow", [128, 4], F32)
        zrow = sb(stack, nc, "zrow", [128, 4], F32)
        jnk = sb(stack, nc, "jnk", [128, 256], F32)
        pp = [ps(stack, nc, f"pp{i}", [128, 512], F32) for i in range(2)]
        ppv = ps(stack, nc, "ppv", [128, 128], F32)
        ptP = [ps(stack, nc, f"ptP{i}", [128, 128], BF16) for i in range(4)]

        with ExitStack() as stackA:
            khead = sb(stackA, nc, "khead", [128, S], BF16)
            vhead = sb(stackA, nc, "vhead", [128, S], BF16)
            S_sb = sb(stackA, nc, "S_sb", [128, S], F32)
            P_bf = sb(stackA, nc, "P_bf", [128, S], BF16)
            mask_sb = sb(stackA, nc, "mask_sb", [128, TPC * 1024], F32)

            p.op("sync", lambda E: E.dma_start(ident[:], identd[:, :]), inc="c_dma", k=16)
            if maskd is not None:
                p.op("sync", lambda E: E.dma_start(
                    mask_sb[:].rearrange("p (s m) -> p s m", s=TPC),
                    maskd[:, :].rearrange("(s p) m -> p s m", p=128)), inc="c_dma", k=16)
            p.wait("tensor", "c_dma")
            p.wait("vector", "c_dma")

            nT = 0
            for h in range(nheads):
                if p.cnt.get("hd_done", 0) > 0:
                    p.wait("sync", "hd_done", p.cnt["hd_done"])
                p.op("sync", lambda E, h=h: E.dma_start(qhead[:], qf8[h*128:(h+1)*128, :]),
                     inc="h_dma", k=16)
                if fc is None:
                    p.op("sync", lambda E, h=h: E.dma_start(khead[:], kT[h*128:(h+1)*128, :]),
                         inc="h_dma", k=16)
                    p.op("sync", lambda E, h=h: E.dma_start(vhead[:], vpre[h*128:(h+1)*128, :]),
                         inc="h_dma", k=16)
                else:
                    kv_ = khead[:].rearrange("p (s r d) -> p s r d", r=8, d=128)
                    vv_ = vhead[:].rearrange("p (s r d) -> p s r d", r=8, d=128)
                    for r in range(8):
                        p.op("sync", lambda E, h=h, r=r, kv_=kv_: E.dma_start(
                            kv_[:, :, r, :],
                            agk[r*D + h*128 : r*D + (h+1)*128, :].rearrange(
                                "p (s d) -> p s d", d=128)), inc="h_dma", k=16)
                        p.op("sync", lambda E, h=h, r=r, vv_=vv_: E.dma_start(
                            vv_[:, :, r, :],
                            agv[r*TOK : (r+1)*TOK, h*128:(h+1)*128].rearrange(
                                "(s p) d -> p s d", p=128)), inc="h_dma", k=16)
                p.wait("tensor", "h_dma")
                for s in range(TPC):
                    ext = S if full else (s + 1) * 1024
                    nqk = ext // 512
                    npv = ext // 128
                    for kc in range(nqk):
                        pb = kc % 2
                        if p.cnt.get("qk_evac", 0) >= 2:
                            p.wait("tensor", "qk_evac", p.cnt.get("qk_evac", 0) - 1)
                        p.op("tensor", lambda E, pb=pb, s=s, kc=kc: E.matmul(
                            pp[pb][:], qhead[:, s*128:(s+1)*128],
                            khead[:, kc*512:(kc+1)*512], start=True, stop=True),
                            inc="qk_mm")
                        p.wait("vector", "qk_mm")
                        p.op("vector", lambda E, pb=pb, kc=kc: E.tensor_copy(
                            S_sb[:, kc*512:(kc+1)*512], pp[pb][:]), inc="qk_evac")
                    if not full:
                        p.op("vector", lambda E, s=s, ext=ext: E.tensor_add(
                            S_sb[:, ext-1024:ext], S_sb[:, ext-1024:ext],
                            mask_sb[:, s*1024:(s+1)*1024]))
                    p.op("vector", lambda E, ext=ext: E.reduce_max(
                        mrow[:, 0:1], S_sb[:, 0:ext], axis=AX.X, negate=True))
                    p.op("vector", lambda E: E.tensor_copy(mrow[:, 1:2], mrow[:, 0:1]),
                         inc="v_stat")
                    p.wait("scalar", "v_stat")
                    p.op("scalar", lambda E, ext=ext: E.activation(
                        P_bf[:, 0:ext], S_sb[:, 0:ext], AF.Exp, bias=mrow[:, 0:1],
                        accum_out=zrow[:, 0:1]))
                    p.op("scalar", lambda E: E.activation(
                        jnk[:], S_sb[:, 0:256], AF.Copy), inc="s_exp")
                    p.wait("vector", "s_exp")
                    p.op("vector", lambda E: E.reciprocal(zrow[:, 1:2], zrow[:, 0:1]),
                         inc="v_zinv")
                    p.wait("tensor", "s_exp")
                    for g0 in range(0, npv, 4):
                        gn = min(4, npv - g0)
                        if p.cnt.get("pt_evac", 0) > 0:
                            p.wait("tensor", "pt_evac", p.cnt["pt_evac"])
                        if p.cnt.get("pt_evs", 0) > 0:
                            p.wait("tensor", "pt_evs", p.cnt["pt_evs"])
                        for i_ in range(gn):
                            p.op("tensor", lambda E, i_=i_, ck=g0+i_: E.transpose(
                                ptP[i_][:], P_bf[:, ck*128:(ck+1)*128], ident[:]),
                                inc="pt_tp")
                        p.wait("vector", "pt_tp")
                        p.wait("scalar", "pt_tp")
                        if p.cnt.get("pv_mm", 0) > 0:
                            p.wait("vector", "pv_mm", p.cnt["pv_mm"])
                            p.wait("scalar", "pv_mm", p.cnt["pv_mm"])
                        for i_ in range(gn):
                            if i_ % 2 == 0:
                                p.op("vector", lambda E, i_=i_: E.tensor_copy(
                                    PT_sb[i_][:], ptP[i_][:]), inc="pt_evac")
                            else:
                                p.op("scalar", lambda E, i_=i_: E.copy(
                                    PT_sb[i_][:], ptP[i_][:]), inc="pt_evs")
                        p.wait("tensor", "pt_evac")
                        p.wait("tensor", "pt_evs")
                        for i_ in range(gn):
                            p.op("tensor", lambda E, i_=i_, ck=g0+i_, npv=npv: E.matmul(
                                ppv[:], PT_sb[i_][:], vhead[:, ck*128:(ck+1)*128],
                                start=(ck == 0), stop=(ck == npv - 1)), inc="pv_mm")
                    if debug_dump and h == 0 and s == 0:
                        p.wait("gpsimd", "v_zinv")
                        p.op("gpsimd", lambda E, ext=ext: E.dma_start(dbg_s[:, :], S_sb[:, 0:1024]), inc="dbg", k=16)
                        p.op("gpsimd", lambda E, ext=ext: E.dma_start(dbg_p[:, :], P_bf[:, 0:1024]), inc="dbg", k=16)
                        p.op("gpsimd", lambda E: E.dma_start(dbg_mz[:, 0:1], mrow[:, 0:1], allow_slow_non_contiguous=True), inc="dbg", k=16)
                        p.op("gpsimd", lambda E: E.dma_start(dbg_mz[:, 1:2], zrow[:, 1:2], allow_slow_non_contiguous=True), inc="dbg", k=16)
                        p.wait("vector", "dbg")
                    p.wait("vector", "pv_mm")
                    p.wait("vector", "v_zinv")
                    p.op("vector", lambda E, s=s, h=h: E.tensor_scalar(
                        attn_acc[:, s*D + h*128 : s*D + (h+1)*128], ppv[:],
                        zrow[:, 1:2], None, ALU.mult), inc="v_ep")
                    p.wait("tensor", "v_ep")
                p.op("vector", lambda E: E.memset(mrow[:, 3:4], 0.0), inc="hd_done")

        xb = [sb(stack, nc, f"lxb{i}", [128, D], F32) for i in range(2)]
        sq = sb(stack, nc, "lsq", [128, D], F32)
        ob = [sb(stack, nc, f"lob{i}", [128, D], F32 if out_ln else BF16) for i in range(2)]
        mt = sb(stack, nc, "lmt", [128, 4], F32)
        vt_ = sb(stack, nc, "lvt", [128, 4], F32)
        eps_t = sb(stack, nc, "leps", [128, 1], F32)
        p.op("vector", lambda E: E.memset(eps_t[:], 1e-5), inc="ln_v")
        p.wait("sync", "v_ep")
        for ti in range(TPC):
            b = ti % 2
            if out_ln:
                if p.cnt.get("ln_dma", 0) >= 32:
                    p.wait("sync", "ln_v", p.cnt.get("ln_v", 0) - 1)
                p.op("sync", lambda E, b=b, ti=ti: E.dma_start(xb[b][:], x[ti*128:(ti+1)*128, :]),
                     inc="ln_dma", k=16)
                p.wait("vector", "ln_dma")
            if p.cnt.get("ln_out", 0) >= 32:
                p.wait("vector", "ln_out", p.cnt["ln_out"] - 16)
            if out_ln:
                p.op("vector", lambda E, b=b, ti=ti: E.tensor_add(
                    ob[b][:], xb[b][:], attn_acc[:, ti*D:(ti+1)*D]))
                emit_ln_tile(p, ob[b][:], ob[b][:], sq, mt, vt_, D, eps_t, "ln")
                p.op("vector", lambda E: E.memset(mt[:, 3:4], 0.0), inc="ln_v")
            else:
                p.op("vector", lambda E, b=b, ti=ti: E.tensor_copy(
                    ob[b][:], attn_acc[:, ti*D:(ti+1)*D]), inc="ln_v")
            p.wait("gpsimd", "ln_v")
            p.op("gpsimd", lambda E, b=b, ti=ti: E.dma_start(y[ti*128:(ti+1)*128, :], ob[b][:]),
                 inc="ln_out", k=16)
        p.wait("gpsimd", "ln_out")
        if fc is None:
            p.finalize()
    return nc



def _prep_l0a_qkv_inputs(inputs):
    x_full = np.ascontiguousarray(inputs["L0_attn_x"].reshape(S, D))
    W = inputs["L0_attn_Wqkv_w"]
    bvec = inputs["L0_attn_Wqkv_b"]
    perm = l0a_perm()
    colidx = np.empty(3 * D, np.int64)
    for sec in range(2):
        for h in range(NH):
            base = sec * D + h * HD
            colidx[base:base + HD] = base + perm
    colidx[2 * D:] = np.arange(2 * D, 3 * D)
    wT = np.ascontiguousarray(W.T[:, colidx])
    brow = np.ascontiguousarray(bvec[colidx][None, :])
    ident = np.eye(128, dtype=np.float32).astype(BF)
    maps = []
    qscale = 1.0 / np.sqrt(HD)
    for c in range(N_CORES):
        rows = core_rows(c)
        pos = rows.astype(np.float64)
        ctq, stq = l0a_tables(pos, qscale)
        ctk, stk = l0a_tables(pos, 1.0)
        maps.append({
            "x": np.ascontiguousarray(x_full[rows]),
            "wT": wT, "brow": brow, "ident": ident,
            "ctq": ctq, "stq": stq, "ctk": ctk, "stk": stk,
        })
    return maps


def emit_fp4_decode(p, pk_ap, nwords, wordoff, nf1, vals, t1, t2, ir1, ir2,
                    sc_ap_of_g, wdec_bf, eng="vector"):
    F = nwords * 8
    nf1v = nf1[:, 0:F].rearrange("p (w k) -> p w k", k=8)
    for k in range(8):
        if k > 0:
            p.op(eng, lambda E, k=k: E.tensor_single_scalar(
                ir1[:, 0:nwords], pk_ap, 4 * k, ALU.logical_shift_right))
            src = lambda: ir1[:, 0:nwords]
        else:
            src = lambda: pk_ap
        p.op(eng, lambda E, src=src: E.tensor_single_scalar(
            ir2[:, 0:nwords], src(), 15, ALU.bitwise_and))
        p.op(eng, lambda E, k=k: E.tensor_copy(nf1v[:, :, k], ir2[:, 0:nwords]))
    p.op(eng, lambda E: E.tensor_single_scalar(vals[:, 0:F], nf1[:, 0:F], 8.0, ALU.is_ge))
    p.op(eng, lambda E: E.scalar_tensor_tensor(
        nf1[:, 0:F], vals[:, 0:F], -8.0, nf1[:, 0:F], ALU.mult, ALU.add))
    p.op(eng, lambda E: E.tensor_scalar(vals[:, 0:F], vals[:, 0:F], -2.0, 1.0,
                                        ALU.mult, ALU.add))
    p.op(eng, lambda E: E.tensor_scalar(t1[:, 0:F], nf1[:, 0:F], 6.0, 0.0,
                                        ALU.subtract, ALU.max))
    p.op(eng, lambda E: E.tensor_scalar(t2[:, 0:F], nf1[:, 0:F], 4.0, 0.0,
                                        ALU.subtract, ALU.max))
    p.op(eng, lambda E: E.tensor_add(t2[:, 0:F], t2[:, 0:F], nf1[:, 0:F]))
    p.op(eng, lambda E: E.scalar_tensor_tensor(
        t1[:, 0:F], t2[:, 0:F], 0.5, t1[:, 0:F], ALU.mult, ALU.add))
    p.op(eng, lambda E: E.tensor_mul(vals[:, 0:F], t1[:, 0:F], vals[:, 0:F]))
    ngr = F // 128
    for g in range(ngr):
        p.op(eng, lambda E, g=g: E.tensor_scalar(
            wdec_bf[:, wordoff*8 + g*128 : wordoff*8 + (g+1)*128],
            vals[:, g*128:(g+1)*128], sc_ap_of_g(g), None, ALU.mult))


def build_l1f(fc=None):
    nc = bass.Bass("TRN2", target_bir_lowering=False) if fc is None else fc.nc
    x5 = _dr(fc, nc, "x5", [TOK, D], F32, "ExternalInput")
    gup = _dr(fc, nc, "gup", [11264, 256], I32, "ExternalInput")
    gus = _dr(fc, nc, "gus", [11264, 16], F16, "ExternalInput")
    dnp = _dr(fc, nc, "dnp", [D, 704], I32, "ExternalInput")
    dns = _dr(fc, nc, "dns", [D, 44], F16, "ExternalInput")
    identd = _dr(fc, nc, "ident", [128, 128], BF16, "ExternalInput")
    x6 = _dr(fc, nc, "x6", [TOK, D], F32, "ExternalOutput")
    mTs = _dr(fc, nc, "mTs", [5632, TOK], BF16, "Internal")
    wdnT = _dr(fc, nc, "wdnT", [5632, D], BF16, "Internal")

    with ExitStack() as stack:
        p = Prog(nc, stack) if fc is None else fc.p
        xT = sb(stack, nc, "xT", [128, 16 * TOK], BF16)
        yacc = sb(stack, nc, "yacc", [128, TPC * D], F32)
        ident = sb(stack, nc, "id", [128, 128], BF16)

        with ExitStack() as sA:
            xbuf = [sb(sA, nc, f"xb{i}", [128, D], F32) for i in range(2)]
            xbf = [sb(sA, nc, f"xf{i}", [128, D], BF16) for i in range(2)]
            with ExitStack() as sP:
                ptile = [ps(sP, nc, f"pt{i}", [128, 128], BF16) for i in range(2)]
                p.op("sync", lambda E: E.dma_start(ident[:], identd[:, :]), inc="c_dma", k=16)
                p.wait("vector", "c_dma")
                p.wait("tensor", "c_dma")
                emit_load_xT(p, x5, xT, xbuf, xbf, ident, ptile)
                for e_ in ENGINES:
                    p.wait(e_, "xt_evac")

        def _phase1():
            with ExitStack() as sA, ExitStack() as sP:
                pk = [sb(sA, nc, f"pk{i}", [128, 704], I32) for i in range(2)]
                scf16 = sb(sA, nc, "scf16", [128, 44], F16)
                scf = sb(sA, nc, "scf", [128, 44], F32)
                nf1 = sb(sA, nc, "nf1", [128, 2048], F32)
                vals = sb(sA, nc, "vals", [128, 2048], F32)
                t1 = sb(sA, nc, "t1", [128, 2048], F32)
                t2 = sb(sA, nc, "t2", [128, 2048], F32)
                ir1 = sb(sA, nc, "ir1", [128, 256], I32)
                ir2 = sb(sA, nc, "ir2", [128, 256], I32)
                wdec = sb(sA, nc, "wdec", [128, 5632], BF16)
                tpb = sb(sA, nc, "tpb", [128, 44 * 128], BF16)
                ptile = [ps(sP, nc, f"ptd{i}", [128, 128], BF16) for i in range(2)]
                CH = [(0, 256), (256, 256), (512, 192)]
                for ot in range(16):
                    b = ot % 2
                    if ot >= 2:
                        p.wait("sync", "dd_v", p.cnt.get("dd_v", 0))
                    p.op("sync", lambda E, b=b, ot=ot: E.dma_start(
                        pk[b][:], dnp[ot*128:(ot+1)*128, :]), inc="dd_dma", k=16)
                    p.op("sync", lambda E, ot=ot: E.dma_start(
                        scf16[:], dns[ot*128:(ot+1)*128, :]), inc="dd_dma", k=16)
                    p.wait("vector", "dd_dma")
                    p.op("vector", lambda E: E.tensor_copy(scf[:], scf16[:]))
                    for (off, nw) in CH:
                        emit_fp4_decode(p, pk[b][:, off:off+nw], nw, off, nf1, vals, t1, t2,
                                        ir1, ir2,
                                        (lambda g, off=off: scf[:, off//16 + g: off//16 + g + 1]),
                                        wdec)
                    p.op("vector", lambda E: E.tensor_copy(scf16[:], scf16[:]), inc="dd_v")
                    p.wait("tensor", "dd_v")
                    for ic in range(44):
                        pb = ic % 2
                        if p.cnt.get("dd_ev", 0) >= 2:
                            p.wait("tensor", "dd_ev", p.cnt["dd_ev"] - 1)
                        p.op("tensor", lambda E, pb=pb, ic=ic: E.transpose(
                            ptile[pb][:], wdec[:, ic*128:(ic+1)*128], ident[:]), inc="dd_tp")
                        p.wait("vector", "dd_tp")
                        if ic < 2 and p.cnt.get("dd_out", 0) > 0:
                            p.wait("vector", "dd_out", p.cnt["dd_out"])
                        p.op("vector", lambda E, pb=pb, ic=ic: E.tensor_copy(
                            tpb[:, ic*128:(ic+1)*128], ptile[pb][:]), inc="dd_ev")
                    p.wait("gpsimd", "dd_ev")
                    p.op("gpsimd", lambda E, ot=ot: E.dma_start(
                        wdnT[:, ot*128:(ot+1)*128].rearrange("(a p) t -> p a t", p=128),
                        tpb[:].rearrange("p (a t) -> p a t", a=44)), inc="dd_out", k=16)
                for e_ in ("sync", "vector", "tensor", "gpsimd", "scalar"):
                    p.wait(e_, "dd_out")

        def _phase2():
            with ExitStack() as sA, ExitStack() as sP:
                pk = [sb(sA, nc, f"gpk{i}", [128, 256], I32) for i in range(2)]
                scf16 = sb(sA, nc, "gscf16", [128, 16], F16)
                scf = sb(sA, nc, "gscf", [128, 16], F32)
                nf1 = sb(sA, nc, "gnf1", [128, 2048], F32)
                vals = sb(sA, nc, "gvals", [128, 2048], F32)
                t1 = sb(sA, nc, "gt1", [128, 2048], F32)
                t2 = sb(sA, nc, "gt2", [128, 2048], F32)
                ir1 = sb(sA, nc, "gir1", [128, 256], I32)
                ir2 = sb(sA, nc, "gir2", [128, 256], I32)
                wdec = sb(sA, nc, "gwdec", [128, 2048], BF16)
                rhs_g = sb(sA, nc, "rhs_g", [128, 16 * 512], BF16)
                rhs_u = sb(sA, nc, "rhs_u", [128, 16 * 512], BF16)
                sgm = sb(sA, nc, "gsgm", [128, 512], BF16)
                mch = sb(sA, nc, "gmch", [128, 512], BF16)
                mTb = sb(sA, nc, "gmTb", [128, 512], BF16)
                ptile = [ps(sP, nc, f"ptg{i}", [128, 128], BF16) for i in range(2)]
                pg = ps(sP, nc, "pgg", [128, 512], F32)
                pu = ps(sP, nc, "pgu", [128, 512], F32)

                def decode_tile_to_rhs(orow0, rhs, subcol, tag):
                    b = (orow0 // 128) % 2
                    p.wait("sync", tag + "_v", p.cnt.get(tag + "_v", 0))
                    p.op("sync", lambda E, b=b, orow0=orow0: E.dma_start(
                        pk[b][:], gup[orow0:orow0+128, :]), inc=tag + "_dma", k=16)
                    p.op("sync", lambda E, orow0=orow0: E.dma_start(
                        scf16[:], gus[orow0:orow0+128, :]), inc=tag + "_dma", k=16)
                    p.wait("vector", tag + "_dma")
                    p.op("vector", lambda E: E.tensor_copy(scf[:], scf16[:]))
                    emit_fp4_decode(p, pk[b][:], 256, 0, nf1, vals, t1, t2, ir1, ir2,
                                    lambda g: scf[:, g:g+1], wdec)
                    p.op("vector", lambda E: E.tensor_copy(scf16[:], scf16[:]), inc=tag + "_v")
                    p.wait("tensor", tag + "_v")
                    for fj in range(16):
                        pb = fj % 2
                        if p.cnt.get(tag + "_ev", 0) >= 2:
                            p.wait("tensor", tag + "_ev", p.cnt[tag + "_ev"] - 1)
                        p.op("tensor", lambda E, pb=pb, fj=fj: E.transpose(
                            ptile[pb][:], wdec[:, fj*128:(fj+1)*128], ident[:]), inc=tag + "_tp")
                        p.wait("vector", tag + "_tp")
                        p.op("vector", lambda E, pb=pb, fj=fj, rhs=rhs, subcol=subcol: E.tensor_copy(
                            rhs[:, fj*512 + subcol*128 : fj*512 + (subcol+1)*128], ptile[pb][:]),
                            inc=tag + "_ev")

                for ib in range(11):
                    p.wait("vector", "gu_umm", p.cnt.get("gu_umm", 0))
                    p.wait("vector", "gu_gmm", p.cnt.get("gu_gmm", 0))
                    for ot in range(4):
                        decode_tile_to_rhs(ib*512 + ot*128, rhs_g, ot, "dd")
                    for ot in range(4):
                        decode_tile_to_rhs(5632 + ib*512 + ot*128, rhs_u, ot, "dd")
                    p.wait("tensor", "gd_ev")
                    for ti in range(TPC):
                        if p.cnt.get("gu_sil", 0) > 0:
                            p.wait("tensor", "gu_sil", p.cnt["gu_sil"])
                        for j in range(16):
                            p.op("tensor", lambda E, j=j, ti=ti: E.matmul(
                                pg[:], xT[:, j*TOK + ti*128 : j*TOK + (ti+1)*128],
                                rhs_g[:, j*512:(j+1)*512], start=(j == 0), stop=(j == 15)),
                                inc="gu_gmm")
                        p.wait("scalar", "gu_gmm")
                        if p.cnt.get("gu_mev", 0) > 0:
                            p.wait("scalar", "gu_mev", p.cnt["gu_mev"])
                        p.op("scalar", lambda E: E.activation(sgm[:], pg[:], AF.Silu),
                             inc="gu_sil")
                        if p.cnt.get("gu_mev", 0) > 0:
                            p.wait("tensor", "gu_mev", p.cnt["gu_mev"])
                        for j in range(16):
                            p.op("tensor", lambda E, j=j, ti=ti: E.matmul(
                                pu[:], xT[:, j*TOK + ti*128 : j*TOK + (ti+1)*128],
                                rhs_u[:, j*512:(j+1)*512], start=(j == 0), stop=(j == 15)),
                                inc="gu_umm")
                        p.wait("vector", "gu_umm")
                        p.wait("vector", "gu_sil")
                        if p.cnt.get("dd_tp", 0) > 0:
                            p.wait("vector", "dd_tp", p.cnt["dd_tp"])
                        p.op("vector", lambda E: E.tensor_mul(mch[:], pu[:], sgm[:]),
                             inc="gu_mev")
                        p.wait("tensor", "gu_mev")
                        for ic in range(4):
                            tb = ic % 2
                            if p.cnt.get("dd_ev", 0) >= 2:
                                p.wait("tensor", "dd_ev", p.cnt["dd_ev"] - 1)
                            p.op("tensor", lambda E, tb=tb, ic=ic: E.transpose(
                                ptile[tb][:], mch[:, ic*128:(ic+1)*128], ident[:]),
                                inc="dd_tp")
                            p.wait("vector", "dd_tp")
                            if p.cnt.get("gu_md", 0) > 0:
                                p.wait("vector", "gu_md", p.cnt["gu_md"])
                            p.op("vector", lambda E, tb=tb, ic=ic: E.tensor_copy(
                                mTb[:, ic*128:(ic+1)*128], ptile[tb][:]), inc="dd_ev")
                        p.wait("gpsimd", "dd_ev")
                        p.op("gpsimd", lambda E, ib=ib, ti=ti: E.dma_start(
                            mTs[ib*512:(ib+1)*512, ti*128:(ti+1)*128].rearrange(
                                "(a p) t -> p a t", p=128),
                            mTb[:].rearrange("p (a t) -> p a t", a=4)), inc="gu_md", k=16)
                for e_ in ENGINES:
                    p.wait(e_, "gu_md")

        def _phase3():
            with ExitStack() as sS, ExitStack() as sP:
                wsbf = sb(sS, nc, "wsbf", [128, 11 * 512], BF16)
                mTp = sb(sS, nc, "mTp", [128, 11 * 128], BF16)
                psh = [ps(sP, nc, f"psh{i}", [128, 512], F32) for i in range(8)]
                tag = "fd"
                for oc in range(4):
                    for icg in range(4):
                        p.wait("sync", tag + "_mm")
                        p.op("sync", lambda E, oc=oc, icg=icg: E.dma_start(
                            wsbf[:].rearrange("p (a o) -> p a o", a=11),
                            wdnT[icg*1408:(icg+1)*1408, oc*512:(oc+1)*512].rearrange(
                                "(a p) o -> p a o", p=128)), inc=tag + "_wd", k=16)
                        for ti in range(TPC):
                            p.op("sync", lambda E, icg=icg, ti=ti: E.dma_start(
                                mTp[:].rearrange("p (a t) -> p a t", a=11),
                                mTs[icg*1408:(icg+1)*1408, ti*128:(ti+1)*128].rearrange(
                                    "(a p) t -> p a t", p=128)), inc=tag + "_md", k=16)
                            p.wait("tensor", tag + "_md")
                            p.wait("tensor", tag + "_wd")
                            for ic in range(11):
                                p.op("tensor", lambda E, ti=ti, ic=ic, icg=icg: E.matmul(
                                    psh[ti][:], mTp[:, ic*128:(ic+1)*128],
                                    wsbf[:, ic*512:(ic+1)*512],
                                    start=(icg == 0 and ic == 0),
                                    stop=(icg == 3 and ic == 10),
                                    skip_group_check=True), inc=tag + "_mm")
                            p.wait("sync", tag + "_mm")
                    p.wait("vector", tag + "_mm")
                    for ti in range(TPC):
                        p.op("vector", lambda E, ti=ti, oc=oc: E.tensor_copy(
                            yacc[:, ti*D + oc*512 : ti*D + (oc+1)*512], psh[ti][:]),
                            inc=tag + "_ev")
                    p.wait("tensor", tag + "_ev")
                for e_ in ENGINES:
                    p.wait(e_, "fd_ev")


        _phase1()
        _phase2()
        _phase3()
        xb = [sb(stack, nc, f"lxb{i}", [128, D], F32) for i in range(2)]
        sq = sb(stack, nc, "lsq", [128, D], F32)
        ob = [sb(stack, nc, f"lob{i}", [128, D], F32) for i in range(2)]
        mt = sb(stack, nc, "lmt", [128, 4], F32)
        vt_ = sb(stack, nc, "lvt", [128, 4], F32)
        rs = sb(stack, nc, "lrs", [128, 4], F32)
        eps5 = sb(stack, nc, "leps5", [128, 1], F32)
        eps6 = sb(stack, nc, "leps6", [128, 1], F32)
        p.op("vector", lambda E: E.memset(eps5[:], 1e-5))
        p.op("vector", lambda E: E.memset(eps6[:], 1e-6), inc="ln_v")
        for ti in range(TPC):
            b = ti % 2
            if p.cnt.get("ln_dma", 0) >= 32:
                p.wait("sync", "ln_v", p.cnt.get("ln_v", 0) - 1)
            p.op("sync", lambda E, b=b, ti=ti: E.dma_start(xb[b][:], x5[ti*128:(ti+1)*128, :]),
                 inc="ln_dma", k=16)
            p.wait("scalar", "ln_v", p.cnt.get("ln_v", 0))
            p.op("scalar", lambda E, ti=ti: E.activation(sq[:], yacc[:, ti*D:(ti+1)*D],
                                                         AF.Square, accum_out=rs[:, 0:1]))
            p.op("scalar", lambda E, ti=ti: E.activation(rs[:, 3:4], yacc[:, ti*D:ti*D+1], AF.Copy))
            p.op("scalar", lambda E: E.activation(rs[:, 1:2], rs[:, 0:1], AF.Ln,
                                                  scale=1.0/D, bias=eps6[:]))
            p.op("scalar", lambda E, ti=ti: E.activation(rs[:, 3:4], yacc[:, ti*D:ti*D+1], AF.Copy))
            p.op("scalar", lambda E: E.activation(rs[:, 2:3], rs[:, 1:2], AF.Exp, scale=-0.5))
            p.op("scalar", lambda E, ti=ti: E.activation(rs[:, 3:4], yacc[:, ti*D:ti*D+1], AF.Copy),
                 inc="ln_s")
            p.wait("vector", "ln_s")
            p.wait("vector", "ln_dma")
            if p.cnt.get("ln_out", 0) >= 32:
                p.wait("vector", "ln_out", p.cnt["ln_out"] - 16)
            p.op("vector", lambda E, b=b, ti=ti: E.scalar_tensor_tensor(
                ob[b][:], yacc[:, ti*D:(ti+1)*D], rs[:, 2:3], xb[b][:],
                ALU.mult, ALU.add))
            emit_ln_tile(p, ob[b][:], ob[b][:], sq, mt, vt_, D, eps5, "ln")
            p.op("vector", lambda E: E.memset(mt[:, 3:4], 0.0), inc="ln_v")
            p.wait("gpsimd", "ln_v")
            p.op("gpsimd", lambda E, b=b, ti=ti: E.dma_start(x6[ti*128:(ti+1)*128, :], ob[b][:]),
                 inc="ln_out", k=16)
        p.wait("gpsimd", "ln_out")
        if fc is None:
            p.finalize()
    return nc


def build_l1a_qkv(fc=None):
    nc = bass.Bass("TRN2", target_bir_lowering=False) if fc is None else fc.nc
    x = _dr(fc, nc, "x", [TOK, D], F32, "ExternalInput")
    wT = _dr(fc, nc, "wT", [D, 3 * D], F32, "ExternalInput")
    identd = _dr(fc, nc, "ident", [128, 128], BF16, "ExternalInput")
    qf = _dr(fc, nc, "qf", [D, TOK], BF16, "ExternalOutput")
    kf = _dr(fc, nc, "kf", [D, TOK], BF16, "ExternalOutput")
    vt = _dr(fc, nc, "vt", [TOK, D], BF16, "ExternalOutput")

    with ExitStack() as stack:
        p = Prog(nc, stack) if fc is None else fc.p
        xT = sb(stack, nc, "xT", [128, 16 * TOK], BF16)
        ident = sb(stack, nc, "id", [128, 128], BF16)
        ones_b = sb(stack, nc, "ones", [1, TOK], BF16)
        qe = sb(stack, nc, "qe", [128, TOK], F32)
        qsw = sb(stack, nc, "qsw", [128, TOK], F32)
        qbf = sb(stack, nc, "qbf", [128, TOK], BF16)
        _BFV[id(qsw)] = qbf
        vout = sb(stack, nc, "vout", [128, 256], BF16)
        ptile = [ps(stack, nc, f"pt{i}", [128, 128], BF16) for i in range(2)]
        pp = [ps(stack, nc, f"pp{i}", [128, 512], F32) for i in range(2)]

        with ExitStack() as stackA:
            xbuf = [sb(stackA, nc, f"xb{i}", [128, D], F32) for i in range(2)]
            xbf = [sb(stackA, nc, f"xf{i}", [128, D], BF16) for i in range(2)]
            p.op("sync", lambda E: E.dma_start(ident[:], identd[:, :]), inc="c_dma", k=16)
            p.wait("vector", "c_dma")
            p.op("vector", lambda E: E.memset(ones_b[:], 1.0), inc="c_v")
            p.wait("tensor", "c_v")
            p.wait("sync", "c_v")
            emit_load_xT(p, x, xT, xbuf, xbf, ident, ptile)
            p.wait("tensor", "xt_evac")
            p.wait("sync", "xt_evac")
            p.wait("gpsimd", "xt_evac")
            p.wait("scalar", "xt_evac")
            p.wait("vector", "xt_evac")

        wld = [sb(stack, nc, f"wld{i}", [128, D], F32) for i in range(2)]
        wbf = [sb(stack, nc, f"wbf{i}", [128, D], BF16) for i in range(2)]
        wldv = [sb(stack, nc, f"wldv{i}", [128, 16 * 256], F32) for i in range(2)]
        wbfv = [sb(stack, nc, f"wbfv{i}", [128, 16 * 256], BF16) for i in range(2)]

        emit_fm_proj_rope(p, nc, wT, NH, 0, xT, None, wld, wbf, None, ones_b,
                          None, None, qe, qsw, qf, pp, "fm", rope=False,
                          scale_imm=HD ** -0.5, add_bias=False)
        p.wait("tensor", "fm_evac")
        p.wait("vector", "fm_out")
        p.wait("sync", "fm_mm")
        emit_fm_proj_rope(p, nc, wT, NH, D, xT, None, wld, wbf, None, ones_b,
                          None, None, qe, qsw, kf, pp, "fm", rope=False,
                          add_bias=False)
        p.wait("tensor", "fm_evac")
        emit_tm_proj(p, wT, 2 * D, D, xT, wldv, wbfv, None, ones_b, vt, vout,
                     pp, "fm", ocw=256, add_bias=False)
        p.wait("gpsimd", "fm_out")
        p.wait("gpsimd", "fm_out")
        p.wait("gpsimd", "fm_out")
        if fc is None:
            p.finalize()
    return nc


def build_proj_ln(fc=None):
    nc = bass.Bass("TRN2", target_bir_lowering=False) if fc is None else fc.nc
    x = _dr(fc, nc, "x", [TOK, D], F32, "ExternalInput")
    ctx = _dr(fc, nc, "ctx", [TOK, D], BF16, "ExternalInput")
    wT = _dr(fc, nc, "wT", [D, D], F32, "ExternalInput")
    brow = _dr(fc, nc, "brow", [1, D], F32, "ExternalInput")
    identd = _dr(fc, nc, "ident", [128, 128], BF16, "ExternalInput")
    y = _dr(fc, nc, "y", [TOK, D], F32, "ExternalOutput")

    with ExitStack() as stack:
        p = Prog(nc, stack) if fc is None else fc.p
        xacc = sb(stack, nc, "xacc", [128, TPC * D], F32)
        ident = sb(stack, nc, "id", [128, 128], BF16)
        brow_b = sb(stack, nc, "browb", [1, D], BF16)
        ones_b = sb(stack, nc, "ones", [1, 128], BF16)
        brow_f = sb(stack, nc, "browf", [1, D], F32)

        p.op("sync", lambda E: E.dma_start(ident[:], identd[:, :]), inc="c_dma", k=16)
        p.op("sync", lambda E: E.dma_start(brow_f[:], brow[:, :]), inc="c_dma", k=16)
        p.op("sync", lambda E: E.dma_start(
            xacc[:].rearrange("p (a n) -> p a n", a=TPC),
            x[:, :].rearrange("(a p) n -> p a n", p=128)), inc="c_dma", k=16)
        p.wait("vector", "c_dma")
        p.op("vector", lambda E: E.tensor_copy(brow_b[:], brow_f[:]), inc="c_v")
        p.op("vector", lambda E: E.memset(ones_b[:], 1.0), inc="c_v")
        p.wait("tensor", "c_v")
        p.wait("sync", "c_v")

        with ExitStack() as sA:
            ctxT = sb(sA, nc, "ctxT", [128, 16 * TOK], BF16)
            cbf = [sb(sA, nc, f"cbf{i}", [128, D], BF16) for i in range(2)]
            wld = [sb(sA, nc, f"wld{i}", [128, 16 * 256], F32) for i in range(2)]
            wbf = [sb(sA, nc, f"wbf{i}", [128, 16 * 256], BF16) for i in range(2)]
            with ExitStack() as sP:
                ptile = [ps(sP, nc, f"pt{i}", [128, 128], BF16) for i in range(2)]
                emit_loadT_bf16(p, ctx, ctxT, cbf, ident, ptile, tag="xt")
                p.wait("tensor", "xt_evac")
                p.wait("sync", "xt_evac")
            with ExitStack() as sP:
                pp = [ps(sP, nc, f"pp{i}", [128, 512], F32) for i in range(2)]
                nf = 16; ocw = 256; noc = D // ocw
                for oc in range(noc):
                    b = oc % 2
                    if p.cnt.get("pj_mm", 0) >= 2 * TPC * (nf + 1):
                        p.wait("sync", "pj_mm", p.cnt["pj_mm"] - TPC * (nf + 1))
                    p.op("sync", lambda E, b=b, oc=oc: E.dma_start(
                        wld[b][:].rearrange("p (a o) -> p a o", a=16),
                        wT[:, oc*ocw:(oc+1)*ocw].rearrange("(a p) o -> p a o", p=128)),
                        inc="pj_wdma", k=16)
                    p.wait("vector", "pj_wdma")
                    p.op("vector", lambda E, b=b: E.tensor_copy(wbf[b][:], wld[b][:]),
                         inc="pj_wcast")
                    p.wait("tensor", "pj_wcast")
                    for ti in range(TPC):
                        pb = ti % 2
                        if p.cnt.get("pj_evac", 0) >= 2:
                            p.wait("tensor", "pj_evac", p.cnt["pj_evac"] - 1)
                        for j in range(nf):
                            p.op("tensor", lambda E, pb=pb, b=b, j=j, ti=ti: E.matmul(
                                pp[pb][:, 0:ocw], ctxT[:, j*TOK + ti*128 : j*TOK + (ti+1)*128],
                                wbf[b][:, j*ocw:(j+1)*ocw],
                                start=(j == 0), stop=False), inc="pj_mm")
                        p.op("tensor", lambda E, pb=pb, oc=oc: E.matmul(
                            pp[pb][:, 0:ocw], ones_b[0:1, 0:128],
                            brow_b[0:1, oc*ocw:(oc+1)*ocw],
                            start=False, stop=True), inc="pj_mm")
                        p.wait("vector", "pj_mm")
                        p.op("vector", lambda E, pb=pb, ti=ti, oc=oc: E.tensor_add(
                            xacc[:, ti*D + oc*ocw : ti*D + (oc+1)*ocw],
                            xacc[:, ti*D + oc*ocw : ti*D + (oc+1)*ocw],
                            pp[pb][:, 0:ocw]), inc="pj_evac")
            p.wait("scalar", "pj_evac")
            p.wait("vector", "pj_evac")
            p.wait("sync", "pj_evac")
            p.wait("gpsimd", "pj_evac")
            p.wait("tensor", "pj_evac")

        sq = sb(stack, nc, "lsq", [128, D], F32)
        ob = [sb(stack, nc, f"lob{i}", [128, D], F32) for i in range(2)]
        mt = sb(stack, nc, "lmt", [128, 4], F32)
        vt_ = sb(stack, nc, "lvt", [128, 4], F32)
        eps_t = sb(stack, nc, "leps", [128, 1], F32)
        p.op("vector", lambda E: E.memset(eps_t[:], 1e-5), inc="ln_v")
        for ti in range(TPC):
            b = ti % 2
            if p.cnt.get("ln_out", 0) >= 32:
                p.wait("vector", "ln_out", p.cnt["ln_out"] - 16)
            emit_ln_tile(p, xacc[:, ti*D:(ti+1)*D], ob[b][:], sq, mt, vt_, D, eps_t, "ln")
            p.op("vector", lambda E: E.memset(mt[:, 3:4], 0.0), inc="ln_v")
            p.wait("gpsimd", "ln_v")
            p.op("gpsimd", lambda E, b=b, ti=ti: E.dma_start(y[ti*128:(ti+1)*128, :], ob[b][:]),
                 inc="ln_out", k=16)
        p.wait("gpsimd", "ln_out")
        if fc is None:
            p.finalize()
    return nc


def attn_ranks(c):
    return [8 * s + c for s in range(TPC)]

def attn_rows(c):
    idx = []
    for r in attn_ranks(c):
        idx.extend(range(r * 128, (r + 1) * 128))
    return np.asarray(idx)


def _prep_attn_inputs(qfm_bf, kfm_bf, vtm_bf, x_resid, maskval):
    vpre = np.ascontiguousarray(
        vtm_bf.reshape(NT, 128, NH, HD).transpose(2, 1, 0, 3).reshape(D, S))
    ident = np.eye(128, dtype=np.float32).astype(BF)
    maps = []
    for c in range(N_CORES):
        rows = attn_rows(c)
        qf8 = np.ascontiguousarray(qfm_bf[:, rows])
        mask = np.zeros((TPC, 128, 1024), np.float32)
        for s in range(TPC):
            rank = 8 * s + c
            kk0 = (s * 8) * 128
            kk = kk0 + np.arange(1024)[None, :]
            qpos = rank * 128 + np.arange(128)[:, None]
            mask[s] = np.where(kk > qpos, maskval, 0.0)
        maps.append({
            "qf8": qf8, "kT": kfm_bf, "vpre": vpre,
            "x": np.ascontiguousarray(x_resid[rows]),
            "mask": mask.reshape(TOK, 1024), "ident": ident,
        })
    return maps


def _bar(p):
    for e in ENGINES:
        s = p.sem("bar_" + e)
        p.ops[e].append(lambda E, s=s: E.sem_inc(s, 1))
        p.cnt["bar_" + e] += 1
    for e in ENGINES:
        for f in ENGINES:
            if f != e:
                p.wait(e, "bar_" + f)


def phase_res_ln(p, nc, x_dram, ctx_dram, out_dram):
    with ExitStack() as stack:
        xb = [sb(stack, nc, f"rlxb{i}", [128, D], F32) for i in range(2)]
        cb = [sb(stack, nc, f"rlcb{i}", [128, D], BF16) for i in range(2)]
        sq = sb(stack, nc, "rlsq", [128, D], F32)
        ob = [sb(stack, nc, f"rlob{i}", [128, D], F32) for i in range(2)]
        mt = sb(stack, nc, "rlmt", [128, 4], F32)
        vt_ = sb(stack, nc, "rlvt", [128, 4], F32)
        eps_t = sb(stack, nc, "rleps", [128, 1], F32)
        p.op("vector", lambda E: E.memset(eps_t[:], 1e-5), inc="rl_v")
        for ti in range(TPC):
            b = ti % 2
            if p.cnt.get("rl_dma", 0) >= 64:
                p.wait("sync", "rl_v", p.cnt.get("rl_v", 0) - 1)
            p.op("sync", lambda E, b=b, ti=ti: E.dma_start(
                xb[b][:], x_dram[ti*128:(ti+1)*128, :]), inc="rl_dma", k=16)
            p.op("sync", lambda E, b=b, ti=ti: E.dma_start(
                cb[b][:], ctx_dram[ti*128:(ti+1)*128, :]), inc="rl_dma", k=16)
            p.wait("vector", "rl_dma")
            if p.cnt.get("rl_out", 0) >= 32:
                p.wait("vector", "rl_out", p.cnt["rl_out"] - 16)
            p.op("vector", lambda E, b=b: E.tensor_add(ob[b][:], xb[b][:], cb[b][:]))
            emit_ln_tile(p, ob[b][:], ob[b][:], sq, mt, vt_, D, eps_t, "rl")
            p.op("vector", lambda E: E.memset(mt[:, 3:4], 0.0), inc="rl_v")
            p.wait("gpsimd", "rl_v")
            p.op("gpsimd", lambda E, b=b, ti=ti: E.dma_start(
                out_dram[ti*128:(ti+1)*128, :], ob[b][:]), inc="rl_out", k=16)
        p.wait("gpsimd", "rl_out")


def build_fused():
    nc = bass.Bass("TRN2", target_bir_lowering=False)
    EI = lambda n, s, d: nc.dram_tensor(n, s, d, kind="ExternalInput")
    IT = lambda n, s, d: nc.dram_tensor(n, s, d)
    ident = EI("ident", [128, 128], BF16)
    x0 = EI("x0", [TOK, D], F32)
    wT0 = EI("wT0", [D, 3 * D], F32)
    brow0 = EI("brow0", [1, 3 * D], F32)
    t0 = {k: EI("t0" + k, [HD, TOK], F32) for k in ("cq", "sq", "ck", "sk")}
    mask0 = EI("mask0", [TOK, 1024], F32)
    wq1 = EI("wq1", [D, D], F32); wk1 = EI("wk1", [D, D], F32); wv1 = EI("wv1", [D, D], F32)
    brow1 = EI("brow1", [1, 3 * D], F32)
    gam1 = EI("gam1", [128, D], F32)
    t1 = {k: EI("t1" + k, [HD, TOK], F32) for k in ("cq", "sq", "ck", "sk")}
    mask1 = EI("mask1", [TOK, 1024], F32)
    wo = EI("wo", [D, D], F32); wr = EI("wr", [D, 8], F32); weg = EI("weg", [D, 1], F32)
    gam2 = EI("gam2", [128, D], F32)
    wg = EI("wg", [8, D, 1408], F32); wu = EI("wu", [8, D, 1408], F32)
    wd = EI("wd", [8, 1408, D], F32)
    wsg = EI("wsg", [D, 5632], F32); wsu = EI("wsu", [D, 5632], F32)
    wsd = EI("wsd", [5632, D], F32)
    wT1 = EI("wT1", [D, 3 * D], F32)
    wTp = EI("wTp", [D, D], F32); browp = EI("browp", [1, D], F32)
    gup = EI("gup", [11264, 256], I32); gus = EI("gus", [11264, 16], F16)
    dnp = EI("dnp", [D, 704], I32); dns = EI("dns", [D, 44], F16)
    x6 = nc.dram_tensor("x6", [TOK, D], F32, kind="ExternalOutput")
    qf_d = IT("qf_d", [D, TOK], BF16)
    kf_d = IT("kf_d", [D, TOK], BF16)
    vt_d = IT("vt_d", [TOK, D], BF16)
    agk = IT("agk", [8 * D, TOK], BF16)
    agv = IT("agv", [S, D], BF16)
    ctx_d = IT("ctx_d", [TOK, D], BF16)
    x1_d = IT("x1_d", [TOK, D], F32)
    x2_d = IT("x2_d", [TOK, D], F32)
    x4_d = IT("x4_d", [TOK, D], F32)
    x5_d = IT("x5_d", [TOK, D], F32)
    h2T_d = IT("h2T_d", [D, TOK], BF16)
    comb_d = IT("comb_d", [TOK, 8], F32)
    sg_d = IT("sg_d", [TOK, 1], F32)
    mTs = IT("mTs_d", [5632, TOK], BF16)
    wdnT = IT("wdnT_d", [5632, D], BF16)

    with ExitStack() as stack:
        p = Prog(nc, stack)
        fc = FC(nc, p, stack)

        def ag_kv():
            p.wait("gpsimd", "fm_out")
            p.wait("gpsimd", "fm_out")
            p.op("gpsimd", lambda E: E.collective_compute(
                "AllGather", ALU.bypass, replica_groups=[list(range(8))],
                ins=[kf_d.ap()], outs=[agk.ap()]), inc="cc", k=1)
            p.op("gpsimd", lambda E: E.collective_compute(
                "AllGather", ALU.bypass, replica_groups=[list(range(8))],
                ins=[vt_d.ap()], outs=[agv.ap()]), inc="cc", k=1)
            p.wait("gpsimd", "cc")
            _bar(p)

        fc.io = {"x": x0, "wT": wT0, "brow": brow0, "ctq": t0["cq"], "stq": t0["sq"],
                 "ctk": t0["ck"], "stk": t0["sk"], "ident": ident,
                 "qf": qf_d, "kf": kf_d, "vt": vt_d}
        build_l0a_qkv(fc=fc)
        _bar(p)
        ag_kv()
        fc.io = {"qf8": qf_d, "agk": agk, "agv": agv, "mask": mask0, "ident": ident,
                 "y": ctx_d}
        build_attn(out_ln=False, fc=fc)
        _bar(p)
        phase_res_ln(p, nc, x0, ctx_d, x1_d)
        _bar(p)
        fc.io = {"x": x1_d, "wq": wq1, "wk": wk1, "wv": wv1, "brow": brow1,
                 "gam": gam1, "ctq": t1["cq"], "stq": t1["sq"], "ctk": t1["ck"],
                 "stk": t1["sk"], "ident": ident, "qf": qf_d, "kf": kf_d, "vt": vt_d}
        build_l0f_qkv(fc=fc)
        _bar(p)
        ag_kv()
        fc.io = {"qf8": qf_d, "agk": agk, "agv": agv, "mask": mask1, "ident": ident,
                 "y": ctx_d}
        build_attn(out_ln=False, fc=fc)
        _bar(p)
        fc.io = {"x1": x1_d, "ctx": ctx_d, "wo": wo, "wr": wr, "weg": weg,
                 "gam": gam2, "ident": ident, "x2": x2_d, "h2T": h2T_d,
                 "comb": comb_d, "sg": sg_d}
        build_l0f_moe_a(fc=fc)
        _bar(p)
        fc.io = {"x1": x1_d, "x2": x2_d, "h2T": h2T_d, "comb": comb_d, "sg": sg_d,
                 "wg": wg, "wu": wu, "wd": wd, "wsg": wsg, "wsu": wsu, "wsd": wsd,
                 "ident": ident, "x4": x4_d, "mTs": mTs}
        build_l0f_moe_b(fc=fc)
        _bar(p)
        fc.io = {"x": x4_d, "wT": wT1, "ident": ident, "qf": qf_d, "kf": kf_d,
                 "vt": vt_d}
        build_l1a_qkv(fc=fc)
        _bar(p)
        ag_kv()
        fc.io = {"qf8": qf_d, "agk": agk, "agv": agv, "ident": ident, "y": ctx_d}
        build_attn(out_ln=False, full=True, fc=fc)
        _bar(p)
        fc.io = {"x": x4_d, "ctx": ctx_d, "wT": wTp, "brow": browp, "ident": ident,
                 "y": x5_d}
        build_proj_ln(fc=fc)
        _bar(p)
        fc.io = {"x5": x5_d, "gup": gup, "gus": gus, "dnp": dnp, "dns": dns,
                 "ident": ident, "x6": x6, "mTs": mTs, "wdnT": wdnT}
        build_l1f(fc=fc)
        p.finalize()
    return nc


def _prep_fused_inputs(inputs):
    ident = np.eye(128, dtype=np.float32).astype(BF)
    x_full = np.ascontiguousarray(np.asarray(inputs["L0_attn_x"], np.float32).reshape(S, D))
    W = np.asarray(inputs["L0_attn_Wqkv_w"], np.float32)
    bvec = np.asarray(inputs["L0_attn_Wqkv_b"], np.float32)
    perm = l0a_perm()
    colidx = np.empty(3 * D, np.int64)
    for sec in range(2):
        for h in range(NH):
            base = sec * D + h * HD
            colidx[base:base + HD] = base + perm
    colidx[2 * D:] = np.arange(2 * D, 3 * D)
    wT0 = np.ascontiguousarray(W.T[:, colidx])
    brow0 = np.ascontiguousarray(bvec[colidx][None, :])
    brow1 = np.concatenate([np.asarray(inputs["L0_ffn_bq"], np.float32),
                            np.asarray(inputs["L0_ffn_bk"], np.float32),
                            np.asarray(inputs["L0_ffn_bv"], np.float32)])[None, :]
    brow1 = np.ascontiguousarray(brow1)
    gam1 = np.ascontiguousarray(np.broadcast_to(
        np.asarray(inputs["L0_ffn_w_ln1"], np.float32)[None, :], (128, D)))
    gam2 = np.ascontiguousarray(np.broadcast_to(
        np.asarray(inputs["L0_ffn_w_ln2"], np.float32)[None, :], (128, D)))
    W1 = np.asarray(inputs["L1_attn_Wqkv"], np.float32)
    idx = np.empty(3 * D, np.int64)
    for h in range(NH):
        for cset in range(3):
            idx[cset*D + h*128 : cset*D + (h+1)*128] = np.arange(
                h*384 + cset*128, h*384 + (cset+1)*128)
    wT1 = np.ascontiguousarray(W1[idx].T)
    wTp = np.ascontiguousarray(np.asarray(inputs["L1_attn_Wproj"], np.float32).T)
    browp = np.ascontiguousarray(np.asarray(inputs["L1_attn_bproj"], np.float32)[None, :])
    gus = np.ascontiguousarray(np.asarray(inputs["L1_ffn_gate_up_scales"]).T)
    dns = np.ascontiguousarray(np.asarray(inputs["L1_ffn_down_scales"]).T)
    gup = np.ascontiguousarray(np.asarray(inputs["L1_ffn_gate_up_packed"], np.int32))
    dnp = np.ascontiguousarray(np.asarray(inputs["L1_ffn_down_packed"], np.int32))
    qscale = HD ** -0.5
    NEGMIN = float(np.finfo(np.float32).min)
    maps = []
    for c in range(N_CORES):
        rows = attn_rows(c)
        pos = rows.astype(np.float64)
        t0cq, t0sq = l0a_tables(pos, qscale)
        t0ck, t0sk = l0a_tables(pos, 1.0)
        t1cq, t1sq = l0f_tables(pos, qscale)
        t1ck, t1sk = l0f_tables(pos, 1.0)
        mask_a = np.zeros((TPC, 128, 1024), np.float32)
        mask_f = np.zeros((TPC, 128, 1024), np.float32)
        for s_ in range(TPC):
            rank = 8 * s_ + c
            kk = (s_ * 8) * 128 + np.arange(1024)[None, :]
            qpos = rank * 128 + np.arange(128)[:, None]
            mask_a[s_] = np.where(kk > qpos, -10000.0, 0.0)
            mask_f[s_] = np.where(kk > qpos, NEGMIN, 0.0)
        maps.append({
            "ident": ident,
            "x0": np.ascontiguousarray(x_full[rows]),
            "wT0": wT0, "brow0": brow0,
            "t0cq": t0cq, "t0sq": t0sq, "t0ck": t0ck, "t0sk": t0sk,
            "mask0": mask_a.reshape(TOK, 1024),
            "wq1": np.asarray(inputs["L0_ffn_wq"], np.float32),
            "wk1": np.asarray(inputs["L0_ffn_wk"], np.float32),
            "wv1": np.asarray(inputs["L0_ffn_wv"], np.float32),
            "brow1": brow1, "gam1": gam1,
            "t1cq": t1cq, "t1sq": t1sq, "t1ck": t1ck, "t1sk": t1sk,
            "mask1": mask_f.reshape(TOK, 1024),
            "wo": np.asarray(inputs["L0_ffn_wo"], np.float32),
            "wr": np.asarray(inputs["L0_ffn_w_router"], np.float32),
            "weg": np.asarray(inputs["L0_ffn_w_sh_eg"], np.float32),
            "gam2": gam2,
            "wg": np.asarray(inputs["L0_ffn_w_gate_e"], np.float32),
            "wu": np.asarray(inputs["L0_ffn_w_up_e"], np.float32),
            "wd": np.asarray(inputs["L0_ffn_w_down_e"], np.float32),
            "wsg": np.asarray(inputs["L0_ffn_w_sh_gate"], np.float32),
            "wsu": np.asarray(inputs["L0_ffn_w_sh_up"], np.float32),
            "wsd": np.asarray(inputs["L0_ffn_w_sh_down"], np.float32),
            "wT1": wT1, "wTp": wTp, "browp": browp,
            "gup": gup, "gus": gus, "dnp": dnp, "dns": dns,
        })
    return maps


def kernel(**inputs):
    _ = (inputs.get("L1_attn_key"), inputs.get("L1_attn_value"))
    nc = build_fused()
    maps = _prep_fused_inputs(inputs)
    res = run_bass_kernel_spmd(nc, maps, core_ids=list(range(N_CORES)))
    out = np.empty((S, D), np.float32)
    for c in range(N_CORES):
        out[attn_rows(c)] = np.asarray(res.results[c]["x6"])
    return out.reshape(1, S, D)


def _run(nc, maps):
    return run_bass_kernel_spmd(nc, maps, core_ids=list(range(N_CORES)))


def _gather_rows(res, key, rows_fn, dtype, ncols=D):
    out = np.empty((S, ncols), dtype)
    for c in range(N_CORES):
        out[rows_fn(c)] = np.asarray(res.results[c][key])
    return out


def _gather_cols(res, key, rows_fn, dtype, nrows=D):
    out = np.empty((nrows, S), dtype)
    for c in range(N_CORES):
        out[:, rows_fn(c)] = np.asarray(res.results[c][key])
    return out


def kernel_v0(**inputs):
    ident = np.eye(128, dtype=np.float32).astype(BF)
    x0 = np.ascontiguousarray(np.asarray(inputs["L0_attn_x"], np.float32).reshape(S, D))

    res = _run(build_l0a_qkv(), _prep_l0a_qkv_inputs(inputs))
    qfm = _gather_cols(res, "qf", core_rows, BF)
    kfm = _gather_cols(res, "kf", core_rows, BF)
    vtm = _gather_rows(res, "vt", core_rows, BF)

    res = _run(build_attn(out_ln=True),
               _prep_attn_inputs(qfm, kfm, vtm, x0, -10000.0))
    x1 = _gather_rows(res, "y", attn_rows, np.float32)

    res = _run(build_l0f_qkv(), _prep_l0f_qkv_inputs(inputs, x1))
    qfm = _gather_cols(res, "qf", core_rows, BF)
    kfm = _gather_cols(res, "kf", core_rows, BF)
    vtm = _gather_rows(res, "vt", core_rows, BF)

    res = _run(build_attn(out_ln=False),
               _prep_attn_inputs(qfm, kfm, vtm, x1, float(np.finfo(np.float32).min)))
    ctx = _gather_rows(res, "y", attn_rows, BF)

    gam2 = np.ascontiguousarray(np.broadcast_to(
        np.asarray(inputs["L0_ffn_w_ln2"], np.float32)[None, :], (128, D)))
    nc = build_l0f_moe_a()
    maps = []
    for c in range(N_CORES):
        rows = core_rows(c)
        maps.append({
            "x1": np.ascontiguousarray(x1[rows]),
            "ctx": np.ascontiguousarray(ctx[rows]),
            "wo": np.asarray(inputs["L0_ffn_wo"], np.float32),
            "wr": np.asarray(inputs["L0_ffn_w_router"], np.float32),
            "weg": np.asarray(inputs["L0_ffn_w_sh_eg"], np.float32),
            "gam": gam2, "ident": ident,
        })
    res = _run(nc, maps)
    x2s = [np.asarray(res.results[c]["x2"]) for c in range(N_CORES)]
    h2Ts = [np.asarray(res.results[c]["h2T"]) for c in range(N_CORES)]
    combs = [np.asarray(res.results[c]["comb"]) for c in range(N_CORES)]
    sgs = [np.asarray(res.results[c]["sg"]) for c in range(N_CORES)]

    nc = build_l0f_moe_b()
    maps = []
    for c in range(N_CORES):
        rows = core_rows(c)
        maps.append({
            "x1": np.ascontiguousarray(x1[rows]),
            "x2": x2s[c], "h2T": h2Ts[c], "comb": combs[c], "sg": sgs[c],
            "wg": np.asarray(inputs["L0_ffn_w_gate_e"], np.float32),
            "wu": np.asarray(inputs["L0_ffn_w_up_e"], np.float32),
            "wd": np.asarray(inputs["L0_ffn_w_down_e"], np.float32),
            "wsg": np.asarray(inputs["L0_ffn_w_sh_gate"], np.float32),
            "wsu": np.asarray(inputs["L0_ffn_w_sh_up"], np.float32),
            "wsd": np.asarray(inputs["L0_ffn_w_sh_down"], np.float32),
            "ident": ident,
        })
    res = _run(nc, maps)
    x4 = _gather_rows(res, "x4", core_rows, np.float32)

    W = np.asarray(inputs["L1_attn_Wqkv"], np.float32)
    idx = np.empty(3 * D, np.int64)
    for h in range(NH):
        for cset in range(3):
            idx[cset*D + h*128 : cset*D + (h+1)*128] = np.arange(
                h*384 + cset*128, h*384 + (cset+1)*128)
    wT = np.ascontiguousarray(W[idx].T)
    nc = build_l1a_qkv()
    maps = [{"x": np.ascontiguousarray(x4[core_rows(c)]), "wT": wT, "ident": ident}
            for c in range(N_CORES)]
    res = _run(nc, maps)
    qfm = _gather_cols(res, "qf", core_rows, BF)
    kfm = _gather_cols(res, "kf", core_rows, BF)
    vtm = _gather_rows(res, "vt", core_rows, BF)

    res = _run(build_attn(out_ln=False, full=True),
               _prep_attn_inputs(qfm, kfm, vtm, x4, 0.0))
    ctx = _gather_rows(res, "y", attn_rows, BF)

    nc = build_proj_ln()
    wTp = np.ascontiguousarray(np.asarray(inputs["L1_attn_Wproj"], np.float32).T)
    brow = np.ascontiguousarray(np.asarray(inputs["L1_attn_bproj"], np.float32)[None, :])
    maps = [{"x": np.ascontiguousarray(x4[core_rows(c)]),
             "ctx": np.ascontiguousarray(ctx[core_rows(c)]),
             "wT": wTp, "brow": brow, "ident": ident} for c in range(N_CORES)]
    res = _run(nc, maps)
    x5 = _gather_rows(res, "y", core_rows, np.float32)

    nc = build_l1f()
    gus = np.ascontiguousarray(np.asarray(inputs["L1_ffn_gate_up_scales"]).T)
    dns = np.ascontiguousarray(np.asarray(inputs["L1_ffn_down_scales"]).T)
    gup = np.ascontiguousarray(np.asarray(inputs["L1_ffn_gate_up_packed"], np.int32))
    dnp = np.ascontiguousarray(np.asarray(inputs["L1_ffn_down_packed"], np.int32))
    maps = [{"x5": np.ascontiguousarray(x5[core_rows(c)]),
             "gup": gup, "gus": gus, "dnp": dnp, "dns": dns, "ident": ident}
            for c in range(N_CORES)]
    res = _run(nc, maps)
    x6 = _gather_rows(res, "x6", core_rows, np.float32)
    return x6.reshape(1, S, D)
